# version 03
# baseline (speedup 1.0000x reference)
import math
import os

import numpy as np
import ml_dtypes

import concourse.bass as bass
import concourse.mybir as mybir
from concourse.bass_utils import run_bass_kernel_spmd

F32 = mybir.dt.float32
BF16 = mybir.dt.bfloat16
I32 = mybir.dt.int32
U32 = mybir.dt.uint32
U8 = mybir.dt.uint8
ALU = mybir.AluOpType
AF = mybir.ActivationFunctionType
AX = mybir.AxisListType
BF = ml_dtypes.bfloat16

B = 4
N = 4096
D = 1024
HID = 2048
QKD = 128
GSZ = 256
NEXP = 8
FFNH = 4096
CAP = 5120
NH = 2048
CHK = 512
NCHK = NH // CHK
NSUB = CHK // 128
EPS = 1e-5
ISL = 4096
EG = 512
ZROW = NEXP * ISL
NGROUP = ISL // EG
HUGE = 1.0e9

ENGINES = ("pe", "act", "dve", "pool", "sp")


class Buf:
    __slots__ = ("space", "lo", "hi", "name")

    def __init__(self, space, lo, hi, name=""):
        self.space, self.lo, self.hi, self.name = space, lo, hi, name

    def sub(self, lo_b, hi_b):
        return Buf(self.space, self.lo + lo_b, self.lo + hi_b, self.name)


class _Seg:
    __slots__ = ("lo", "hi", "writer", "readers")

    def __init__(self, lo, hi, writer, readers):
        self.lo, self.hi, self.writer, self.readers = lo, hi, writer, readers


class Op:
    __slots__ = ("idx", "engine", "emit", "reads", "writes", "is_dma", "gate",
                 "edges", "needs_inc", "rank", "dma_sem", "dma_target", "tag")

    def __init__(self, idx, engine, emit, reads, writes, is_dma, gate, tag):
        self.idx, self.engine, self.emit = idx, engine, emit
        self.reads, self.writes = list(reads), list(writes)
        self.is_dma, self.gate, self.tag = is_dma, gate, tag
        self.edges = {}
        self.needs_inc = False
        self.rank = None
        self.dma_sem = None
        self.dma_target = None


class Gate:
    __slots__ = ("gid", "cond_ap", "cond_buf", "thresh", "key")

    def __init__(self, gid, cond_ap, cond_buf, thresh, key):
        self.gid, self.cond_ap, self.cond_buf, self.thresh = gid, cond_ap, cond_buf, thresh
        self.key = key


class Prog:
    def __init__(self, n_dma_sems=12):
        self.ops = []
        self._cur_gate = None
        self._gid = 0
        self.n_dma = n_dma_sems
        self._rr = 0

    def op(self, engine, emit, reads=(), writes=(), tag=""):
        o = Op(len(self.ops), engine, emit, reads, writes, False, self._cur_gate, tag)
        self.ops.append(o)
        return o

    def dma(self, engine, emit, reads=(), writes=(), tag=""):
        o = Op(len(self.ops), engine, emit, reads, writes, True, self._cur_gate, tag)
        self.ops.append(o)
        return o

    class _G:
        def __init__(s, p, g):
            s.p, s.g = p, g

        def __enter__(s):
            assert s.p._cur_gate is None
            s.p._cur_gate = s.g
            return s.g

        def __exit__(s, *a):
            s.p._cur_gate = None
            return False

    def gate(self, cond_ap, cond_buf, thresh, key=None):
        g = Gate(self._gid, cond_ap, cond_buf, thresh,
                 key if key is not None else ("g", self._gid))
        self._gid += 1
        return Prog._G(self, g)

    def _edges(self):
        spaces = {}
        gate_first = set()
        for o in self.ops:
            if o.gate is not None:
                key = (o.gate.gid, o.engine)
                if key not in gate_first:
                    gate_first.add(key)
                    o.reads = o.reads + [o.gate.cond_buf]
        for o in self.ops:
            def add(j):
                if j is not None and j != o.idx:
                    o.edges[j] = True
            for b in o.reads:
                for s in spaces.setdefault(b.space, []):
                    if s.lo < b.hi and b.lo < s.hi:
                        add(s.writer)
                        s.readers.append(o.idx)
            for b in o.writes:
                ss = spaces.setdefault(b.space, [])
                new = []
                for s in ss:
                    if s.lo < b.hi and b.lo < s.hi:
                        add(s.writer)
                        for r in s.readers:
                            add(r)
                        if s.lo < b.lo:
                            new.append(_Seg(s.lo, b.lo, s.writer, list(s.readers)))
                        if b.hi < s.hi:
                            new.append(_Seg(b.hi, s.hi, s.writer, list(s.readers)))
                    else:
                        new.append(s)
                new.append(_Seg(b.lo, b.hi, o.idx, []))
                spaces[b.space] = new
        for o in self.ops:
            keep = {}
            for j in o.edges:
                pj = self.ops[j]
                if (not pj.is_dma) and (not o.is_dma) and pj.engine == "pe" \
                        and o.engine == "pe":
                    continue
                keep[j] = True
            o.edges = keep
            for j in keep:
                self.ops[j].needs_inc = True
        cnt = {e: 0 for e in ENGINES}
        dma_t = [0] * self.n_dma
        for o in self.ops:
            if o.is_dma:
                o.needs_inc = True
                o.dma_sem = self._rr
                self._rr = (self._rr + 1) % self.n_dma
                o.dma_target = dma_t[o.dma_sem] + 16
                dma_t[o.dma_sem] = o.dma_target
            elif o.needs_inc:
                cnt[o.engine] += 1
                o.rank = cnt[o.engine]
        self.final_dma = dma_t

    def build_streams(self, nc):
        self._edges()
        self.sem_eng = {e: nc.alloc_semaphore(f"se_{e}") for e in ENGINES}
        self.sem_dma = [nc.alloc_semaphore(f"sd_{i}") for i in range(self.n_dma)]

        def sem_of(key):
            return self.sem_eng[key[1]] if key[0] == "e" else self.sem_dma[key[1]]

        def waits_of(o):
            w = {}
            for j in o.edges:
                pj = self.ops[j]
                if pj.is_dma:
                    k, v = ("d", pj.dma_sem), pj.dma_target
                else:
                    if pj.rank is None:
                        continue
                    k, v = ("e", pj.engine), pj.rank
                if w.get(k, 0) < v:
                    w[k] = v
            if o.is_dma and o.dma_target > 16:
                k = ("d", o.dma_sem)
                if w.get(k, 0) < o.dma_target - 16:
                    w[k] = o.dma_target - 16
            return w

        def emit_one(eng, o, sat):
            for k, v in sorted(waits_of(o).items()):
                if sat.get(k, 0) < v:
                    eng.wait_ge(sem_of(k), v)
                    sat[k] = v
            inst = o.emit(eng)
            if o.is_dma:
                inst.then_inc(self.sem_dma[o.dma_sem], 16)
            elif o.rank is not None:
                inst.then_inc(self.sem_eng[o.engine], 1)

        def stream(ename):
            def run(eng):
                sat = {}
                cond_regs = {}
                ops = [o for o in self.ops if o.engine == ename]
                i = 0
                while i < len(ops):
                    o = ops[i]
                    if o.gate is None:
                        emit_one(eng, o, sat)
                        i += 1
                        continue
                    g = o.gate
                    group = []
                    while i < len(ops) and ops[i].gate is g:
                        group.append(ops[i])
                        i += 1
                    for k, v in sorted(waits_of(group[0]).items()):
                        if sat.get(k, 0) < v:
                            eng.wait_ge(sem_of(k), v)
                            sat[k] = v
                    if g.key in cond_regs:
                        greg = cond_regs[g.key]
                    else:
                        greg = eng.alloc_register(f"greg_{ename}_{len(cond_regs)}")
                        cond_regs[g.key] = greg
                        cv = eng.value_load(g.cond_ap)
                        eng.reg_mov(greg, cv)
                    with eng.If_cmp(greg, g.thresh, "IS_GE"):
                        gsat = dict(sat)
                        for oo in group:
                            emit_one(eng, oo, gsat)
                    with eng.Else():
                        incs = {}
                        first_t = {}
                        first_rank = None
                        for oo in group:
                            if oo.is_dma:
                                k = ("d", oo.dma_sem)
                                incs[k] = incs.get(k, 0) + 16
                                if k not in first_t:
                                    first_t[k] = oo.dma_target - 16
                            elif oo.rank is not None:
                                k = ("e", ename)
                                incs[k] = incs.get(k, 0) + 1
                                if first_rank is None:
                                    first_rank = oo.rank
                        for k, v in sorted(incs.items()):
                            if k[0] == "d" and first_t.get(k, 0) > 0:
                                eng.wait_ge(sem_of(k), first_t[k])
                            elif k[0] == "e" and first_rank and first_rank > 1:
                                eng.wait_ge(sem_of(k), first_rank - 1)
                            eng.sem_inc(sem_of(k), v)
            return run
        return stream


class Arena:
    def __init__(self, nc, name, part_bytes):
        self.t = nc.alloc_sbuf_tensor(name, [128, part_bytes], U8)
        self.ap_all = self.t.ap()
        self.space = "sb:" + name
        self.cap = part_bytes
        self.off = 0
        self.marks = []
        self.peak = 0

    def alloc(self, rows, cols, dt, name="", align=64):
        esz = mybir.dt.size(dt)
        off = (self.off + align - 1) // align * align
        sz = cols * esz
        assert off + sz <= self.cap, f"arena OOM at {name}: {off}+{sz} > {self.cap}"
        self.off = off + sz
        self.peak = max(self.peak, self.off)
        ap = self.ap_all[0:rows, off:off + sz].bitcast(dt)
        return ap, Buf(self.space, off, off + sz, name)

    def mark(self):
        self.marks.append(self.off)

    def release(self):
        self.off = self.marks.pop()


def _np_bf16(a):
    return np.ascontiguousarray(np.asarray(a).astype(BF))


def _rel_bias_masks():
    pos = np.arange(GSZ)
    n = -(pos[None, :] - pos[:, None])
    nb = 16
    ret = (n < 0).astype(np.int64) * nb
    n = np.abs(n)
    mx = nb // 2
    small = n < mx
    vl = mx + (np.log(np.maximum(n, 1) / mx) / math.log(128 / mx) * (nb - mx)).astype(np.int64)
    vl = np.minimum(vl, nb - 1)
    bucket = ret + np.where(small, n, vl)
    bt = bucket.T
    masks = np.zeros((32, GSZ, GSZ), np.float32)
    sc = QKD ** 0.5
    for k in range(32):
        masks[k] = (bt == k) * sc
    return _np_bf16(masks.reshape(32 * GSZ, GSZ))


def _l_strict128():
    return _np_bf16((np.arange(128)[:, None] < np.arange(128)[None, :]).astype(np.float32))


def _e16_const():
    e = np.zeros((128, 8 * 16), np.float32)
    for m in range(8):
        for j in range(2):
            e[j * 64:(j + 1) * 64, m * 16 + 2 * m + j] = 0.125
    return _np_bf16(e)


def _f16_const():
    f = np.zeros((16, 8 * 128), np.float32)
    for m in range(8):
        for j in range(2):
            f[2 * m + j, m * 128 + j * 64:m * 128 + (j + 1) * 64] = 1.0
    return _np_bf16(f)


def _t1_const():
    t1 = np.zeros((128, 16), np.float32)
    for tp in range(16):
        t1[tp * 8:(tp + 1) * 8, tp + 1:] = 1.0
    return _np_bf16(t1)


def _ohe_const():
    o = np.zeros((128, 8), np.float32)
    for t in range(16):
        for e in range(8):
            o[t * 8 + e, e] = 1.0
    return _np_bf16(o)


def build_program(debug=0):
    PH = int(os.environ.get("KPHASE", "6"))
    nc = bass.Bass("TRN2", target_bir_lowering=False, enable_asserts=False)

    def ein(name, shape, dt=F32):
        return nc.dram_tensor(name, list(shape), dt, kind="ExternalInput").ap()

    xs = ein("xs", (B, NH, D))
    sel = ein("sel", (1, 16))
    nonce = ein("nonce", (1, 4))
    norm_g = ein("norm_g", (1, D))
    norm_b = ein("norm_b", (1, D))
    w_h = ein("w_h", (D, 2 * HID))
    b_h = ein("b_h", (1, 2 * HID))
    w_qk = ein("w_qk", (D, QKD))
    b_qk = ein("b_qk", (1, QKD))
    qk_gamma = ein("qk_gamma", (4, QKD))
    qk_beta = ein("qk_beta", (4, QKD))
    rel_emb = ein("rel_emb", (1, 32))
    w_o = ein("w_o", (HID, D))
    b_o = ein("b_o", (1, D))
    ln1_w = ein("ln1_w", (1, D))
    ln1_b = ein("ln1_b", (1, D))
    ln2_w = ein("ln2_w", (1, D))
    ln2_b = ein("ln2_b", (1, D))
    qkv_wT = ein("qkv_wT", (D, HID))
    proj_wT = ein("proj_wT", (D, D))
    router_w = ein("router_w", (D, NEXP))
    moe_w1 = ein("moe_w1", (4, D, FFNH))
    moe_w2 = ein("moe_w2", (4, FFNH, D))

    out = nc.dram_tensor("out", [NH, D], F32, kind="ExternalOutput").ap()

    sh_flags = nc.dram_tensor("sh_flags", [1, 64], F32, kind="Internal", addr_space="Shared").ap()
    sh_cnt = nc.dram_tensor("sh_cnt", [2, 32], F32, kind="Internal", addr_space="Shared").ap()
    sh_x2 = nc.dram_tensor("sh_x2", [2 * B * NH, D], BF16, kind="Internal", addr_space="Shared").ap()
    sh_linkv = nc.dram_tensor("sh_linkv", [2, QKD, HID], F32, kind="Internal", addr_space="Shared").ap()
    sh_S = nc.dram_tensor("sh_S", [NEXP * ISL, 1], F32, kind="Internal", addr_space="Shared").ap()
    sh_y1 = nc.dram_tensor("sh_y1", [NEXP * ISL + 1, D // 2], F32, kind="Internal", addr_space="Shared").ap()
    sh_y2 = nc.dram_tensor("sh_y2", [NEXP * ISL + 1, D // 2], F32, kind="Internal", addr_space="Shared").ap()

    pr_haa = nc.dram_tensor("pr_haa", [NH, D], F32, kind="Internal").ap()
    pr_x = nc.dram_tensor("pr_x", [NH, D], F32, kind="Internal").ap()
    pr_xn = nc.dram_tensor("pr_xn", [NH, D], F32, kind="Internal").ap()
    pr_outp = nc.dram_tensor("pr_outp", [NH, D], F32, kind="Internal").ap()
    pr_norm = nc.dram_tensor("pr_norm", [4 * HID, NH], BF16, kind="Internal").ap()
    pr_gate = nc.dram_tensor("pr_gate", [HID, NH], BF16, kind="Internal").ap()
    pr_quad = nc.dram_tensor("pr_quad", [HID, NH], BF16, kind="Internal").ap()

    dbg = {}
    b_dbg = {}
    if debug:
        for nm, shape in [("xn", (NH, D)), ("attn", (D, NH)), ("haa", (NH, D)),
                          ("logits", (4 * NH, NEXP)), ("cnt", (1, 32)),
                          ("gidx", (128, 32)), ("pk", (128, 32)), ("linkv", (QKD, HID)),
                          ("y1", (4 * ISL, 512)), ("y2", (4 * ISL, 512)),
                          ("ngi", (1, 64))]:
            dbg[nm] = nc.dram_tensor("dbg_" + nm, list(shape), F32, kind="ExternalOutput").ap()
            b_dbg[nm] = Buf("dram:dbg_" + nm, 0, 1 << 40, nm)

    c_idb = nc.inline_tensor(np.eye(128, dtype=BF), "c_idb").ap()
    c_idf = nc.inline_tensor(np.eye(128, dtype=np.float32), "c_idf").ap()
    c_l128 = nc.inline_tensor(_l_strict128(), "c_l128").ap()
    c_relm = nc.inline_tensor(_rel_bias_masks(), "c_relm").ap()
    c_e16 = nc.inline_tensor(_e16_const(), "c_e16").ap()
    c_f16 = nc.inline_tensor(_f16_const(), "c_f16").ap()
    c_t1 = nc.inline_tensor(_t1_const(), "c_t1").ap()
    c_ohe = nc.inline_tensor(_ohe_const(), "c_ohe").ap()
    c_misc = nc.inline_tensor(np.concatenate([
        np.ones((1, 128), np.float32),
        np.arange(128, dtype=np.float32)[None, :],
    ], axis=0), "c_misc").ap()
    c_onescb = nc.inline_tensor(np.ones((128, 1), dtype=BF), "c_onescb").ap()
    c_iota8c = nc.inline_tensor(np.arange(8, dtype=np.float32)[:, None], "c_iota8c").ap()
    c_ones8f = nc.inline_tensor(np.ones((8, 1), np.float32), "c_ones8f").ap()
    c_onesr_b16 = nc.inline_tensor(np.ones((1, 128), dtype=BF), "c_onesr16").ap()
    c_estr = nc.inline_tensor((np.arange(8, dtype=np.float32) * ISL)[:, None], "c_estr").ap()

    P = Prog()
    ar = Arena(nc, "arena", 190 * 1024)

    ps = [nc.alloc_psum_tensor(f"ps{i}", [128, 512], F32).ap() for i in range(4)]
    psb = [Buf(f"psum{i}", 0, 2048) for i in range(4)]
    pd = [nc.alloc_psum_tensor(f"pd{i}", [128, 1024], F32).ap() for i in range(2)]
    pdb = [Buf(f"psumd{i}", 0, 4096) for i in range(2)]
    pst_b = ps[3].bitcast(BF16)
    pdt_b = pd[1].bitcast(BF16)

    def DB(name):
        return Buf("dram:" + name, 0, 1 << 40, name)

    bI = {nm: DB(nm) for nm in ["xs", "w_h", "w_qk", "w_o", "qkv_wT", "proj_wT",
                                "router_w", "moe_w1", "moe_w2", "b_h", "b_qk",
                                "qk_gamma", "qk_beta", "rel_emb", "b_o", "norm_g",
                                "norm_b", "ln1_w", "ln1_b", "ln2_w", "ln2_b",
                                "sel", "nonce", "consts"]}
    b_out = DB("out")
    b_haa = DB("pr_haa")
    b_x = DB("pr_x")
    b_xn = DB("pr_xn")
    b_outp = DB("pr_outp")
    b_norm = DB("pr_norm")
    b_gate = DB("pr_gate")
    b_quad = DB("pr_quad")
    b_shx2 = DB("sh_x2")
    b_shcnt = DB("sh_cnt")
    b_shlinkv = DB("sh_linkv")
    b_shS = DB("sh_S")
    b_shy1 = DB("sh_y1")
    b_shy2 = DB("sh_y2")
    b_flags = DB("sh_flags")
    btok = [DB(f"btok{i}") for i in range(4)]

    chsem = {e: nc.alloc_semaphore("ch_" + e) for e in ENGINES}
    chcnt = {e: 0 for e in ENGINES}

    def mkch(ename):
        def ch(e, inst):
            chcnt[ename] += 1
            inst.then_inc(chsem[ename], 1)
            e.wait_ge(chsem[ename], chcnt[ename])
            return inst
        return ch
    CH = {e: mkch(e) for e in ENGINES}

    def vload(e, ap, lo, hi):
        return nc.s_assert_within(e.value_load(ap), lo, hi, skip_runtime_assert=True)

    rbc_nh = nc.gpsimd.alloc_register("rbc_nh")
    rbc_x2 = nc.gpsimd.alloc_register("rbc_x2")
    rbc_S = nc.gpsimd.alloc_register("rbc_S")
    rbc_y = nc.gpsimd.alloc_register("rbc_y")
    rbc_nrm = nc.gpsimd.alloc_register("rbc_nrm")
    rbc_lkv = nc.gpsimd.alloc_register("rbc_lkv")

    def _set_rbc(e):
        e.reg_mov(rbc_nh, NH - 1)
        e.reg_mov(rbc_x2, 2 * B * NH - 1)
        e.reg_mov(rbc_S, NEXP * ISL - 1)
        e.reg_mov(rbc_nrm, 4 * HID - 1)
        e.reg_mov(rbc_lkv, 2 * QKD - 1)
        return e.reg_mov(rbc_y, NEXP * ISL)
    P.op("pool", _set_rbc, tag="rbc")

    selr, selr_b = ar.alloc(1, 16, F32, "selr")
    tot16k, tot16k_b = ar.alloc(1, 1, F32, "tot16k")
    noncer, noncer_b = ar.alloc(1, 4, F32, "noncer")
    NBAR = 4
    magic, magic_b = ar.alloc(1, 4 * (NBAR + 2), F32, "magic")
    hint, hint_b = ar.alloc(1, 4, I32, "hint")
    pint, pint_b = ar.alloc(1, 4, I32, "pint")
    barsc, barsc_b = ar.alloc(1, 32, F32, "barsc")
    cols, cols_b = ar.alloc(128, 80, F32, "cols")
    selc, selc_b = ar.alloc(128, 8, F32, "selc")
    routw, routw_b = ar.alloc(128, 8 * NEXP, F32, "routw")
    biasT, biasT_b = ar.alloc(128, 2 * GSZ, BF16, "biasT")
    m1o, m1o_b = ar.alloc(128, 16 * NEXP, BF16, "m1o")
    m2o, m2o_b = ar.alloc(128, 16 * NEXP, BF16, "m2o")
    dzo, dzo_b = ar.alloc(128, 16, F32, "dzo")
    cntrow, cntrow_b = ar.alloc(1, 32, F32, "cntrow")
    gidx_all, gidx_all_b = ar.alloc(128, 32, F32, "gidx_all")
    pk_all, pk_all_b = ar.alloc(128, 32, F32, "pk_all")
    ngi, ngi_b = ar.alloc(1, 8, I32, "ngi")
    P.op("dve", lambda e: e.memset(tot16k[0:1, 0:1], 16384.0),
         writes=[tot16k_b], tag="t16k")
    s_idb, s_idb_b = ar.alloc(128, 128, BF16, "s_idb")
    s_idf, s_idf_b = ar.alloc(128, 128, F32, "s_idf")
    s_l128, s_l128_b = ar.alloc(128, 128, BF16, "s_l128")
    s_e16, s_e16_b = ar.alloc(128, 8 * 16, BF16, "s_e16")
    s_f16, s_f16_b = ar.alloc(16, 8 * 128, BF16, "s_f16")
    s_t1, s_t1_b = ar.alloc(128, 16, BF16, "s_t1")
    s_ohe, s_ohe_b = ar.alloc(128, 8, BF16, "s_ohe")
    s_ones1f, s_ones1f_b = ar.alloc(1, 128, F32, "s_ones1f")
    s_iota1f, s_iota1f_b = ar.alloc(1, 128, F32, "s_iota1f")
    s_onescb, s_onescb_b = ar.alloc(128, 1, BF16, "s_onescb")
    s_iota8c, s_iota8c_b = ar.alloc(8, 1, F32, "s_iota8c")
    s_ones8f, s_ones8f_b = ar.alloc(8, 1, F32, "s_ones8f")
    s_estr, s_estr_b = ar.alloc(8, 1, F32, "s_estr")
    rowbuf, rowbuf_b = ar.alloc(1, D, F32, "rowbuf")

    P.dma("sp", lambda e: e.dma_start(out=selr[:, :], in_=sel[:, :]), reads=[bI["sel"]],
          writes=[selr_b], tag="ld_sel")
    P.dma("sp", lambda e: e.dma_start(out=noncer[:, :], in_=nonce[:, :]), reads=[bI["nonce"]],
          writes=[noncer_b], tag="ld_nonce")
    for apx, src, bwr in [(s_idb, c_idb, s_idb_b), (s_idf, c_idf, s_idf_b),
                          (s_l128, c_l128, s_l128_b), (s_t1, c_t1, s_t1_b),
                          (s_ohe, c_ohe, s_ohe_b), (s_onescb, c_onescb, s_onescb_b),
                          (s_iota8c, c_iota8c, s_iota8c_b), (s_estr, c_estr, s_estr_b),
                          (s_ones8f, c_ones8f, s_ones8f_b)]:
        P.dma("sp", lambda e, apx=apx, src=src: e.dma_start(out=apx[:, :], in_=src[:, :]),
              reads=[bI["consts"]], writes=[bwr], tag="ldc")
    P.dma("sp", lambda e: e.dma_start(out=s_e16[:, :], in_=c_e16[:, :]),
          reads=[bI["consts"]], writes=[s_e16_b], tag="ldc_e16")
    P.dma("sp", lambda e: e.dma_start(out=s_f16[:, :], in_=c_f16[:, :]),
          reads=[bI["consts"]], writes=[s_f16_b], tag="ldc_f16")
    P.dma("sp", lambda e: e.dma_start(out=s_ones1f[:, :], in_=c_misc[0:1, :]),
          reads=[bI["consts"]], writes=[s_ones1f_b], tag="ldc_ones")
    P.dma("sp", lambda e: e.dma_start(out=s_iota1f[:, :], in_=c_misc[1:2, :]),
          reads=[bI["consts"]], writes=[s_iota1f_b], tag="ldc_iota")

    def mk_magic(e):
        c = CH["dve"]
        for k in range(NBAR + 2):
            e.tensor_scalar(magic[0:1, 4 * k:4 * k + 4], noncer[0:1, 0:4], float(k),
                            None, ALU.add)
        e.tensor_copy(hint[0:1, 0:1], selr[0:1, 8:9])
        return e.tensor_copy(pint[0:1, 0:2], selr[0:1, 9:11])
    P.op("dve", mk_magic, reads=[noncer_b, selr_b], writes=[magic_b, hint_b, pint_b],
         tag="magics")

    barsem = [nc.alloc_semaphore(f"bsem{i}") for i in range(NBAR)]
    bpol = [nc.alloc_semaphore(f"bpol{i}") for i in range(NBAR)]

    def barrier(k, guards, tag=""):
        def emit(e):
            h = vload(e, hint[0:1, 0:1], 0, 1)
            e.dma_start(out=sh_flags[0:1, bass.ts(h, 4)],
                        in_=magic[0:1, 4 * k:4 * (k + 1)]).then_inc(barsem[k - 1], 16)
            e.wait_ge(barsem[k - 1], 16)
            e.tensor_copy(barsc[0:1, 24:25], tot16k[0:1, 0:1])
            a = barsc[0:1, 8:16]
            b2 = barsc[0:1, 16:24]
            with e.register(f"nd{k}") as ndr, e.register(f"bt{k}") as tgt:
                e.reg_mov(ndr, 1)
                e.reg_mov(tgt, 0)
                with e.While(ndr):
                    e.dma_start(out=barsc[0:1, 0:8], in_=sh_flags[0:1, 0:8]).then_inc(bpol[k - 1], 16)
                    e.reg_add(tgt, tgt, 16)
                    e.wait_ge(bpol[k - 1], tgt)
                    e.tensor_tensor(a[0:1, 0:4], barsc[0:1, 0:4],
                                    magic[0:1, 4 * k:4 * k + 4], ALU.subtract)
                    e.tensor_tensor(a[0:1, 4:8], barsc[0:1, 4:8],
                                    magic[0:1, 4 * k:4 * k + 4], ALU.subtract)
                    e.tensor_tensor(a[0:1, 0:8], a[0:1, 0:8], a[0:1, 0:8], ALU.mult)
                    e.tensor_tensor(b2[0:1, 0:4], barsc[0:1, 0:4],
                                    magic[0:1, 4 * (k + 1):4 * (k + 1) + 4], ALU.subtract)
                    e.tensor_tensor(b2[0:1, 4:8], barsc[0:1, 4:8],
                                    magic[0:1, 4 * (k + 1):4 * (k + 1) + 4], ALU.subtract)
                    e.tensor_tensor(b2[0:1, 0:8], b2[0:1, 0:8], b2[0:1, 0:8], ALU.mult)
                    e.tensor_tensor(a[0:1, 0:8], a[0:1, 0:8], b2[0:1, 0:8], ALU.mult)
                    e.tensor_tensor(a[0:1, 0:4], a[0:1, 0:4], a[0:1, 4:8], ALU.add)
                    e.tensor_tensor(a[0:1, 0:2], a[0:1, 0:2], a[0:1, 2:4], ALU.add)
                    e.tensor_tensor(barsc[0:1, 24:25], a[0:1, 0:1], a[0:1, 1:2], ALU.add)
                    nd = e.value_load(barsc[0:1, 24:25].bitcast(I32))
                    e.reg_mov(ndr, nd)
            return e.engine_nop()
        P.op("pool", emit, reads=list(guards) + [magic_b, hint_b, barsc_b, b_flags,
                                                 tot16k_b],
             writes=[btok[k - 1], barsc_b, b_flags], tag=tag or f"bar{k}")

    def colify(dst_col_ap, dst_buf, src_row_ap, src_buf, tag, scale=None):
        P.op("pe", lambda e: e.matmul(ps[0][:, 0:1], src_row_ap, s_ones1f[0:1, 0:1],
                                      start=True, stop=True),
             reads=[src_buf, s_ones1f_b], writes=[psb[0]], tag="cf_" + tag)
        if scale is None:
            P.op("dve", lambda e: e.tensor_copy(dst_col_ap, ps[0][:, 0:1]),
                 reads=[psb[0]], writes=[dst_buf], tag="cfe_" + tag)
        else:
            P.op("dve", lambda e: e.tensor_scalar(dst_col_ap, ps[0][:, 0:1], scale,
                                                  None, ALU.mult),
                 reads=[psb[0]], writes=[dst_buf], tag="cfe_" + tag)

    def bcast_row(dst_ap, dst_buf, src_row_ap, src_buf, width, tag):
        for f0 in range(0, width, 512):
            w = min(512, width - f0)
            P.op("pe", lambda e, f0=f0, w=w: e.matmul(
                ps[0][:, 0:w], s_ones1f[0:1, 0:128], src_row_ap[0:1, f0:f0 + w],
                start=True, stop=True),
                reads=[src_buf, s_ones1f_b], writes=[psb[0]], tag=f"bc_{tag}{f0}")
            P.op("dve", lambda e, f0=f0, w=w: e.tensor_copy(dst_ap[:, f0:f0 + w],
                                                            ps[0][:, 0:w]),
                 reads=[psb[0]], writes=[dst_buf], tag=f"bce_{tag}{f0}")

    def ld_row(src_ap, src_buf, w, tag):
        P.dma("sp", lambda e: e.dma_start(out=rowbuf[0:1, 0:w], in_=src_ap[:, :]),
              reads=[src_buf], writes=[rowbuf_b.sub(0, w * 4)], tag="ldr_" + tag)

    def cols_from_row(col0, nf, tag, off=0, scale=None):
        for i in range(nf):
            colify(cols[:, col0 + i:col0 + i + 1], cols_b.sub((col0 + i) * 4, (col0 + i + 1) * 4),
                   rowbuf[0:1, off + i * 128:off + (i + 1) * 128],
                   rowbuf_b.sub((off + i * 128) * 4, (off + (i + 1) * 128) * 4),
                   f"{tag}{i}", scale=scale)

    C_LN1W, C_LN1B, C_LN2W, C_LN2B, C_NG, C_NB, C_BQK = 0, 8, 16, 24, 32, 40, 48
    C_G0, C_BT0, C_BHG = 49, 53, 57

    ld_row(ln1_w, bI["ln1_w"], D, "ln1w"); cols_from_row(C_LN1W, 8, "ln1w")
    ld_row(ln1_b, bI["ln1_b"], D, "ln1b"); cols_from_row(C_LN1B, 8, "ln1b")
    ld_row(ln2_w, bI["ln2_w"], D, "ln2w"); cols_from_row(C_LN2W, 8, "ln2w")
    ld_row(ln2_b, bI["ln2_b"], D, "ln2b"); cols_from_row(C_LN2B, 8, "ln2b")
    ld_row(norm_g, bI["norm_g"], D, "ng"); cols_from_row(C_NG, 8, "ng")
    ld_row(norm_b, bI["norm_b"], D, "nb"); cols_from_row(C_NB, 8, "nb")
    ld_row(b_qk, bI["b_qk"], QKD, "bqk"); cols_from_row(C_BQK, 1, "bqk")
    gb_sc = {0: 1.0 / 16, 1: 1.0, 2: 1.0 / 16, 3: 1.0 / 4096}
    for hh in range(4):
        P.dma("sp", lambda e, hh=hh: e.dma_start(out=rowbuf[0:1, 0:QKD],
                                                 in_=qk_gamma[hh:hh + 1, :]),
              reads=[bI["qk_gamma"]], writes=[rowbuf_b.sub(0, QKD * 4)], tag=f"ldg{hh}")
        cols_from_row(C_G0 + hh, 1, f"g{hh}", scale=gb_sc[hh])
        P.dma("sp", lambda e, hh=hh: e.dma_start(out=rowbuf[0:1, 0:QKD],
                                                 in_=qk_beta[hh:hh + 1, :]),
              reads=[bI["qk_beta"]], writes=[rowbuf_b.sub(0, QKD * 4)], tag=f"ldbt{hh}")
        cols_from_row(C_BT0 + hh, 1, f"bt{hh}", scale=gb_sc[hh])
    for piece in range(2):
        P.dma("sp", lambda e, piece=piece: e.dma_start(
            out=rowbuf[0:1, 0:D], in_=b_h[0:1, HID + piece * D:HID + (piece + 1) * D]),
            reads=[bI["b_h"]], writes=[rowbuf_b], tag=f"ldbhg{piece}")
        cols_from_row(C_BHG + piece * 8, 8, f"bhg{piece}")
    P.op("pe", lambda e: e.matmul(ps[0][:, 0:8], s_ones1f[0:1, 0:128], selr[0:1, 0:8],
                                  start=True, stop=True),
         reads=[selr_b, s_ones1f_b], writes=[psb[0]], tag="selbc")
    P.op("dve", lambda e: e.tensor_copy(selc[:, 0:8], ps[0][:, 0:8]),
         reads=[psb[0]], writes=[selc_b], tag="selbc_e")
    iotacol, iotacol_b = ar.alloc(128, 1, F32, "iotacol")
    colify(iotacol[:, 0:1], iotacol_b, s_iota1f[0:1, 0:128], s_iota1f_b, "iota")
    epscol, epscol_b = ar.alloc(128, 1, F32, "epscol")
    P.op("dve", lambda e: e.memset(epscol[:, 0:1], EPS), writes=[epscol_b], tag="eps")
    hp8k, hp8k_b = ar.alloc(128, 1, F32, "hp8k")
    P.op("pe", lambda e: e.matmul(ps[0][:, 0:1], s_ones1f[0:1, 0:128], selr[0:1, 8:9],
                                  start=True, stop=True),
         reads=[selr_b, s_ones1f_b], writes=[psb[0]], tag="hbc")
    P.op("dve", lambda e: e.tensor_scalar(hp8k[:, 0:1], ps[0][:, 0:1], float(B * NH),
                                          None, ALU.mult),
         reads=[psb[0]], writes=[hp8k_b], tag="hbce")

    for f in range(8):
        P.dma("sp", lambda e, f=f: e.dma_start(out=routw[:, f * 8:(f + 1) * 8],
                                               in_=router_w[f * 128:(f + 1) * 128, :]),
              reads=[bI["router_w"]], writes=[routw_b.sub(f * 32, (f + 1) * 32)],
              tag=f"ld_rout{f}")

    ar.mark()
    relrow, relrow_b = ar.alloc(1, 32, F32, "relrow")
    relcols, relcols_b = ar.alloc(128, 32, F32, "relcols")
    mstage, mstage_b = ar.alloc(128, GSZ, BF16, "mstage")
    bacc, bacc_b = ar.alloc(128, 2 * GSZ, F32, "bacc")
    P.dma("sp", lambda e: e.dma_start(out=relrow[0:1, :], in_=rel_emb[:, :]),
          reads=[bI["rel_emb"]], writes=[relrow_b], tag="ld_rel")
    bcast_row(relcols, relcols_b, relrow[0:1, 0:32], relrow_b, 32, "rel")
    P.op("dve", lambda e: e.memset(bacc[:, :], 0.0), writes=[bacc_b], tag="bacc0")
    for k in range(32):
        for jt in range(2):
            P.dma("sp", lambda e, k=k, jt=jt: e.dma_start(
                out=mstage[:, :], in_=c_relm[(2 * k + jt) * 128:(2 * k + jt + 1) * 128, :]),
                reads=[bI["consts"]], writes=[mstage_b], tag=f"ldm{k}_{jt}")
            P.op("dve", lambda e, k=k, jt=jt: e.scalar_tensor_tensor(
                bacc[:, jt * GSZ:(jt + 1) * GSZ], mstage[:, :], relcols[:, k:k + 1],
                bacc[:, jt * GSZ:(jt + 1) * GSZ], ALU.mult, ALU.add),
                reads=[mstage_b, relcols_b, bacc_b.sub(jt * GSZ * 4, (jt + 1) * GSZ * 4)],
                writes=[bacc_b.sub(jt * GSZ * 4, (jt + 1) * GSZ * 4)], tag=f"bst{k}_{jt}")
    P.op("dve", lambda e: e.tensor_copy(biasT[:, :], bacc[:, :]),
         reads=[bacc_b], writes=[biasT_b], tag="biasT")
    ar.release()

    ar.mark()
    zS, zS_b = ar.alloc(128, 64, F32, "zS")
    zrowz, zrowz_b = ar.alloc(1, D, F32, "zrowz")
    P.op("dve", lambda e: e.memset(zrowz[0:1, :], 0.0), writes=[zrowz_b], tag="zr0")
    P.dma("sp", lambda e: e.dma_start(out=sh_y1[ZROW:ZROW + 1, :], in_=zrowz[0:1, 0:D // 2]),
          reads=[zrowz_b], writes=[b_shy1.sub(ZROW * 2048, (ZROW + 1) * 2048)], tag="y_z1")
    P.dma("sp", lambda e: e.dma_start(out=sh_y2[ZROW:ZROW + 1, :], in_=zrowz[0:1, 0:D // 2]),
          reads=[zrowz_b], writes=[b_shy2.sub(ZROW * 2048, (ZROW + 1) * 2048)], tag="y_z2")
    P.op("dve", lambda e: e.memset(zS[:, :], 0.0), writes=[zS_b], tag="zS")
    for i in range(4):
        P.dma("sp", lambda e, i=i: e.dma_start(
            out=sh_S[i * 8192:(i + 1) * 8192, 0:1].rearrange("(p g) one -> p (g one)", p=128),
            in_=zS[:, :]),
            reads=[zS_b], writes=[b_shS.sub(i * 8192 * 4, (i + 1) * 8192 * 4)], tag=f"zS{i}")
    ar.release()

    ar.mark()
    qkvw, qkvw_b = ar.alloc(128, 8 * HID, BF16, "qkvw")
    projw, projw_b = ar.alloc(128, 8 * D, BF16, "projw")
    for f in range(8):
        P.dma("pool", lambda e, f=f: e.dma_start(out=qkvw[:, f * HID:(f + 1) * HID],
                                                 in_=qkv_wT[f * 128:(f + 1) * 128, :]),
              reads=[bI["qkv_wT"]], writes=[qkvw_b.sub(f * HID * 2, (f + 1) * HID * 2)],
              tag=f"ld_qkvw{f}")
        P.dma("pool", lambda e, f=f: e.dma_start(out=projw[:, f * D:(f + 1) * D],
                                                 in_=proj_wT[f * 128:(f + 1) * 128, :]),
              reads=[bI["proj_wT"]], writes=[projw_b.sub(f * D * 2, (f + 1) * D * 2)],
              tag=f"ld_projw{f}")

    xck, xck_b = ar.alloc(128, NSUB * D, F32, "xck")
    xnk, xnk_b = ar.alloc(128, NSUB * D, F32, "xnk")
    scr4, scr4_b = ar.alloc(128, D, F32, "scr4")
    st8, st8_b = ar.alloc(128, 8, F32, "st8")
    ln1T, ln1T_b = ar.alloc(128, 8 * CHK, BF16, "ln1T")
    qkvT, qkvT_b = ar.alloc(128, 8 * CHK, BF16, "qkvT")
    kvs, kvs_b = [], []
    for s in range(4):
        a_, b_ = ar.alloc(128, 8 * CHK, BF16, f"kv{s}")
        kvs.append(a_); kvs_b.append(b_)
    zbuf, zbuf_b = ar.alloc(16, 4 * CHK, F32, "zbuf")
    zmx, zmx_b = ar.alloc(16, CHK, F32, "zmx")
    wrows, wrows_b = ar.alloc(16, 4 * CHK, BF16, "wrows")
    kd, kd_b = ar.alloc(128, CHK, BF16, "kd")
    zp, zp_b = ar.alloc(128, 8 * CHK, BF16, "zp")
    aom, aom_b = ar.alloc(128, CHK, F32, "aom")
    tmpf, tmpf_b = ar.alloc(128, D, F32, "tmpf")
    haak, haak_b = ar.alloc(128, D, F32, "haak")
    x2b, x2b_b = ar.alloc(128, D, BF16, "x2b")
    lgrow, lgrow_b = ar.alloc(128, NSUB * NEXP, F32, "lgrow")
    topsc, topsc_b = ar.alloc(128, 4 * NEXP, F32, "topsc")
    msb, msb_b = ar.alloc(128, NEXP, BF16, "msb")
    idxc, idxc_b = ar.alloc(128, 8, F32, "idxc")
    idxu, idxu_b = ar.alloc(128, 8, U32, "idxu")

    assert ar.off <= ar.cap, ar.off

    def tcol(apx, t, w):
        return apx[:, t * w:(t + 1) * w]

    for c in range(NCHK):
        r0 = c * CHK
        for s in range(4):
            sfx = f"_{c}_{s}"
            for q in range(NSUB):
                P.dma("sp", lambda e, s=s, r0=r0, q=q: e.dma_start(
                    out=xck[:, q * D:(q + 1) * D],
                    in_=xs[s, r0 + q * 128:r0 + (q + 1) * 128, :]),
                    reads=[bI["xs"]], writes=[xck_b], tag=f"ldx{q}" + sfx)

            def haaidx(e, s=s, r0=r0):
                ch = CH["dve"]
                for q in range(NSUB):
                    ch(e, e.tensor_scalar(idxc[:, q:q + 1], selc[:, s:s + 1], -HUGE, HUGE,
                                          ALU.mult, ALU.add))
                    ch(e, e.tensor_tensor(idxc[:, q:q + 1], idxc[:, q:q + 1], iotacol[:, 0:1],
                                          ALU.add))
                    ch(e, e.tensor_scalar(idxc[:, q:q + 1], idxc[:, q:q + 1],
                                          float(r0 + q * 128), None, ALU.add))
                    ch(e, e.tensor_tensor(idxc[:, 4 + q:5 + q], iotacol[:, 0:1],
                                          hp8k[:, 0:1], ALU.add))
                    ch(e, e.tensor_scalar(idxc[:, 4 + q:5 + q], idxc[:, 4 + q:5 + q],
                                          float(s * NH + r0 + q * 128), None, ALU.add))
                return e.tensor_copy(idxu[:, 0:8], idxc[:, 0:8])
            P.op("dve", haaidx, reads=[selc_b, iotacol_b, idxc_b, idxu_b, hp8k_b],
                 writes=[idxc_b, idxu_b], tag="hidx" + sfx)
            for q in range(NSUB):
                P.dma("pool", lambda e, q=q: e.indirect_dma_start(
                    pr_x[:, :], bass.IndirectOffsetOnAxis(idxu[:, q:q + 1], 0),
                    xck[:, q * D:(q + 1) * D], None,
                    bounds_check=rbc_nh, oob_is_err=False),
                    reads=[xck_b, idxu_b], writes=[b_x], tag=f"xsp{q}" + sfx)

            for q in range(NSUB):
                xq = xck[:, q * D:(q + 1) * D]
                xnq = xnk[:, q * D:(q + 1) * D]

                def ln(e, xq=xq, xnq=xnq):
                    ch = CH["dve"]
                    ch(e, e.reduce_sum(st8[:, 0:1], xq, axis=AX.X))
                    ch(e, e.tensor_scalar(st8[:, 0:1], st8[:, 0:1], 1.0 / D, None, ALU.mult))
                    ch(e, e.tensor_scalar(xnq, xq, st8[:, 0:1], None, ALU.subtract))
                    ch(e, e.tensor_tensor(scr4[:, :], xnq, xnq, ALU.mult))
                    ch(e, e.reduce_sum(st8[:, 1:2], scr4[:, :], axis=AX.X))
                    return e.tensor_scalar(st8[:, 1:2], st8[:, 1:2], 1.0 / D, None, ALU.mult)
                P.op("dve", ln, reads=[xck_b, st8_b, xnk_b, scr4_b],
                     writes=[xnk_b, scr4_b, st8_b], tag="ln" + sfx + f"{q}")
                P.op("act", lambda e: e.activation(st8[:, 2:3], st8[:, 1:2], AF.Sqrt,
                                                   bias=epscol[:, 0:1], scale=1.0),
                     reads=[st8_b, epscol_b], writes=[st8_b], tag="lns" + sfx + f"{q}")

                def fin(e, xq=xq, xnq=xnq):
                    ch = CH["dve"]
                    ch(e, e.reciprocal(st8[:, 3:4], st8[:, 2:3]))
                    ch(e, e.tensor_scalar(xnq, xnq, st8[:, 3:4], None, ALU.mult))
                    ch(e, e.tensor_tensor(st8[:, 4:5], st8[:, 3:4], st8[:, 3:4], ALU.mult))
                    return e.tensor_tensor(st8[:, 4:5], st8[:, 4:5], st8[:, 1:2], ALU.mult)
                P.op("dve", fin, reads=[st8_b, xnk_b], writes=[st8_b, xnk_b],
                     tag="lnf" + sfx + f"{q}")
                P.op("act", lambda e: e.activation(st8[:, 5:6], st8[:, 4:5], AF.Sqrt,
                                                   bias=epscol[:, 0:1], scale=1.0),
                     reads=[st8_b, epscol_b], writes=[st8_b], tag="lns2" + sfx + f"{q}")

                def mku(e, xq=xq, xnq=xnq):
                    ch = CH["dve"]
                    ch(e, e.reciprocal(st8[:, 6:7], st8[:, 5:6]))
                    return e.tensor_scalar(xq, xnq, st8[:, 6:7], None, ALU.mult)
                P.op("dve", mku, reads=[st8_b, xnk_b], writes=[xck_b, st8_b],
                     tag="mku" + sfx + f"{q}")

            for q in range(NSUB):
                P.dma("pool", lambda e, q=q: e.indirect_dma_start(
                    pr_xn[:, :], bass.IndirectOffsetOnAxis(idxu[:, q:q + 1], 0),
                    xnk[:, q * D:(q + 1) * D], None,
                    bounds_check=rbc_nh, oob_is_err=False),
                    reads=[xnk_b, idxu_b], writes=[b_xn], tag=f"nsp2{q}" + sfx)

            for f in range(8):
                for q in range(NSUB):
                    P.op("pe", lambda e, f=f, q=q: e.transpose(
                        ps[3][:, q * 128:(q + 1) * 128],
                        xck[:, q * D + f * 128:q * D + (f + 1) * 128], s_idf[:, :]),
                        reads=[xck_b, s_idf_b], writes=[psb[3]], tag=f"trU{f}{q}" + sfx)
                P.op("act", lambda e, f=f: e.activation(
                    tcol(ln1T, f, CHK), ps[3][:, 0:CHK], AF.Identity,
                    bias=cols[:, C_LN1B + f:C_LN1B + f + 1],
                    scale=cols[:, C_LN1W + f:C_LN1W + f + 1]),
                    reads=[psb[3], cols_b], writes=[ln1T_b], tag=f"l1T{f}" + sfx)
                P.op("act", lambda e, f=f: e.activation(
                    tmpf.bitcast(BF16)[:, 0:CHK], ps[3][:, 0:CHK], AF.Identity,
                    bias=cols[:, C_NB + f:C_NB + f + 1],
                    scale=cols[:, C_NG + f:C_NG + f + 1]),
                    reads=[psb[3], cols_b], writes=[tmpf_b], tag=f"nrm{f}" + sfx)
                P.dma("sp", lambda e, f=f, s=s, r0=r0: e.dma_start(
                    out=pr_norm[s * HID + f * 128:s * HID + (f + 1) * 128, r0:r0 + CHK],
                    in_=tmpf.bitcast(BF16)[:, 0:CHK]),
                    reads=[tmpf_b], writes=[b_norm], tag=f"nsp{f}" + sfx)

            for f in range(16):
                for cc in range(8):
                    P.op("pe", lambda e, f=f, cc=cc: e.matmul(
                        ps[0][:, 0:CHK],
                        qkvw[:, cc * HID + f * 128:cc * HID + (f + 1) * 128],
                        tcol(ln1T, cc, CHK), start=(cc == 0), stop=(cc == 7)),
                        reads=[qkvw_b, ln1T_b], writes=[psb[0]], tag=f"qkv{f}" + sfx)
                if f % 2 == 0:
                    P.op("dve", lambda e, f=f: e.tensor_copy(tcol(qkvT, f // 2, CHK),
                                                             ps[0][:, 0:CHK]),
                         reads=[psb[0]], writes=[qkvT_b], tag=f"qkve{f}" + sfx)
                else:
                    P.op("dve", lambda e, f=f, s=s: e.tensor_copy(tcol(kvs[s], f // 2, CHK),
                                                                  ps[0][:, 0:CHK]),
                         reads=[psb[0]], writes=[kvs_b[s]], tag=f"qkvo{f}" + sfx)

            for t in range(s + 1):
                for m in range(8):
                    def zprod(e, m=m, t=t):
                        ch = CH["dve"]
                        ch(e, e.tensor_copy(kd[0:64, :], kvs[t][0:64, m * CHK:(m + 1) * CHK]))
                        ch(e, e.tensor_copy(kd[64:128, :], kvs[t][0:64, m * CHK:(m + 1) * CHK]))
                        return e.tensor_tensor(
                            zp[:, m * CHK:(m + 1) * CHK],
                            qkvT[:, m * CHK:(m + 1) * CHK], kd[:, :], ALU.mult)
                    P.op("dve", zprod, reads=[qkvT_b, kvs_b[t], kd_b],
                         writes=[zp_b, kd_b], tag=f"zp{m}_{t}" + sfx)
                for m in range(8):
                    P.op("pe", lambda e, m=m: e.matmul(
                        ps[1][0:16, 0:CHK], s_e16[:, m * 16:(m + 1) * 16],
                        zp[:, m * CHK:(m + 1) * CHK], start=(m == 0), stop=(m == 7)),
                        reads=[zp_b, s_e16_b], writes=[psb[1]], tag=f"zr{m}_{t}" + sfx)
                P.op("dve", lambda e, t=t: e.tensor_copy(zbuf[0:16, t * CHK:(t + 1) * CHK],
                                                         ps[1][0:16, 0:CHK]),
                     reads=[psb[1]], writes=[zbuf_b], tag=f"ze{t}" + sfx)

            def soft(e, s=s):
                ch = CH["dve"]
                ch(e, e.tensor_copy(zmx[0:16, :], zbuf[0:16, 0:CHK]))
                for t in range(1, s + 1):
                    ch(e, e.tensor_tensor(zmx[0:16, :], zmx[0:16, :],
                                          zbuf[0:16, t * CHK:(t + 1) * CHK], ALU.max))
                for t in range(s + 1):
                    ch(e, e.tensor_tensor(zbuf[0:16, t * CHK:(t + 1) * CHK],
                                          zbuf[0:16, t * CHK:(t + 1) * CHK],
                                          zmx[0:16, :], ALU.subtract))
                return e.engine_nop()
            P.op("dve", soft, reads=[zbuf_b, zmx_b], writes=[zbuf_b, zmx_b], tag="sft" + sfx)

            def expo(e, s=s):
                last = None
                for t in range(s + 1):
                    last = e.activation(zbuf[0:16, t * CHK:(t + 1) * CHK],
                                        zbuf[0:16, t * CHK:(t + 1) * CHK], AF.Exp)
                    if t < s:
                        CH["act"](e, last)
                return last
            P.op("act", expo, reads=[zbuf_b], writes=[zbuf_b], tag="exp" + sfx)

            def den(e, s=s):
                ch = CH["dve"]
                ch(e, e.tensor_copy(zmx[0:16, :], zbuf[0:16, 0:CHK]))
                for t in range(1, s + 1):
                    ch(e, e.tensor_tensor(zmx[0:16, :], zmx[0:16, :],
                                          zbuf[0:16, t * CHK:(t + 1) * CHK], ALU.add))
                ch(e, e.reciprocal(zmx[0:16, :], zmx[0:16, :]))
                for t in range(s + 1):
                    if t < s:
                        ch(e, e.tensor_tensor(wrows[0:16, t * CHK:(t + 1) * CHK],
                                              zbuf[0:16, t * CHK:(t + 1) * CHK],
                                              zmx[0:16, :], ALU.mult))
                    else:
                        return e.tensor_tensor(wrows[0:16, t * CHK:(t + 1) * CHK],
                                               zbuf[0:16, t * CHK:(t + 1) * CHK],
                                               zmx[0:16, :], ALU.mult)
            P.op("dve", den, reads=[zbuf_b, zmx_b, wrows_b], writes=[zmx_b, wrows_b],
                 tag="den" + sfx)

            for m in range(8):
                for t in range(s + 1):
                    P.op("pe", lambda e, m=m, t=t: e.matmul(
                        ps[2][:, 0:CHK], s_f16[0:16, m * 128:(m + 1) * 128],
                        wrows[0:16, t * CHK:(t + 1) * CHK], start=True, stop=True),
                        reads=[wrows_b, s_f16_b], writes=[psb[2]], tag=f"wb{m}{t}" + sfx)
                    for j in range(2):
                        def acc(e, m=m, j=j, t=t):
                            ch = CH["dve"]
                            ch(e, e.tensor_tensor(
                                tmpf[64 * j:64 * j + 64, 0:CHK],
                                ps[2][64 * j:64 * j + 64, 0:CHK],
                                kvs[t][64:128, m * CHK:(m + 1) * CHK], ALU.mult))
                            if t == 0:
                                return e.tensor_copy(aom[64 * j:64 * j + 64, 0:CHK],
                                                     tmpf[64 * j:64 * j + 64, 0:CHK])
                            return e.tensor_tensor(aom[64 * j:64 * j + 64, 0:CHK],
                                                   aom[64 * j:64 * j + 64, 0:CHK],
                                                   tmpf[64 * j:64 * j + 64, 0:CHK], ALU.add)
                        P.op("dve", acc, reads=[psb[2], kvs_b[t], tmpf_b, aom_b],
                             writes=[tmpf_b, aom_b], tag=f"ac{m}{j}{t}" + sfx)
                P.op("dve", lambda e, m=m: e.tensor_copy(tcol(zp, m, CHK), aom[:, 0:CHK]),
                     reads=[aom_b], writes=[zp_b], tag=f"aob{m}" + sfx)
                if debug and s == 0:
                    P.dma("sp", lambda e, m=m, r0=r0: e.dma_start(
                        out=dbg["attn"][m * 128:(m + 1) * 128, r0:r0 + CHK],
                        in_=aom[:, 0:CHK]),
                        reads=[aom_b], writes=[b_dbg["attn"]], tag=f"dba{m}" + sfx)


            for q in range(NSUB):
                qfx = sfx + f"q{q}"
                for half in range(2):
                    for cc in range(8):
                        P.op("pe", lambda e, q=q, half=half, cc=cc: e.matmul(
                            pd[0][:, half * 512:(half + 1) * 512],
                            zp[:, cc * CHK + q * 128:cc * CHK + q * 128 + 128],
                            projw[:, cc * D + half * 512:cc * D + (half + 1) * 512],
                            start=(cc == 0), stop=(cc == 7)),
                            reads=[zp_b, projw_b], writes=[pdb[0]], tag=f"pj{half}{cc}" + qfx)
                P.op("dve", lambda e, q=q: e.tensor_tensor(
                    haak[:, :], pd[0][:, 0:D], xnk[:, q * D:(q + 1) * D], ALU.add),
                    reads=[pdb[0], xnk_b], writes=[haak_b], tag="haa" + qfx)
                P.dma("pool", lambda e, q=q: e.indirect_dma_start(
                    pr_haa[:, :], bass.IndirectOffsetOnAxis(idxu[:, q:q + 1], 0),
                    haak[:, :], None, bounds_check=rbc_nh, oob_is_err=False),
                    reads=[haak_b, idxu_b], writes=[b_haa], tag="hsp" + qfx)

                def ln2(e):
                    ch = CH["dve"]
                    ch(e, e.reduce_sum(st8[:, 0:1], haak[:, :], axis=AX.X))
                    ch(e, e.tensor_scalar(st8[:, 0:1], st8[:, 0:1], 1.0 / D, None, ALU.mult))
                    ch(e, e.tensor_scalar(tmpf[:, 0:D], haak[:, :], st8[:, 0:1],
                                          None, ALU.subtract))
                    ch(e, e.tensor_tensor(scr4[:, :], tmpf[:, 0:D], tmpf[:, 0:D], ALU.mult))
                    ch(e, e.reduce_sum(st8[:, 1:2], scr4[:, :], axis=AX.X))
                    return e.tensor_scalar(st8[:, 1:2], st8[:, 1:2], 1.0 / D, None, ALU.mult)
                P.op("dve", ln2, reads=[haak_b, tmpf_b, scr4_b, st8_b],
                     writes=[tmpf_b, scr4_b, st8_b], tag="ln2" + qfx)
                P.op("act", lambda e: e.activation(st8[:, 2:3], st8[:, 1:2], AF.Sqrt,
                                                   bias=epscol[:, 0:1], scale=1.0),
                     reads=[st8_b, epscol_b], writes=[st8_b], tag="ln2s" + qfx)

                def c2(e):
                    ch = CH["dve"]
                    ch(e, e.reciprocal(st8[:, 3:4], st8[:, 2:3]))
                    ch(e, e.tensor_scalar(x2b[:, :], tmpf[:, 0:D], st8[:, 3:4],
                                          None, ALU.mult))
                    return e.tensor_scalar(tmpf[:, 0:D], tmpf[:, 0:D], st8[:, 3:4],
                                           None, ALU.mult)
                P.op("dve", c2, reads=[tmpf_b, st8_b], writes=[x2b_b, tmpf_b, st8_b],
                     tag="c2" + qfx)

                P.dma("pool", lambda e, q=q: e.indirect_dma_start(
                    sh_x2[:, :], bass.IndirectOffsetOnAxis(idxu[:, 4 + q:5 + q], 0),
                    x2b[:, :], None, bounds_check=rbc_x2, oob_is_err=False),
                    reads=[x2b_b, idxu_b], writes=[b_shx2], tag="wx2" + qfx)

                for f in range(8):
                    P.op("pe", lambda e, f=f: e.transpose(
                        pd[1][:, f * 128:(f + 1) * 128], tmpf[:, f * 128:(f + 1) * 128],
                        s_idf[:, :]),
                        reads=[tmpf_b, s_idf_b], writes=[pdb[1]], tag=f"tx2{f}" + qfx)
                for f in range(8):
                    P.op("act", lambda e, f=f: e.activation(
                        scr4[:, f * 128:(f + 1) * 128], pd[1][:, f * 128:(f + 1) * 128],
                        AF.Identity, bias=cols[:, C_LN2B + f:C_LN2B + f + 1],
                        scale=cols[:, C_LN2W + f:C_LN2W + f + 1]),
                        reads=[pdb[1], cols_b], writes=[scr4_b], tag=f"x2T{f}" + qfx)
                for cc in range(8):
                    P.op("pe", lambda e, cc=cc: e.matmul(
                        ps[1][:, 0:NEXP], scr4[:, cc * 128:(cc + 1) * 128],
                        routw[:, cc * 8:(cc + 1) * 8], start=(cc == 0), stop=(cc == 7)),
                        reads=[scr4_b, routw_b], writes=[psb[1]], tag=f"lg{cc}" + qfx)
                P.op("dve", lambda e, q=q: e.tensor_copy(lgrow[:, q * 8:(q + 1) * 8],
                                                         ps[1][:, 0:NEXP]),
                     reads=[psb[1]], writes=[lgrow_b], tag="lge" + qfx)
                if debug:
                    P.dma("sp", lambda e, q=q, r0=r0, s=s: e.dma_start(
                        out=dbg["logits"][s * NH + r0 + q * 128:s * NH + r0 + (q + 1) * 128, :],
                        in_=lgrow[:, q * 8:(q + 1) * 8]),
                        reads=[lgrow_b], writes=[b_dbg["logits"]], tag="dlg" + qfx)
                if debug and s == 0:
                    P.dma("sp", lambda e, q=q, r0=r0: e.dma_start(
                        out=dbg["haa"][r0 + q * 128:r0 + (q + 1) * 128, :], in_=haak[:, :]),
                        reads=[haak_b], writes=[b_dbg["haa"]], tag="dh" + qfx)
                    P.dma("sp", lambda e, q=q, r0=r0: e.dma_start(
                        out=dbg["xn"][r0 + q * 128:r0 + (q + 1) * 128, :],
                        in_=xnk[:, q * D:(q + 1) * D]),
                        reads=[xnk_b], writes=[b_dbg["xn"]], tag="dxn" + qfx)

                tix = c * NSUB + q
                zq = lgrow[:, q * 8:(q + 1) * 8]

                def top2(e, zq=zq):
                    ch = CH["dve"]
                    ch(e, e.reduce_max(st8[:, 0:1], zq, axis=AX.X))
                    ch(e, e.tensor_scalar(topsc[:, 0:8], zq, st8[:, 0:1], None, ALU.is_equal))
                    ch(e, e.scalar_tensor_tensor(topsc[:, 8:16], topsc[:, 0:8], -HUGE, zq,
                                                 ALU.mult, ALU.add))
                    ch(e, e.reduce_max(st8[:, 1:2], topsc[:, 8:16], axis=AX.X))
                    ch(e, e.tensor_scalar(topsc[:, 16:24], topsc[:, 8:16], st8[:, 1:2],
                                          None, ALU.is_equal))
                    ch(e, e.tensor_tensor(st8[:, 2:3], st8[:, 0:1], st8[:, 1:2], ALU.subtract))
                    ch(e, e.tensor_tensor(topsc[:, 24:32], topsc[:, 0:8], topsc[:, 16:24],
                                          ALU.add))
                    return e.tensor_copy(msb[:, 0:8], topsc[:, 24:32])
                P.op("dve", top2, reads=[lgrow_b, topsc_b, st8_b, msb_b],
                     writes=[topsc_b, st8_b, msb_b], tag="t2" + qfx)

                P.op("pe", lambda e: e.matmul(ps[1][0:1, 0:8], s_onescb[:, 0:1], msb[:, 0:8],
                                              start=True, stop=True),
                     reads=[msb_b, s_onescb_b], writes=[psb[1]], tag="cm" + qfx)
                first = (c == 0 and q == 0)
                P.op("dve", lambda e, s=s, first=first: e.tensor_copy(
                    cntrow[0:1, s * 8:(s + 1) * 8], ps[1][0:1, 0:8]) if first else
                    e.tensor_tensor(cntrow[0:1, s * 8:(s + 1) * 8],
                                    cntrow[0:1, s * 8:(s + 1) * 8], ps[1][0:1, 0:8], ALU.add),
                    reads=[psb[1]] + ([] if first else [cntrow_b]), writes=[cntrow_b],
                    tag="ca" + qfx)

                def ret(e, s=s, tix=tix):
                    ch = CH["dve"]
                    if s == 0:
                        ch(e, e.tensor_scalar(dzo[:, tix:tix + 1], st8[:, 2:3],
                                              selc[:, s:s + 1], None, ALU.mult))
                        ch(e, e.tensor_scalar(m1o[:, tix * 8:(tix + 1) * 8], topsc[:, 0:8],
                                              selc[:, s:s + 1], None, ALU.mult))
                        return e.tensor_scalar(m2o[:, tix * 8:(tix + 1) * 8], topsc[:, 16:24],
                                               selc[:, s:s + 1], None, ALU.mult)
                    ch(e, e.scalar_tensor_tensor(dzo[:, tix:tix + 1], st8[:, 2:3],
                                                 selc[:, s:s + 1], dzo[:, tix:tix + 1],
                                                 ALU.mult, ALU.add))
                    ch(e, e.scalar_tensor_tensor(m1o[:, tix * 8:(tix + 1) * 8], topsc[:, 0:8],
                                                 selc[:, s:s + 1], m1o[:, tix * 8:(tix + 1) * 8],
                                                 ALU.mult, ALU.add))
                    return e.scalar_tensor_tensor(m2o[:, tix * 8:(tix + 1) * 8], topsc[:, 16:24],
                                                  selc[:, s:s + 1], m2o[:, tix * 8:(tix + 1) * 8],
                                                  ALU.mult, ALU.add)
                P.op("dve", ret, reads=[st8_b, topsc_b, selc_b, dzo_b, m1o_b, m2o_b],
                     writes=[dzo_b, m1o_b, m2o_b], tag="ret" + qfx)


    if debug:
        P.dma("sp", lambda e: e.dma_start(out=dbg["cnt"][0:1, :], in_=cntrow[0:1, :]),
              reads=[cntrow_b], writes=[b_dbg["cnt"]], tag="dcnt")

    def wcnt(e):
        hv = vload(e, hint[0:1, 0:1], 0, 1)
        return e.dma_start(out=sh_cnt[bass.ts(hv, 1), :], in_=cntrow[0:1, :])
    P.dma("pool", wcnt, reads=[cntrow_b, hint_b], writes=[b_shcnt], tag="wcnt")

    barrier(1, [b_shcnt, b_shx2, b_haa, b_norm, b_x, b_xn], tag="B1")

    def finish(mini=True):
        if mini:
            mini_tail()
        P.op("pool", lambda e: e.engine_nop(), reads=[b_out], tag="fence")
        stream = P.build_streams(nc)
        with nc.Block() as blk:
            blk.tensor(stream("pe"))
            blk.scalar(stream("act"))
            blk.vector(stream("dve"))
            blk.gpsimd(stream("pool"))
            blk.sync(stream("sp"))
        return nc, dbg

    def mini_tail():
        ar.mark()
        oacx, oacx_b = ar.alloc(128, D, F32, "oacx")
        xre2, xre2_b = ar.alloc(128, D, F32, "xre2")
        hre2, hre2_b = ar.alloc(128, D, F32, "hre2")
        for t in range(16):
            r0 = t * 128
            P.dma("sp", lambda e, r0=r0: e.dma_start(out=xre2[:, :], in_=pr_x[r0:r0 + 128, :]),
                  reads=[b_x, btok[0]], writes=[xre2_b], tag=f"tx{t}")
            P.dma("sp", lambda e, r0=r0: e.dma_start(out=hre2[:, :], in_=pr_haa[r0:r0 + 128, :]),
                  reads=[b_haa, btok[0]], writes=[hre2_b], tag=f"th{t}")
            P.op("dve", lambda e: e.tensor_tensor(oacx[:, :], xre2[:, :], hre2[:, :], ALU.add),
                 reads=[xre2_b, hre2_b, oacx_b], writes=[oacx_b], tag=f"ta{t}")
            P.dma("sp", lambda e, r0=r0: e.dma_start(out=out[r0:r0 + 128, :], in_=oacx[:, :]),
                  reads=[oacx_b], writes=[b_out.sub(r0 * D * 4, (r0 + 128) * D * 4)],
                  tag=f"to{t}")

    if PH <= 1:
        return finish()

    ar.release()
    ar.mark()
    cA, cA_b = ar.alloc(1, 32, F32, "cA")
    cB, cB_b = ar.alloc(1, 32, F32, "cB")
    cG, cG_b = ar.alloc(1, 32, F32, "cG")
    rows8, rows8_b = ar.alloc(1, 64, F32, "rows8")
    colsc, colsc_b = ar.alloc(8, 8, F32, "colsc")
    msum2, msum2_b = ar.alloc(128, 8, BF16, "msum2")
    totrow, totrow_b = ar.alloc(1, 128, F32, "totrow")
    totcol, totcol_b = ar.alloc(128, 1, F32, "totcol")
    totmask, totmask_b = ar.alloc(128, 8, BF16, "totmask")
    basefm, basefm_b = ar.alloc(8, 16, F32, "basefm")
    posw, posw_b = ar.alloc(8, 6 * 128, F32, "posw")
    mkT, mkT_b = ar.alloc(8, 2 * 128, F32, "mkT")
    srow, srow_b = ar.alloc(1, 128, F32, "srow")
    scol, scol_b = ar.alloc(128, 4, F32, "scol")
    scatv, scatv_b = ar.alloc(128, 1, F32, "scatv")
    hpcol, hpcol_b = ar.alloc(128, 1, F32, "hpcol")
    sidxu2, sidxu2_b = ar.alloc(128, 2, U32, "sidxu2")
    p1all, p1all_b = ar.alloc(128, 16, F32, "p1all")

    csum, csum_b = ar.alloc(1, 64, F32, "csum")

    cpol = nc.alloc_semaphore("cpol")

    def ldcnt(e):
        e.tensor_copy(csum[0:1, 63:64], tot16k[0:1, 0:1])
        with e.register("ndc") as ndr, e.register("ctg") as tgt:
            e.reg_mov(ndr, 1)
            e.reg_mov(tgt, 0)
            with e.While(ndr):
                e.dma_start(out=cA[0:1, :], in_=sh_cnt[0:1, :]).then_inc(cpol, 16)
                e.dma_start(out=cB[0:1, :], in_=sh_cnt[1:2, :]).then_inc(cpol, 16)
                e.reg_add(tgt, tgt, 32)
                e.wait_ge(cpol, tgt)
                e.tensor_tensor(csum[0:1, 0:16], cA[0:1, 0:16], cA[0:1, 16:32], ALU.add)
                e.tensor_tensor(csum[0:1, 32:48], cB[0:1, 0:16], cB[0:1, 16:32], ALU.add)
                e.tensor_tensor(csum[0:1, 0:8], csum[0:1, 0:8], csum[0:1, 8:16], ALU.add)
                e.tensor_tensor(csum[0:1, 32:40], csum[0:1, 32:40], csum[0:1, 40:48], ALU.add)
                e.tensor_tensor(csum[0:1, 0:4], csum[0:1, 0:4], csum[0:1, 4:8], ALU.add)
                e.tensor_tensor(csum[0:1, 32:36], csum[0:1, 32:36], csum[0:1, 36:40], ALU.add)
                e.tensor_tensor(csum[0:1, 0:2], csum[0:1, 0:2], csum[0:1, 2:4], ALU.add)
                e.tensor_tensor(csum[0:1, 32:34], csum[0:1, 32:34], csum[0:1, 34:36], ALU.add)
                e.tensor_tensor(csum[0:1, 0:1], csum[0:1, 0:1], csum[0:1, 1:2], ALU.add)
                e.tensor_tensor(csum[0:1, 32:33], csum[0:1, 32:33], csum[0:1, 33:34], ALU.add)
                e.tensor_tensor(csum[0:1, 0:1], csum[0:1, 0:1], tot16k[0:1, 0:1], ALU.subtract)
                e.tensor_tensor(csum[0:1, 32:33], csum[0:1, 32:33], tot16k[0:1, 0:1], ALU.subtract)
                e.tensor_tensor(csum[0:1, 0:1], csum[0:1, 0:1], csum[0:1, 0:1], ALU.mult)
                e.tensor_tensor(csum[0:1, 32:33], csum[0:1, 32:33], csum[0:1, 32:33], ALU.mult)
                e.tensor_tensor(csum[0:1, 63:64], csum[0:1, 0:1], csum[0:1, 32:33], ALU.add)
                nd = e.value_load(csum[0:1, 63:64].bitcast(I32))
                e.reg_mov(ndr, nd)
        return e.engine_nop()
    P.op("pool", ldcnt, reads=[b_shcnt, btok[0], tot16k_b, cA_b, cB_b, csum_b],
         writes=[cA_b, cB_b, csum_b], tag="ldcnt")
    P.op("dve", lambda e: e.tensor_tensor(cG[0:1, :], cA[0:1, :], cB[0:1, :], ALU.add),
         reads=[cA_b, cB_b], writes=[cG_b], tag="cG")
    if debug:
        P.dma("sp", lambda e: e.dma_start(out=dbg["ngi"][0:1, 8:40], in_=cB[0:1, :]),
              reads=[cB_b], writes=[b_dbg["ngi"]], tag="dcB")

    def mkrows(e):
        ch = CH["dve"]
        ch(e, e.tensor_scalar(rows8[0:1, 0:8], cG[0:1, 0:8], selr[0:1, 4:5], None, ALU.mult))
        for s in range(1, 4):
            ch(e, e.scalar_tensor_tensor(rows8[0:1, 0:8], cG[0:1, 8 * s:8 * s + 8],
                                         selr[0:1, 4 + s:5 + s], rows8[0:1, 0:8],
                                         ALU.mult, ALU.add))
        ch(e, e.tensor_scalar(rows8[0:1, 8:16], cA[0:1, 0:8], selr[0:1, 0:1], None, ALU.mult))
        for s in range(1, 4):
            ch(e, e.scalar_tensor_tensor(rows8[0:1, 8:16], cA[0:1, 8 * s:8 * s + 8],
                                         selr[0:1, s:s + 1], rows8[0:1, 8:16],
                                         ALU.mult, ALU.add))
        ch(e, e.tensor_scalar(rows8[0:1, 8:16], rows8[0:1, 8:16], selr[0:1, 8:9],
                              None, ALU.mult))
        ch(e, e.tensor_scalar(rows8[0:1, 16:24], cG[0:1, 0:8], selr[0:1, 0:1], None, ALU.mult))
        for s in range(1, 4):
            ch(e, e.scalar_tensor_tensor(rows8[0:1, 16:24], cG[0:1, 8 * s:8 * s + 8],
                                         selr[0:1, s:s + 1], rows8[0:1, 16:24],
                                         ALU.mult, ALU.add))
        ch(e, e.tensor_scalar(rows8[0:1, 24:28], rows8[0:1, 16:20], selr[0:1, 8:9], None,
                              ALU.mult))
        ch(e, e.tensor_tensor(rows8[0:1, 28:32], rows8[0:1, 16:20], rows8[0:1, 24:28],
                              ALU.subtract))
        ch(e, e.scalar_tensor_tensor(rows8[0:1, 28:32], rows8[0:1, 20:24], selr[0:1, 8:9],
                                     rows8[0:1, 28:32], ALU.mult, ALU.add))
        ch(e, e.tensor_scalar(rows8[0:1, 32:36], rows8[0:1, 28:32], 0.0, None, ALU.is_gt))
        for g in range(1, NGROUP):
            ch(e, e.tensor_scalar(rows8[0:1, 36:40], rows8[0:1, 28:32], float(512 * g),
                                  None, ALU.is_gt))
            ch(e, e.tensor_tensor(rows8[0:1, 32:36], rows8[0:1, 32:36], rows8[0:1, 36:40],
                                  ALU.add))
        return e.tensor_copy(ngi[0:1, 0:4], rows8[0:1, 32:36])
    P.op("dve", mkrows, reads=[cG_b, cA_b, selr_b, rows8_b, ngi_b],
         writes=[rows8_b, ngi_b], tag="mkrows")

    for (cix, off) in [(0, 0), (1, 8)]:
        P.op("pe", lambda e, cix=cix, off=off: e.matmul(
            ps[0][0:8, 0:1], rows8[0:1, off:off + 8], s_ones1f[0:1, 0:1],
            start=True, stop=True),
            reads=[rows8_b, s_ones1f_b], writes=[psb[0]], tag=f"pcol{cix}")
        P.op("dve", lambda e, cix=cix: e.tensor_copy(colsc[0:8, cix:cix + 1],
                                                     ps[0][0:8, 0:1]),
             reads=[psb[0]], writes=[colsc_b], tag=f"pcole{cix}")
    P.op("pe", lambda e: e.matmul(ps[0][:, 0:1], s_ones1f[0:1, 0:128], selr[0:1, 10:11],
                                  start=True, stop=True),
         reads=[selr_b, s_ones1f_b], writes=[psb[0]], tag="hpc")
    P.op("dve", lambda e: e.tensor_tensor(hpcol[:, 0:1], ps[0][:, 0:1], iotacol[:, 0:1],
                                          ALU.add),
         reads=[psb[0], iotacol_b], writes=[hpcol_b], tag="hpce")
    P.op("act", lambda e: e.activation(p1all[:, :], dzo[:, :], AF.Sigmoid),
         reads=[dzo_b], writes=[p1all_b], tag="p1")

    msall, msall_b = ar.alloc(128, 16 * 8, BF16, "msall")

    for t in range(16):
        tfx = f"_t{t}"
        P.op("dve", lambda e, t=t: e.tensor_tensor(msall[:, t * 8:(t + 1) * 8],
                                                   m1o[:, t * 8:(t + 1) * 8],
                                                   m2o[:, t * 8:(t + 1) * 8], ALU.add),
             reads=[m1o_b, m2o_b], writes=[msall_b], tag="ms" + tfx)
        P.op("pe", lambda e, t=t: e.matmul(ps[1][0:1, 0:8], s_onescb[:, 0:1],
                                           msall[:, t * 8:(t + 1) * 8],
                                           start=True, stop=True),
             reads=[msall_b, s_onescb_b], writes=[psb[1]], tag="tot" + tfx)
        P.op("dve", lambda e, t=t: e.tensor_copy(totrow[0:1, t * 8:(t + 1) * 8],
                                                 ps[1][0:1, 0:8]),
             reads=[psb[1]], writes=[totrow_b], tag="tote" + tfx)
    P.op("pe", lambda e: e.matmul(ps[1][:, 0:1], totrow[0:1, :], s_ones1f[0:1, 0:1],
                                  start=True, stop=True),
         reads=[totrow_b, s_ones1f_b], writes=[psb[1]], tag="totc")
    P.op("dve", lambda e: e.tensor_copy(totcol[:, 0:1], ps[1][:, 0:1]),
         reads=[psb[1]], writes=[totcol_b], tag="totce")
    P.op("dve", lambda e: e.tensor_scalar(totmask[:, 0:8], s_ohe[:, :], totcol[:, 0:1],
                                          None, ALU.mult),
         reads=[s_ohe_b, totcol_b], writes=[totmask_b], tag="tm")
    P.op("pe", lambda e: e.matmul(ps[1][0:8, 0:16], totmask[:, 0:8], s_t1[:, :],
                                  start=True, stop=True),
         reads=[totmask_b, s_t1_b], writes=[psb[1]], tag="bfm")
    P.op("dve", lambda e: e.tensor_copy(basefm[0:8, :], ps[1][0:8, 0:16]),
         reads=[psb[1]], writes=[basefm_b], tag="bfme")

    for t in range(16):
        tfx = f"_t{t}"
        P.op("pe", lambda e, t=t: e.matmul(ps[0][0:8, 0:128], msall[:, t * 8:(t + 1) * 8],
                                           s_l128[:, :], start=True, stop=True),
             reads=[msall_b, s_l128_b], writes=[psb[0]], tag="pfx" + tfx)

        def posops(e, t=t):
            ch = CH["dve"]
            ch(e, e.tensor_scalar(posw[0:8, 0:128], ps[0][0:8, 0:128],
                                  basefm[0:8, t:t + 1], None, ALU.add))
            ch(e, e.tensor_scalar(posw[0:8, 0:128], posw[0:8, 0:128],
                                  colsc[0:8, 1:2], None, ALU.add))
            ch(e, e.tensor_scalar(posw[0:8, 128:256], posw[0:8, 0:128],
                                  colsc[0:8, 0:1], None, ALU.add))
            ch(e, e.tensor_scalar(posw[0:8, 256:384], posw[0:8, 128:256], float(CAP),
                                  None, ALU.is_lt))
            ch(e, e.tensor_scalar(posw[0:8, 384:512], posw[0:8, 256:384], -HUGE, HUGE,
                                  ALU.mult, ALU.add))
            ch(e, e.tensor_scalar(posw[0:8, 512:640], posw[0:8, 0:128],
                                  s_estr[0:8, 0:1], None, ALU.add))
            return e.tensor_tensor(posw[0:8, 512:640], posw[0:8, 512:640],
                                   posw[0:8, 384:512], ALU.add)
        P.op("dve", posops, reads=[psb[0], basefm_b, colsc_b, s_estr_b, posw_b],
             writes=[posw_b], tag="pos" + tfx)
        P.op("dve", lambda e, t=t: e.tensor_scalar(scatv[:, 0:1], hpcol[:, 0:1],
                                                   float(t * 128 + 1), None, ALU.add),
             reads=[hpcol_b], writes=[scatv_b], tag="sv" + tfx)
        for kk, (mo, mo_b) in enumerate([(m1o, m1o_b), (m2o, m2o_b)]):
            kfx = tfx + f"k{kk}"
            P.op("pe", lambda e, mo=mo, t=t: e.transpose(
                pst_b[0:8, 0:128], mo[:, t * 8:(t + 1) * 8], s_idb[:, :]),
                reads=[mo_b, s_idb_b], writes=[psb[3]], tag="mkt" + kfx)
            P.op("dve", lambda e, kk=kk: e.tensor_copy(mkT[0:8, kk * 128:(kk + 1) * 128],
                                                       pst_b[0:8, 0:128]),
                 reads=[psb[3]], writes=[mkT_b], tag="mkte" + kfx)
            P.op("dve", lambda e, kk=kk: e.tensor_tensor(
                posw[0:8, 640:768], mkT[0:8, kk * 128:(kk + 1) * 128],
                posw[0:8, 512:640], ALU.mult),
                reads=[mkT_b, posw_b], writes=[posw_b], tag="rx" + kfx)
            P.op("pe", lambda e: e.matmul(ps[2][0:1, 0:128], s_ones8f[0:8, 0:1],
                                          posw[0:8, 640:768], start=True, stop=True),
                 reads=[posw_b, s_ones8f_b], writes=[psb[2]], tag="rs" + kfx)
            P.op("dve", lambda e: e.tensor_copy(srow[0:1, :], ps[2][0:1, 0:128]),
                 reads=[psb[2]], writes=[srow_b], tag="rse" + kfx)
            P.op("pe", lambda e: e.matmul(ps[2][:, 0:1], srow[0:1, :], s_ones1f[0:1, 0:1],
                                          start=True, stop=True),
                 reads=[srow_b, s_ones1f_b], writes=[psb[2]], tag="rc" + kfx)

            def scmake(e, kk=kk, t=t):
                ch = CH["dve"]
                ch(e, e.tensor_copy(scol[:, 0:1], ps[2][:, 0:1]))
                ch(e, e.tensor_copy(sidxu2[:, kk:kk + 1], scol[:, 0:1]))
                ch(e, e.tensor_scalar(scol[:, 1:2], scol[:, 0:1], float(ZROW), None,
                                      ALU.min))
                return e.tensor_copy(gidx_all[:, (t * 2 + kk):(t * 2 + kk) + 1],
                                     scol[:, 1:2])
            P.op("dve", scmake, reads=[psb[2], scol_b, sidxu2_b, gidx_all_b],
                 writes=[scol_b, sidxu2_b, gidx_all_b], tag="sc" + kfx)
            P.dma("pool", lambda e, kk=kk: e.indirect_dma_start(
                sh_S[:, :], bass.IndirectOffsetOnAxis(sidxu2[:, kk:kk + 1], 0),
                scatv[:, 0:1], None, bounds_check=rbc_S, oob_is_err=False),
                reads=[scatv_b, sidxu2_b, btok[0]], writes=[b_shS], tag="ssc" + kfx)
            P.op("dve", lambda e, kk=kk: e.tensor_tensor(
                posw[0:8, 640:768], mkT[0:8, kk * 128:(kk + 1) * 128],
                posw[0:8, 256:384], ALU.mult),
                reads=[mkT_b, posw_b], writes=[posw_b], tag="kx" + kfx)
            P.op("pe", lambda e: e.matmul(ps[2][0:1, 0:128], s_ones8f[0:8, 0:1],
                                          posw[0:8, 640:768], start=True, stop=True),
                 reads=[posw_b, s_ones8f_b], writes=[psb[2]], tag="ks" + kfx)
            P.op("dve", lambda e: e.tensor_copy(srow[0:1, :], ps[2][0:1, 0:128]),
                 reads=[psb[2]], writes=[srow_b], tag="kse" + kfx)
            P.op("pe", lambda e: e.matmul(ps[2][:, 0:1], srow[0:1, :], s_ones1f[0:1, 0:1],
                                          start=True, stop=True),
                 reads=[srow_b, s_ones1f_b], writes=[psb[2]], tag="kc" + kfx)

            def pkmake(e, kk=kk, t=t):
                ch = CH["dve"]
                if kk == 0:
                    ch(e, e.tensor_tensor(scol[:, 2:3], ps[2][:, 0:1], p1all[:, t:t + 1],
                                          ALU.mult))
                else:
                    ch(e, e.tensor_scalar(scol[:, 3:4], p1all[:, t:t + 1], -1.0, 1.0,
                                          ALU.mult, ALU.add))
                    ch(e, e.tensor_tensor(scol[:, 2:3], ps[2][:, 0:1], scol[:, 3:4],
                                          ALU.mult))
                return e.tensor_copy(pk_all[:, (t * 2 + kk):(t * 2 + kk) + 1], scol[:, 2:3])
            P.op("dve", pkmake, reads=[psb[2], p1all_b, scol_b, pk_all_b],
                 writes=[scol_b, pk_all_b], tag="pk" + kfx)

    if debug:
        P.dma("sp", lambda e: e.dma_start(out=dbg["gidx"][:, :], in_=gidx_all[:, :]),
              reads=[gidx_all_b], writes=[b_dbg["gidx"]], tag="dgix")
        P.dma("sp", lambda e: e.dma_start(out=dbg["pk"][:, :], in_=pk_all[:, :]),
              reads=[pk_all_b], writes=[b_dbg["pk"]], tag="dpk")

    if PH <= 2:
        return finish()
    ar.release()
    ACH = 512
    NACH = NH // ACH
    ar.mark()
    normch, normch_b = ar.alloc(128, 8 * ACH, BF16, "normch")
    lqAll, lqAll_b = ar.alloc(128, NH, BF16, "lqAll")
    lkvm, lkvm_b = ar.alloc(128, HID, F32, "lkvm")
    pNcol, pNcol_b = ar.alloc(128, 1, F32, "pNcol")
    h128c, h128c_b = ar.alloc(128, 1, F32, "h128c")
    nidxu, nidxu_b = ar.alloc(128, 8, U32, "nidxu")

    P.op("pe", lambda e: e.matmul(ps[0][:, 0:1], s_ones1f[0:1, 0:128], selr[0:1, 12:13],
                                  start=True, stop=True),
         reads=[selr_b, s_ones1f_b], writes=[psb[0]], tag="pN")
    P.op("dve", lambda e: e.tensor_copy(pNcol[:, 0:1], ps[0][:, 0:1]),
         reads=[psb[0]], writes=[pNcol_b], tag="pNe")

    def mk_h128(e):
        ch = CH["dve"]
        ch(e, e.tensor_scalar(h128c[:, 0:1], hp8k[:, 0:1], 1.0 / 64.0, None, ALU.mult))
        return e.tensor_tensor(h128c[:, 0:1], h128c[:, 0:1], iotacol[:, 0:1], ALU.add)
    P.op("dve", mk_h128, reads=[hp8k_b, iotacol_b, h128c_b], writes=[h128c_b], tag="h128")

    def mk_nidx(e):
        ch = CH["dve"]
        for f in range(8):
            ch(e, e.tensor_tensor(idxc2[:, f:f + 1], pNcol[:, 0:1], iotacol[:, 0:1], ALU.add))
            ch(e, e.tensor_scalar(idxc2[:, f:f + 1], idxc2[:, f:f + 1], float(f * 128),
                                  None, ALU.add))
        return e.tensor_copy(nidxu[:, 0:8], idxc2[:, 0:8])
    idxc2, idxc2_b = ar.alloc(128, 8, F32, "idxc2")
    P.op("dve", mk_nidx, reads=[pNcol_b, iotacol_b, idxc2_b, nidxu_b],
         writes=[idxc2_b, nidxu_b], tag="nidx")

    ar.mark()
    whv, whv_b = ar.alloc(128, 8 * HID, BF16, "whv")
    whg, whg_b = ar.alloc(128, 8 * HID, BF16, "whg")
    wqkb, wqkb_b = ar.alloc(128, 8 * QKD, BF16, "wqkb")
    bhv16, bhv16_b = ar.alloc(1, HID, BF16, "bhv16")
    s1row, s1row_b = ar.alloc(1, 128, BF16, "s1row")
    vtm, vtm_b = ar.alloc(128, NSUB * HID, BF16, "vtm")
    qhead, qhead_b = ar.alloc(128, 5 * ACH, BF16, "qhead")
    attnT, attnT_b = ar.alloc(128, 2 * GSZ, BF16, "attnT")
    lkN, lkN_b = ar.alloc(128, 4 * 128, BF16, "lkN")
    gsc, gsc_b = ar.alloc(128, ACH, BF16, "gsc")

    for ct in range(8):
        P.dma("pool", lambda e, ct=ct: e.dma_start(
            out=whv[:, ct * HID:(ct + 1) * HID], in_=w_h[ct * 128:(ct + 1) * 128, 0:HID]),
            reads=[bI["w_h"]], writes=[whv_b], tag=f"ldwv{ct}")
        P.dma("pool", lambda e, ct=ct: e.dma_start(
            out=whg[:, ct * HID:(ct + 1) * HID], in_=w_h[ct * 128:(ct + 1) * 128, HID:2 * HID]),
            reads=[bI["w_h"]], writes=[whg_b], tag=f"ldwg{ct}")
        P.dma("pool", lambda e, ct=ct: e.dma_start(
            out=wqkb[:, ct * QKD:(ct + 1) * QKD], in_=w_qk[ct * 128:(ct + 1) * 128, :]),
            reads=[bI["w_qk"]], writes=[wqkb_b], tag=f"ldwqk{ct}")
    P.dma("pool", lambda e: e.dma_start(out=bhv16[0:1, :], in_=b_h[0:1, 0:HID]),
          reads=[bI["b_h"]], writes=[bhv16_b], tag="ldbhv")
    P.dma("sp", lambda e: e.dma_start(out=s1row[0:1, :], in_=c_onesr_b16[:, :]),
          reads=[bI["consts"]], writes=[s1row_b], tag="ld1b")

    for c in range(NACH):
        r0 = c * ACH
        afx = f"_a{c}"
        for f in range(8):
            P.dma("pool", lambda e, f=f, r0=r0: e.indirect_dma_start(
                normch[:, f * ACH:(f + 1) * ACH], None,
                pr_norm[:, :], bass.IndirectOffsetOnAxis(nidxu[:, f:f + 1], 0),
                element_offset=r0, bounds_check=rbc_nrm, oob_is_err=False),
                reads=[b_norm, nidxu_b, btok[0]], writes=[normch_b], tag=f"gnm{f}" + afx)

        for q in range(NSUB):
            for oc in range(4):
                for ct in range(8):
                    P.op("pe", lambda e, q=q, oc=oc, ct=ct: e.matmul(
                        ps[0][:, 0:512], normch[:, ct * ACH + q * 128:ct * ACH + q * 128 + 128],
                        whv[:, ct * HID + oc * 512:ct * HID + (oc + 1) * 512],
                        start=(ct == 0), stop=False),
                        reads=[normch_b, whv_b], writes=[psb[0]], tag=f"v{q}{oc}{ct}" + afx)
                P.op("pe", lambda e, oc=oc: e.matmul(
                    ps[0][:, 0:512], s1row[0:1, 0:128], bhv16[0:1, oc * 512:(oc + 1) * 512],
                    start=False, stop=True),
                    reads=[bhv16_b, s1row_b], writes=[psb[0]], tag=f"vb{q}{oc}" + afx)
                P.op("act", lambda e, q=q, oc=oc: e.activation(
                    vtm[:, q * HID + oc * 512:q * HID + (oc + 1) * 512], ps[0][:, 0:512],
                    AF.Silu),
                    reads=[psb[0]], writes=[vtm_b], tag=f"ve{q}{oc}" + afx)

        for f in range(16):
            for ct in range(8):
                P.op("pe", lambda e, f=f, ct=ct: e.matmul(
                    ps[1][:, 0:512], whg[:, ct * HID + f * 128:ct * HID + (f + 1) * 128],
                    normch[:, ct * ACH:ct * ACH + ACH], start=(ct == 0), stop=(ct == 7)),
                    reads=[normch_b, whg_b], writes=[psb[1]], tag=f"g{f}{ct}" + afx)
            P.op("act", lambda e, f=f: e.activation(
                gsc[:, 0:ACH], ps[1][:, 0:512], AF.Silu,
                bias=cols[:, C_BHG + f:C_BHG + f + 1]),
                reads=[psb[1], cols_b], writes=[gsc_b], tag=f"ge{f}" + afx)
            P.dma("sp", lambda e, f=f, r0=r0: e.dma_start(
                out=pr_gate[f * 128:(f + 1) * 128, r0:r0 + ACH], in_=gsc[:, 0:ACH]),
                reads=[gsc_b], writes=[b_gate], tag=f"gs{f}" + afx)

        for ct in range(8):
            P.op("pe", lambda e, ct=ct: e.matmul(
                ps[2][:, 0:512], wqkb[:, ct * QKD:(ct + 1) * QKD],
                normch[:, ct * ACH:ct * ACH + ACH], start=(ct == 0), stop=(ct == 7)),
                reads=[normch_b, wqkb_b], writes=[psb[2]], tag=f"qk{ct}" + afx)
        P.op("act", lambda e: e.activation(qhead[:, 0:ACH], ps[2][:, 0:512], AF.Silu,
                                           bias=cols[:, C_BQK:C_BQK + 1]),
             reads=[psb[2], cols_b], writes=[qhead_b], tag="qke" + afx)
        for hh in range(4):
            P.op("dve", lambda e, hh=hh: e.tensor_scalar(
                qhead[:, (1 + hh) * ACH:(2 + hh) * ACH], qhead[:, 0:ACH],
                cols[:, C_G0 + hh:C_G0 + hh + 1], cols[:, C_BT0 + hh:C_BT0 + hh + 1],
                ALU.mult, ALU.add),
                reads=[qhead_b, cols_b], writes=[qhead_b], tag=f"hd{hh}" + afx)

        for g in range(2):
            gt0 = g * 256
            for jt in range(2):
                P.op("pe", lambda e, g=g, jt=jt: e.matmul(
                    ps[3][:, 0:256], s_idb[:, :], biasT[:, jt * 256:(jt + 1) * 256],
                    start=True, stop=False),
                    reads=[biasT_b, s_idb_b], writes=[psb[3]], tag=f"sb{g}{jt}" + afx)
                P.op("pe", lambda e, g=g, jt=jt: e.matmul(
                    ps[3][:, 0:256], qhead[:, 3 * ACH + gt0 + jt * 128:3 * ACH + gt0 + jt * 128 + 128],
                    qhead[:, 1 * ACH + gt0:1 * ACH + gt0 + 256], start=False, stop=True),
                    reads=[qhead_b], writes=[psb[3]], tag=f"sm{g}{jt}" + afx)

                def relusq(e, jt=jt):
                    ch = CH["dve"]
                    ch(e, e.tensor_scalar(gsc[:, 0:256], ps[3][:, 0:256], 0.0, None, ALU.max))
                    return e.tensor_tensor(attnT[:, jt * 256:(jt + 1) * 256],
                                           gsc[:, 0:256], gsc[:, 0:256], ALU.mult)
                P.op("dve", relusq, reads=[psb[3], gsc_b, attnT_b],
                     writes=[gsc_b, attnT_b], tag=f"rq{g}{jt}" + afx)
            for f in range(16):
                for jt in range(2):
                    P.op("pe", lambda e, f=f, g=g, jt=jt: e.matmul(
                        ps[3][:, 256:512],
                        vtm[:, (2 * g + jt) * HID + f * 128:(2 * g + jt) * HID + (f + 1) * 128],
                        attnT[:, jt * 256:(jt + 1) * 256], start=(jt == 0), stop=(jt == 1)),
                        reads=[vtm_b, attnT_b], writes=[psb[3]], tag=f"qo{f}{g}{jt}" + afx)
                P.op("dve", lambda e, f=f: e.tensor_copy(gsc[:, 0:256], ps[3][:, 256:512]),
                     reads=[psb[3]], writes=[gsc_b], tag=f"qoe{f}{g}" + afx)
                P.dma("sp", lambda e, f=f, g=g, r0=r0: e.dma_start(
                    out=pr_quad[f * 128:(f + 1) * 128, r0 + g * 256:r0 + (g + 1) * 256],
                    in_=gsc[:, 0:256]),
                    reads=[gsc_b], writes=[b_quad], tag=f"qs{f}{g}" + afx)

        P.op("dve", lambda e, r0=r0: e.tensor_copy(lqAll[:, r0:r0 + ACH],
                                                   qhead[:, 2 * ACH:3 * ACH]),
             reads=[qhead_b], writes=[lqAll_b], tag="lq" + afx)
        for q in range(NSUB):
            P.op("pe", lambda e, q=q: e.transpose(
                pst_b[:, 0:128], qhead[:, 4 * ACH + q * 128:4 * ACH + (q + 1) * 128],
                s_idb[:, :]),
                reads=[qhead_b, s_idb_b], writes=[psb[3]], tag=f"lkT{q}" + afx)
            P.op("dve", lambda e, q=q: e.tensor_copy(lkN[:, q * 128:(q + 1) * 128],
                                                     pst_b[:, 0:128]),
                 reads=[psb[3]], writes=[lkN_b], tag=f"lke{q}" + afx)
        for q in range(NSUB):
            for oc in range(4):
                tgt = pd[0] if oc < 2 else pd[1]
                tb = pdb[0] if oc < 2 else pdb[1]
                off = (oc % 2) * 512
                P.op("pe", lambda e, q=q, oc=oc, tgt=tgt, off=off, c=c: e.matmul(
                    tgt[:, off:off + 512], lkN[:, q * 128:(q + 1) * 128],
                    vtm[:, q * HID + oc * 512:q * HID + (oc + 1) * 512],
                    start=(c == 0 and q == 0), stop=(c == NACH - 1 and q == NSUB - 1)),
                    reads=[lkN_b, vtm_b], writes=[tb], tag=f"lkv{q}{oc}" + afx)

    for half in range(2):
        P.op("dve", lambda e, half=half: e.tensor_copy(
            lkvm[:, half * 1024:(half + 1) * 1024], pd[half][:, 0:1024]),
            reads=[pdb[half]], writes=[lkvm_b], tag=f"lkve{half}")
    lkvi, lkvi_b = ar.alloc(128, 1, U32, "lkvi")
    P.op("dve", lambda e: e.tensor_copy(lkvi[:, 0:1], h128c[:, 0:1]),
         reads=[h128c_b], writes=[lkvi_b], tag="lkvi")
    P.dma("pool", lambda e: e.indirect_dma_start(
        sh_linkv.rearrange("a b c -> (a b) c")[:, :],
        bass.IndirectOffsetOnAxis(lkvi[:, 0:1], 0), lkvm[:, :], None,
        bounds_check=rbc_lkv, oob_is_err=False),
        reads=[lkvm_b, lkvi_b], writes=[b_shlinkv], tag="wlkv")

    if debug:
        P.dma("sp", lambda e: e.dma_start(out=dbg["linkv"][:, :], in_=lkvm[:, :]),
              reads=[lkvm_b], writes=[b_dbg["linkv"]], tag="dlkv")

    barrier(2, [b_shS, b_shlinkv], tag="B2")
    if PH <= 3:
        return finish()

    ar.release()
    ar.mark()
    wob, wob_b = ar.alloc(128, 16 * D, BF16, "wob")
    lkvf, lkvf_b = ar.alloc(128, HID, F32, "lkvf")
    lkvb2, lkvb2_b = ar.alloc(128, HID, BF16, "lkvb2")
    qre, qre_b = ar.alloc(128, 16 * 512, BF16, "qre")
    gre, gre_b = ar.alloc(128, 16 * 512, BF16, "gre")
    Gch, Gch_b = ar.alloc(128, 16 * 512, BF16, "Gch")
    Aev, Aev_b = ar.alloc(128, D, F32, "Aev")

    for ct in range(16):
        P.dma("pool", lambda e, ct=ct: e.dma_start(
            out=wob[:, ct * D:(ct + 1) * D], in_=w_o[ct * 128:(ct + 1) * 128, :]),
            reads=[bI["w_o"]], writes=[wob_b], tag=f"ldwo{ct}")
    P.dma("sp", lambda e: e.dma_start(out=lkvf[:, :],
                                      in_=sh_linkv[0, :, :]),
          reads=[b_shlinkv, btok[1]], writes=[lkvf_b], tag="lkv0")

    lkvp, lkvp_b = ar.alloc(128, HID, F32, "lkvp")
    P.dma("sp", lambda e: e.dma_start(out=lkvp[:, :], in_=sh_linkv[1, :, :]),
          reads=[b_shlinkv, btok[1]], writes=[lkvp_b], tag="lkv1")

    def lkvadd(e):
        ch = CH["dve"]
        ch(e, e.tensor_tensor(lkvf[:, :], lkvf[:, :], lkvp[:, :], ALU.add))
        return e.tensor_copy(lkvb2[:, :], lkvf[:, :])
    P.op("dve", lkvadd, reads=[lkvf_b, lkvp_b, lkvb2_b], writes=[lkvf_b, lkvb2_b],
         tag="lkvadd")

    for c in range(NACH):
        r0 = c * ACH
        afx = f"_p5{c}"
        for f in range(16):
            P.dma("sp", lambda e, f=f, r0=r0: e.dma_start(
                out=qre[:, f * 512:(f + 1) * 512],
                in_=pr_quad[f * 128:(f + 1) * 128, r0:r0 + ACH]),
                reads=[b_quad], writes=[qre_b], tag=f"qr{f}" + afx)
            P.dma("sp", lambda e, f=f, r0=r0: e.dma_start(
                out=gre[:, f * 512:(f + 1) * 512],
                in_=pr_gate[f * 128:(f + 1) * 128, r0:r0 + ACH]),
                reads=[b_gate], writes=[gre_b], tag=f"gr{f}" + afx)
        for f in range(16):
            P.op("pe", lambda e, f=f, r0=r0: e.matmul(
                ps[0][:, 0:512], lkvb2[:, f * 128:(f + 1) * 128], lqAll[:, r0:r0 + ACH],
                start=True, stop=True),
                reads=[lkvb2_b, lqAll_b], writes=[psb[0]], tag=f"lo{f}" + afx)

            def mkG(e, f=f):
                ch = CH["dve"]
                ch(e, e.tensor_tensor(Gch[:, f * 512:(f + 1) * 512], ps[0][:, 0:512],
                                      qre[:, f * 512:(f + 1) * 512], ALU.add))
                return e.tensor_tensor(Gch[:, f * 512:(f + 1) * 512],
                                       Gch[:, f * 512:(f + 1) * 512],
                                       gre[:, f * 512:(f + 1) * 512], ALU.mult)
            P.op("dve", mkG, reads=[psb[0], qre_b, gre_b, Gch_b], writes=[Gch_b],
                 tag=f"G{f}" + afx)
        for q in range(NSUB):
            for half in range(2):
                for ct in range(16):
                    P.op("pe", lambda e, q=q, half=half, ct=ct: e.matmul(
                        pd[0][:, half * 512:(half + 1) * 512],
                        Gch[:, ct * 512 + q * 128:ct * 512 + q * 128 + 128],
                        wob[:, ct * D + half * 512:ct * D + (half + 1) * 512],
                        start=(ct == 0), stop=(ct == 15)),
                        reads=[Gch_b, wob_b], writes=[pdb[0]], tag=f"A{q}{half}{ct}" + afx)
            P.op("dve", lambda e, q=q: e.tensor_copy(Aev[:, :], pd[0][:, 0:D]),
                 reads=[pdb[0]], writes=[Aev_b], tag=f"Ae{q}" + afx)
            P.dma("sp", lambda e, q=q, r0=r0: e.dma_start(
                out=pr_outp[r0 + q * 128:r0 + (q + 1) * 128, :], in_=Aev[:, :]),
                reads=[Aev_b], writes=[b_outp], tag=f"As{q}" + afx)

    if PH <= 4:
        return finish()
    ar.release()
    ar.release()
    ar.mark()
    w1sb, w1sb_b = ar.alloc(128, 8 * FFNH, BF16, "w1sb")
    w2sb2 = [ar.alloc(128, 32 * 512, BF16, f"w2sb{i}") for i in range(2)]
    actb, actb_b = ar.alloc(128, 32 * 512, BF16, "actb")
    x2g2 = [ar.alloc(128, D, BF16, f"x2g{i}") for i in range(2)]
    x2gT, x2gT_b = ar.alloc(128, 8 * 512, BF16, "x2gT")
    ysb2 = [ar.alloc(128, 512, F32, f"ysb{i}") for i in range(2)]
    Sg, Sg_b = ar.alloc(128, 16, F32, "Sg")
    h16c, h16c_b = ar.alloc(128, 1, F32, "h16c")
    eidx, eidx_b = ar.alloc(128, 8, F32, "eidx")
    eidxu, eidxu_b = ar.alloc(128, 8, U32, "eidxu")

    P.op("pe", lambda e: e.matmul(ps[0][:, 0:1], s_ones1f[0:1, 0:128], selr[0:1, 14:15],
                                  start=True, stop=True),
         reads=[selr_b, s_ones1f_b], writes=[psb[0]], tag="h16")
    P.op("dve", lambda e: e.tensor_tensor(h16c[:, 0:1], ps[0][:, 0:1], iotacol[:, 0:1],
                                          ALU.add),
         reads=[psb[0], iotacol_b], writes=[h16c_b], tag="h16e")

    for j in range(4):
        jfx = f"_e{j}"
        for ct in range(8):
            P.dma("pool", lambda e, j=j, ct=ct: e.dma_start(
                out=w1sb[:, ct * FFNH:(ct + 1) * FFNH],
                in_=moe_w1[j, ct * 128:(ct + 1) * 128, :]),
                reads=[bI["moe_w1"]], writes=[w1sb_b], tag=f"lw1{ct}" + jfx)
        for grp in range(NGROUP):
            gfx = jfx + f"g{grp}"
            _m = os.environ.get("KP4MODE", "real")
            if _m == "real":
                _gth = grp + 1
            elif _m == "mix2":
                _gth = 0 if grp == 0 else 9999
            elif _m == "mix1":
                _gth = 0 if j == 0 else 9999
            elif _m == "one_skip":
                _gth = 9999 if (j == 0 and grp == 1) else 0
            elif _m == "one_take":
                _gth = 0 if (j == 0 and grp == 0) else 9999
            elif _m == "jreal":
                _gth = (grp + 1) if j == 0 else 0
            elif _m == "realskip1":
                _gth = 9999 if (j == 0 and grp == 7) else 0
            else:
                _gth = {"open": 0, "else": 9999}.get(_m, 0)
            with P.gate(ngi[0:1, j:j + 1], ngi_b, _gth, key=("ng", j)):
                def mkeidx(e, j=j, grp=grp):
                    for tt in range(4):
                        inst = e.tensor_scalar(eidx[:, tt:tt + 1], h16c[:, 0:1],
                                               float(j * ISL + grp * EG + tt * 128),
                                               None, ALU.add)
                    return inst
                P.op("dve", mkeidx, reads=[h16c_b, eidx_b],
                     writes=[eidx_b], tag="eix" + gfx)
                P.op("dve", lambda e: e.tensor_copy(eidxu[:, 0:4], eidx[:, 0:4]),
                     reads=[eidx_b, eidxu_b], writes=[eidxu_b], tag="eixc" + gfx)
                for tt in range(4):
                    P.dma("pool", lambda e, tt=tt: e.indirect_dma_start(
                        Sg[:, tt:tt + 1], None, sh_S[:, :],
                        bass.IndirectOffsetOnAxis(eidxu[:, tt:tt + 1], 0),
                        bounds_check=rbc_S, oob_is_err=False),
                        reads=[b_shS, eidxu_b, btok[1]], writes=[Sg_b], tag=f"gS{tt}" + gfx)
                def mktid1(e):
                    for tt in range(4):
                        e.tensor_scalar(Sg[:, 4 + tt:5 + tt], Sg[:, tt:tt + 1], 0.0,
                                        None, ALU.is_equal)
                    for tt in range(4):
                        inst = e.tensor_scalar(Sg[:, 8 + tt:9 + tt], Sg[:, tt:tt + 1],
                                               -1.0, None, ALU.add)
                    return inst
                P.op("dve", mktid1, reads=[Sg_b], writes=[Sg_b], tag="tid1" + gfx)

                def mktid2(e):
                    for tt in range(4):
                        inst = e.scalar_tensor_tensor(
                            Sg[:, 12 + tt:13 + tt], Sg[:, 4 + tt:5 + tt], HUGE,
                            Sg[:, 8 + tt:9 + tt], ALU.mult, ALU.add)
                    return inst
                P.op("dve", mktid2, reads=[Sg_b], writes=[Sg_b], tag="tid2" + gfx)
                P.op("dve", lambda e: e.tensor_copy(eidxu[:, 4:8], Sg[:, 12:16]),
                     reads=[Sg_b, eidxu_b], writes=[eidxu_b], tag="tid3" + gfx)
                for tt in range(4):
                    x2g, x2g_b = x2g2[tt % 2]
                    P.op("dve", lambda e, x2g=x2g: e.memset(x2g[:, :], 0.0),
                         writes=[x2g_b], tag=f"z{tt}" + gfx)
                    P.dma("pool", lambda e, tt=tt, x2g=x2g: e.indirect_dma_start(
                        x2g[:, :], None, sh_x2[:, :],
                        bass.IndirectOffsetOnAxis(eidxu[:, 4 + tt:5 + tt], 0),
                        bounds_check=rbc_x2, oob_is_err=False),
                        reads=[b_shx2, eidxu_b, x2g_b, btok[1]], writes=[x2g_b],
                        tag=f"gx{tt}" + gfx)
                    for f in range(8):
                        tb = pst_b if f % 2 == 0 else pdt_b
                        tbb = psb[3] if f % 2 == 0 else pdb[1]
                        P.op("pe", lambda e, f=f, tb=tb, x2g=x2g: e.transpose(
                            tb[:, 0:128], x2g[:, f * 128:(f + 1) * 128], s_idb[:, :]),
                            reads=[x2g_b, s_idb_b], writes=[tbb], tag=f"tg{tt}{f}" + gfx)
                        P.op("act", lambda e, f=f, tt=tt, tb=tb: e.activation(
                            x2gT[:, f * 512 + tt * 128:f * 512 + tt * 128 + 128],
                            tb[:, 0:128], AF.Identity,
                            bias=cols[:, C_LN2B + f:C_LN2B + f + 1],
                            scale=cols[:, C_LN2W + f:C_LN2W + f + 1]),
                            reads=[tbb, cols_b], writes=[x2gT_b], tag=f"tge{tt}{f}" + gfx)
                for f4 in range(32):
                    apk = f4 % 2 * 2
                    for ct in range(8):
                        P.op("pe", lambda e, f4=f4, ct=ct, apk=apk: e.matmul(
                            ps[apk][:, 0:512],
                            w1sb[:, ct * FFNH + f4 * 128:ct * FFNH + (f4 + 1) * 128],
                            x2gT[:, ct * 512:(ct + 1) * 512],
                            start=(ct == 0), stop=(ct == 7)),
                            reads=[w1sb_b, x2gT_b], writes=[psb[apk]], tag=f"a{f4}{ct}" + gfx)
                    P.op("act", lambda e, f4=f4, apk=apk: e.activation(
                        actb[:, f4 * 512:(f4 + 1) * 512], ps[apk][:, 0:512], AF.Gelu),
                        reads=[psb[apk]], writes=[actb_b], tag=f"ag{f4}" + gfx)
                for half in range(2):
                    w2s, w2s_b = w2sb2[half]
                    for ct in range(32):
                        P.dma("pool", lambda e, j=j, ct=ct, half=half, w2s=w2s: e.dma_start(
                            out=w2s[:, ct * 512:(ct + 1) * 512],
                            in_=moe_w2[j, ct * 128:(ct + 1) * 128,
                                       half * 512:(half + 1) * 512]),
                            reads=[bI["moe_w2"]], writes=[w2s_b], tag=f"lw2{ct}{half}" + gfx)
                    for tt in range(4):
                        ypb = ps[1][:, 0:512] if tt % 2 == 0 else pd[0][:, 0:512]
                        ypbb = psb[1] if tt % 2 == 0 else pdb[0]
                        for ct in range(32):
                            P.op("pe", lambda e, tt=tt, ct=ct, ypb=ypb, w2s=w2s: e.matmul(
                                ypb,
                                actb[:, ct * 512 + tt * 128:ct * 512 + tt * 128 + 128],
                                w2s[:, ct * 512:(ct + 1) * 512],
                                start=(ct == 0), stop=(ct == 31)),
                                reads=[actb_b, w2s_b], writes=[ypbb], tag=f"y{tt}{ct}{half}" + gfx)
                        ysb, ysb_b = ysb2[tt % 2]
                        P.op("dve", lambda e, ypb=ypb, ysb=ysb: e.tensor_copy(
                            ysb[:, 0:512], ypb),
                            reads=[ypbb], writes=[ysb_b], tag=f"ye{tt}{half}" + gfx)

                        P.op("dve", lambda e, j=j, grp=grp, tt=tt: e.tensor_scalar(
                            eidx[:, 4:5], h16c[:, 0:1],
                            float(j * ISL + grp * EG + tt * 128), None, ALU.add),
                            reads=[h16c_b, eidx_b], writes=[eidx_b],
                            tag=f"ywi{tt}{half}" + gfx)
                        P.op("dve", lambda e: e.tensor_copy(eidxu[:, 4:5], eidx[:, 4:5]),
                             reads=[eidx_b, eidxu_b], writes=[eidxu_b],
                             tag=f"ywc{tt}{half}" + gfx)
                        P.dma("pool", lambda e, half=half, ysb=ysb: e.indirect_dma_start(
                            (sh_y1 if half == 0 else sh_y2)[:, :],
                            bass.IndirectOffsetOnAxis(eidxu[:, 4:5], 0),
                            ysb[:, 0:512], None, bounds_check=rbc_y, oob_is_err=False),
                            reads=[ysb_b, eidxu_b],
                            writes=[b_shy1 if half == 0 else b_shy2],
                            tag=f"yw{tt}{half}" + gfx)

    barrier(3, [b_shy1, b_shy2], tag="B3")

    if debug:
        ydmp, ydmp_b = ysb2[0]
        P.op("dve", lambda e: e.tensor_copy(ydmp[0:1, 0:8], ngi[0:1, 0:8]),
             reads=[ngi_b], writes=[ydmp_b], tag="dngi")
        P.dma("sp", lambda e: e.dma_start(out=dbg["ngi"][0:1, 0:8], in_=ydmp[0:1, 0:8]),
              reads=[ydmp_b], writes=[b_dbg["ngi"]], tag="dngis")
        for blk in range(4 * ISL // 128):
            for nm, sh in [("y1", sh_y1), ("y2", sh_y2)]:
                P.dma("sp", lambda e, blk=blk, sh=sh: e.dma_start(
                    out=ydmp[:, :], in_=sh[blk * 128:(blk + 1) * 128, :]),
                    reads=[b_shy1, b_shy2, b_dbg["ngi"]], writes=[ydmp_b],
                    tag=f"yd{nm}{blk}")
                P.dma("sp", lambda e, blk=blk, nm=nm: e.dma_start(
                    out=dbg[nm][blk * 128:(blk + 1) * 128, :], in_=ydmp[:, :]),
                    reads=[ydmp_b], writes=[b_dbg[nm]], tag=f"ye{nm}{blk}")

    if PH <= 5:
        return finish()

    ar.release()
    ar.mark()
    oach, oach_b = ar.alloc(128, D, F32, "oach")
    xrel, xrel_b = ar.alloc(128, D, F32, "xrel")
    hrel, hrel_b = ar.alloc(128, D, F32, "hrel")
    nrel, nrel_b = ar.alloc(128, D, F32, "nrel")
    arel, arel_b = ar.alloc(128, D, F32, "arel")
    y1t, y1t_b = ar.alloc(128, D, F32, "y1t")
    y2t, y2t_b = ar.alloc(128, D, F32, "y2t")
    mte, mte_b = ar.alloc(128, D, F32, "mte")
    gix, gix_b = ar.alloc(128, 2, U32, "gix")
    bobc, bobc_b = ar.alloc(128, D, F32, "bobc")
    ld_row(b_o, bI["b_o"], D, "bo")
    bcast_row(bobc, bobc_b, rowbuf[0:1, 0:D], rowbuf_b.sub(0, D * 4), D, "bo")

    for t in range(16):
        r0 = t * 128
        tfx = f"_o{t}"
        P.dma("sp", lambda e, r0=r0: e.dma_start(out=xrel[:, :], in_=pr_x[r0:r0 + 128, :]),
              reads=[b_x, btok[0]], writes=[xrel_b], tag="ox" + tfx)
        P.dma("sp", lambda e, r0=r0: e.dma_start(out=hrel[:, :], in_=pr_haa[r0:r0 + 128, :]),
              reads=[b_haa, btok[0]], writes=[hrel_b], tag="oh" + tfx)
        P.dma("sp", lambda e, r0=r0: e.dma_start(out=nrel[:, :], in_=pr_xn[r0:r0 + 128, :]),
              reads=[b_xn, btok[0]], writes=[nrel_b], tag="on" + tfx)
        P.dma("sp", lambda e, r0=r0: e.dma_start(out=arel[:, :], in_=pr_outp[r0:r0 + 128, :]),
              reads=[b_outp], writes=[arel_b], tag="oa" + tfx)
        P.op("dve", lambda e, t=t: e.tensor_copy(gix[:, 0:2], gidx_all[:, 2 * t:2 * t + 2]),
             reads=[gidx_all_b, gix_b], writes=[gix_b], tag="og" + tfx)
        for kk, (yt, yt_b) in enumerate([(y1t, y1t_b), (y2t, y2t_b)]):
            P.op("dve", lambda e, yt=yt: e.memset(yt[:, :], 0.0), writes=[yt_b],
                 tag=f"yz{kk}" + tfx)
            for half in range(2):
                P.dma("pool", lambda e, kk=kk, yt=yt, half=half: e.indirect_dma_start(
                    yt[:, half * 512:(half + 1) * 512], None,
                    (sh_y1 if half == 0 else sh_y2)[:, :],
                    bass.IndirectOffsetOnAxis(gix[:, kk:kk + 1], 0),
                    bounds_check=rbc_y, oob_is_err=False),
                    reads=[b_shy1 if half == 0 else b_shy2, gix_b, yt_b, btok[2]],
                    writes=[yt_b], tag=f"yg{kk}{half}" + tfx)

        def comb(e, t=t):
            ch = CH["dve"]
            ch(e, e.tensor_scalar(mte[:, :], y1t[:, :], pk_all[:, 2 * t:2 * t + 1],
                                  None, ALU.mult))
            ch(e, e.scalar_tensor_tensor(mte[:, :], y2t[:, :],
                                         pk_all[:, 2 * t + 1:2 * t + 2], mte[:, :],
                                         ALU.mult, ALU.add))
            ch(e, e.tensor_tensor(oach[:, :], xrel[:, :], hrel[:, :], ALU.add))
            ch(e, e.tensor_tensor(oach[:, :], oach[:, :], nrel[:, :], ALU.add))
            ch(e, e.tensor_tensor(oach[:, :], oach[:, :], arel[:, :], ALU.add))
            ch(e, e.tensor_tensor(oach[:, :], oach[:, :], bobc[:, :], ALU.add))
            return e.tensor_tensor(oach[:, :], oach[:, :], mte[:, :], ALU.add)
        P.op("dve", comb,
             reads=[y1t_b, y2t_b, pk_all_b, xrel_b, hrel_b, nrel_b, arel_b, bobc_b,
                    mte_b, oach_b],
             writes=[mte_b, oach_b], tag="oc" + tfx)
        P.dma("sp", lambda e, r0=r0: e.dma_start(out=out[r0:r0 + 128, :], in_=oach[:, :]),
              reads=[oach_b], writes=[b_out.sub(r0 * D * 4, (r0 + 128) * D * 4)],
              tag="os" + tfx)

    return finish(mini=False)


def _in_maps(inputs, rng_nonce):
    x = np.ascontiguousarray(np.asarray(inputs["L0_attn_x"], dtype=np.float32))
    qkv_wT = np.ascontiguousarray(np.asarray(inputs["L0_ffn_qkv_weight"], np.float32).T)
    proj_wT = np.ascontiguousarray(np.asarray(inputs["L0_ffn_proj_weight"], np.float32).T)
    w1 = np.ascontiguousarray(np.asarray(inputs["L0_ffn_moe_w1"], np.float32))
    w2 = np.ascontiguousarray(np.asarray(inputs["L0_ffn_moe_w2"], np.float32))

    def row(name, w):
        return np.asarray(inputs[name], np.float32).reshape(1, w)

    maps = []
    for core in range(8):
        p, h = core // 2, core % 2
        selv = np.zeros((1, 16), np.float32)
        selv[0, p] = 1.0
        selv[0, 4:4 + p] = 1.0
        selv[0, 8] = h
        selv[0, 9] = p
        selv[0, 10] = h * (B * NH) + p * NH
        selv[0, 11] = h * 4
        selv[0, 12] = p * HID
        selv[0, 14] = h * 4 * ISL
        m = {
            "xs": np.ascontiguousarray(x[:, h * NH:(h + 1) * NH, :]),
            "sel": selv,
            "nonce": rng_nonce,
            "norm_g": row("L0_attn_norm_g", D),
            "norm_b": row("L0_attn_norm_b", D),
            "w_h": np.ascontiguousarray(np.asarray(inputs["L0_attn_W_h"], np.float32)),
            "b_h": row("L0_attn_b_h", 2 * HID),
            "w_qk": np.ascontiguousarray(np.asarray(inputs["L0_attn_W_qk"], np.float32)),
            "b_qk": row("L0_attn_b_qk", QKD),
            "qk_gamma": np.ascontiguousarray(np.asarray(inputs["L0_attn_qk_gamma"], np.float32)),
            "qk_beta": np.ascontiguousarray(np.asarray(inputs["L0_attn_qk_beta"], np.float32)),
            "rel_emb": row("L0_attn_rel_emb", 32),
            "w_o": np.ascontiguousarray(np.asarray(inputs["L0_attn_W_o"], np.float32)),
            "b_o": row("L0_attn_b_o", D),
            "ln1_w": row("L0_ffn_ln1_weight", D),
            "ln1_b": row("L0_ffn_ln1_bias", D),
            "ln2_w": row("L0_ffn_ln2_weight", D),
            "ln2_b": row("L0_ffn_ln2_bias", D),
            "qkv_wT": qkv_wT,
            "proj_wT": proj_wT,
            "router_w": np.ascontiguousarray(np.asarray(inputs["L0_ffn_router_weight"], np.float32)),
            "moe_w1": np.ascontiguousarray(w1[4 * h:4 * h + 4]),
            "moe_w2": np.ascontiguousarray(w2[4 * h:4 * h + 4]),
        }
        maps.append(m)
    return maps


_CACHE = {}


def _build(debug=0):
    key = ("nc", debug)
    if key not in _CACHE:
        _CACHE[key] = build_program(debug=debug)
    return _CACHE[key]


def run_device(inputs, debug=0):
    nc, dbg = _build(debug)
    nonce_v = np.random.default_rng().standard_normal((1, 4)).astype(np.float32)
    maps = _in_maps(inputs, nonce_v)
    res = run_bass_kernel_spmd(nc, maps, core_ids=list(range(8)))
    return res


def kernel(**inputs):
    res = run_device(inputs, debug=0)
    outf = np.zeros((B, N, D), np.float32)
    for core in range(8):
        p, h = core // 2, core % 2
        outf[p, h * NH:(h + 1) * NH, :] = res.results[core]["out"]
    return outf


# version 04
# speedup vs baseline: 2.7962x; 2.7962x over previous
import math
import os

import numpy as np
import ml_dtypes

import concourse.bass as bass
import concourse.mybir as mybir
from concourse.bass_utils import run_bass_kernel_spmd

F32 = mybir.dt.float32
BF16 = mybir.dt.bfloat16
I32 = mybir.dt.int32
U32 = mybir.dt.uint32
U8 = mybir.dt.uint8
ALU = mybir.AluOpType
AF = mybir.ActivationFunctionType
AX = mybir.AxisListType
BF = ml_dtypes.bfloat16

B = 4
N = 4096
D = 1024
HID = 2048
QKD = 128
GSZ = 256
NEXP = 8
FFNH = 4096
CAP = 5120
NH = 2048
CHK = 512
NCHK = NH // CHK
NSUB = CHK // 128
EPS = 1e-5
ISL = 4096
EG = 512
ZROW = NEXP * ISL
NGROUP = ISL // EG
HUGE = 1.0e9

ENGINES = ("pe", "act", "dve", "pool", "sp")


class Buf:
    __slots__ = ("space", "lo", "hi", "name")

    def __init__(self, space, lo, hi, name=""):
        self.space, self.lo, self.hi, self.name = space, lo, hi, name

    def sub(self, lo_b, hi_b):
        return Buf(self.space, self.lo + lo_b, self.lo + hi_b, self.name)


class _Seg:
    __slots__ = ("lo", "hi", "writer", "readers")

    def __init__(self, lo, hi, writer, readers):
        self.lo, self.hi, self.writer, self.readers = lo, hi, writer, readers


class Op:
    __slots__ = ("idx", "engine", "emit", "reads", "writes", "is_dma", "gate",
                 "edges", "needs_inc", "rank", "dma_sem", "dma_target", "tag")

    def __init__(self, idx, engine, emit, reads, writes, is_dma, gate, tag):
        self.idx, self.engine, self.emit = idx, engine, emit
        self.reads, self.writes = list(reads), list(writes)
        self.is_dma, self.gate, self.tag = is_dma, gate, tag
        self.edges = {}
        self.needs_inc = False
        self.rank = None
        self.dma_sem = None
        self.dma_target = None


class Gate:
    __slots__ = ("gid", "cond_ap", "cond_buf", "thresh", "key")

    def __init__(self, gid, cond_ap, cond_buf, thresh, key):
        self.gid, self.cond_ap, self.cond_buf, self.thresh = gid, cond_ap, cond_buf, thresh
        self.key = key


class Prog:
    def __init__(self, n_dma_sems=12):
        self.ops = []
        self._cur_gate = None
        self._gid = 0
        self.n_dma = n_dma_sems
        self._rr = 0

    def op(self, engine, emit, reads=(), writes=(), tag=""):
        o = Op(len(self.ops), engine, emit, reads, writes, False, self._cur_gate, tag)
        self.ops.append(o)
        return o

    def dma(self, engine, emit, reads=(), writes=(), tag=""):
        o = Op(len(self.ops), engine, emit, reads, writes, True, self._cur_gate, tag)
        self.ops.append(o)
        return o

    class _G:
        def __init__(s, p, g):
            s.p, s.g = p, g

        def __enter__(s):
            assert s.p._cur_gate is None
            s.p._cur_gate = s.g
            return s.g

        def __exit__(s, *a):
            s.p._cur_gate = None
            return False

    def gate(self, cond_ap, cond_buf, thresh, key=None):
        g = Gate(self._gid, cond_ap, cond_buf, thresh,
                 key if key is not None else ("g", self._gid))
        self._gid += 1
        return Prog._G(self, g)

    def _edges(self):
        spaces = {}
        gate_first = set()
        for o in self.ops:
            if o.gate is not None:
                key = (o.gate.gid, o.engine)
                if key not in gate_first:
                    gate_first.add(key)
                    o.reads = o.reads + [o.gate.cond_buf]
        for o in self.ops:
            def add(j):
                if j is not None and j != o.idx:
                    o.edges[j] = True
            for b in o.reads:
                for s in spaces.setdefault(b.space, []):
                    if s.lo < b.hi and b.lo < s.hi:
                        add(s.writer)
                        s.readers.append(o.idx)
            for b in o.writes:
                ss = spaces.setdefault(b.space, [])
                new = []
                for s in ss:
                    if s.lo < b.hi and b.lo < s.hi:
                        add(s.writer)
                        for r in s.readers:
                            add(r)
                        if s.lo < b.lo:
                            new.append(_Seg(s.lo, b.lo, s.writer, list(s.readers)))
                        if b.hi < s.hi:
                            new.append(_Seg(b.hi, s.hi, s.writer, list(s.readers)))
                    else:
                        new.append(s)
                new.append(_Seg(b.lo, b.hi, o.idx, []))
                spaces[b.space] = new
        for o in self.ops:
            keep = {}
            for j in o.edges:
                pj = self.ops[j]
                if (not pj.is_dma) and (not o.is_dma) and pj.engine == "pe" \
                        and o.engine == "pe":
                    continue
                keep[j] = True
            o.edges = keep
            for j in keep:
                self.ops[j].needs_inc = True
        cnt = {e: 0 for e in ENGINES}
        dma_t = [0] * self.n_dma
        for o in self.ops:
            if o.is_dma:
                o.needs_inc = True
                o.dma_sem = self._rr
                self._rr = (self._rr + 1) % self.n_dma
                o.dma_target = dma_t[o.dma_sem] + 16
                dma_t[o.dma_sem] = o.dma_target
            elif o.needs_inc:
                cnt[o.engine] += 1
                o.rank = cnt[o.engine]
        self.final_dma = dma_t

    def build_streams(self, nc):
        self._edges()
        self.sem_eng = {e: nc.alloc_semaphore(f"se_{e}") for e in ENGINES}
        self.sem_dma = [nc.alloc_semaphore(f"sd_{i}") for i in range(self.n_dma)]

        def sem_of(key):
            return self.sem_eng[key[1]] if key[0] == "e" else self.sem_dma[key[1]]

        def waits_of(o):
            w = {}
            for j in o.edges:
                pj = self.ops[j]
                if pj.is_dma:
                    k, v = ("d", pj.dma_sem), pj.dma_target
                else:
                    if pj.rank is None:
                        continue
                    k, v = ("e", pj.engine), pj.rank
                if w.get(k, 0) < v:
                    w[k] = v
            if o.is_dma and o.dma_target > 16:
                k = ("d", o.dma_sem)
                if w.get(k, 0) < o.dma_target - 16:
                    w[k] = o.dma_target - 16
            return w

        def emit_one(eng, o, sat):
            for k, v in sorted(waits_of(o).items()):
                if sat.get(k, 0) < v:
                    eng.wait_ge(sem_of(k), v)
                    sat[k] = v
            inst = o.emit(eng)
            if o.is_dma:
                inst.then_inc(self.sem_dma[o.dma_sem], 16)
            elif o.rank is not None:
                inst.then_inc(self.sem_eng[o.engine], 1)

        def stream(ename):
            def run(eng):
                sat = {}
                cond_regs = {}
                ops = [o for o in self.ops if o.engine == ename]
                i = 0
                while i < len(ops):
                    o = ops[i]
                    if o.gate is None:
                        emit_one(eng, o, sat)
                        i += 1
                        continue
                    g = o.gate
                    group = []
                    while i < len(ops) and ops[i].gate is g:
                        group.append(ops[i])
                        i += 1
                    for k, v in sorted(waits_of(group[0]).items()):
                        if sat.get(k, 0) < v:
                            eng.wait_ge(sem_of(k), v)
                            sat[k] = v
                    if g.key in cond_regs:
                        greg = cond_regs[g.key]
                    else:
                        greg = eng.alloc_register(f"greg_{ename}_{len(cond_regs)}")
                        cond_regs[g.key] = greg
                        cv = eng.value_load(g.cond_ap)
                        eng.reg_mov(greg, cv)
                    with eng.If_cmp(greg, g.thresh, "IS_GE"):
                        gsat = dict(sat)
                        for oo in group:
                            emit_one(eng, oo, gsat)
                    with eng.Else():
                        incs = {}
                        first_t = {}
                        first_rank = None
                        for oo in group:
                            if oo.is_dma:
                                k = ("d", oo.dma_sem)
                                incs[k] = incs.get(k, 0) + 16
                                if k not in first_t:
                                    first_t[k] = oo.dma_target - 16
                            elif oo.rank is not None:
                                k = ("e", ename)
                                incs[k] = incs.get(k, 0) + 1
                                if first_rank is None:
                                    first_rank = oo.rank
                        for k, v in sorted(incs.items()):
                            if k[0] == "d" and first_t.get(k, 0) > 0:
                                eng.wait_ge(sem_of(k), first_t[k])
                            elif k[0] == "e" and first_rank and first_rank > 1:
                                eng.wait_ge(sem_of(k), first_rank - 1)
                            eng.sem_inc(sem_of(k), v)
            return run
        return stream


class Arena:
    def __init__(self, nc, name, part_bytes):
        self.t = nc.alloc_sbuf_tensor(name, [128, part_bytes], U8)
        self.ap_all = self.t.ap()
        self.space = "sb:" + name
        self.cap = part_bytes
        self.off = 0
        self.marks = []
        self.peak = 0

    def alloc(self, rows, cols, dt, name="", align=64):
        esz = mybir.dt.size(dt)
        off = (self.off + align - 1) // align * align
        sz = cols * esz
        assert off + sz <= self.cap, f"arena OOM at {name}: {off}+{sz} > {self.cap}"
        self.off = off + sz
        self.peak = max(self.peak, self.off)
        ap = self.ap_all[0:rows, off:off + sz].bitcast(dt)
        return ap, Buf(self.space, off, off + sz, name)

    def mark(self):
        self.marks.append(self.off)

    def release(self):
        self.off = self.marks.pop()


def _np_bf16(a):
    return np.ascontiguousarray(np.asarray(a).astype(BF))


def _rel_bias_masks():
    pos = np.arange(GSZ)
    n = -(pos[None, :] - pos[:, None])
    nb = 16
    ret = (n < 0).astype(np.int64) * nb
    n = np.abs(n)
    mx = nb // 2
    small = n < mx
    vl = mx + (np.log(np.maximum(n, 1) / mx) / math.log(128 / mx) * (nb - mx)).astype(np.int64)
    vl = np.minimum(vl, nb - 1)
    bucket = ret + np.where(small, n, vl)
    bt = bucket.T
    masks = np.zeros((32, GSZ, GSZ), np.float32)
    sc = QKD ** 0.5
    for k in range(32):
        masks[k] = (bt == k) * sc
    return _np_bf16(masks.reshape(32 * GSZ, GSZ))


def _l_strict128():
    return _np_bf16((np.arange(128)[:, None] < np.arange(128)[None, :]).astype(np.float32))


def _e16_const():
    e = np.zeros((128, 8 * 16), np.float32)
    for m in range(8):
        for j in range(2):
            e[j * 64:(j + 1) * 64, m * 16 + 2 * m + j] = 0.125
    return _np_bf16(e)


def _f16_const():
    f = np.zeros((16, 8 * 128), np.float32)
    for m in range(8):
        for j in range(2):
            f[2 * m + j, m * 128 + j * 64:m * 128 + (j + 1) * 64] = 1.0
    return _np_bf16(f)


def _t1_const():
    t1 = np.zeros((128, 16), np.float32)
    for tp in range(16):
        t1[tp * 8:(tp + 1) * 8, tp + 1:] = 1.0
    return _np_bf16(t1)


def _ohe_const():
    o = np.zeros((128, 8), np.float32)
    for t in range(16):
        for e in range(8):
            o[t * 8 + e, e] = 1.0
    return _np_bf16(o)


def build_program(debug=0):
    PH = int(os.environ.get("KPHASE", "6"))
    nc = bass.Bass("TRN2", target_bir_lowering=False, enable_asserts=False)

    def ein(name, shape, dt=F32):
        return nc.dram_tensor(name, list(shape), dt, kind="ExternalInput").ap()

    xs = ein("xs", (B, NH, D))
    sel = ein("sel", (1, 16))
    nonce = ein("nonce", (1, 4))
    norm_g = ein("norm_g", (1, D))
    norm_b = ein("norm_b", (1, D))
    w_h = ein("w_h", (D, 2 * HID))
    b_h = ein("b_h", (1, 2 * HID))
    w_qk = ein("w_qk", (D, QKD))
    b_qk = ein("b_qk", (1, QKD))
    qk_gamma = ein("qk_gamma", (4, QKD))
    qk_beta = ein("qk_beta", (4, QKD))
    rel_emb = ein("rel_emb", (1, 32))
    w_o = ein("w_o", (HID, D))
    b_o = ein("b_o", (1, D))
    ln1_w = ein("ln1_w", (1, D))
    ln1_b = ein("ln1_b", (1, D))
    ln2_w = ein("ln2_w", (1, D))
    ln2_b = ein("ln2_b", (1, D))
    qkv_wT = ein("qkv_wT", (D, HID))
    proj_wT = ein("proj_wT", (D, D))
    router_w = ein("router_w", (D, NEXP))
    moe_w1 = ein("moe_w1", (4, D, FFNH))
    moe_w2 = ein("moe_w2", (4, FFNH, D))

    out = nc.dram_tensor("out", [NH, D], F32, kind="ExternalOutput").ap()

    sh_flags = nc.dram_tensor("sh_flags", [1, 64], F32, kind="Internal", addr_space="Shared").ap()
    sh_cnt = nc.dram_tensor("sh_cnt", [2, 32], F32, kind="Internal", addr_space="Shared").ap()
    sh_x2 = nc.dram_tensor("sh_x2", [2 * B * NH, D], BF16, kind="Internal", addr_space="Shared").ap()
    sh_linkv = nc.dram_tensor("sh_linkv", [2, QKD, HID], F32, kind="Internal", addr_space="Shared").ap()
    sh_S = nc.dram_tensor("sh_S", [NEXP * ISL, 1], F32, kind="Internal", addr_space="Shared").ap()
    sh_y1 = nc.dram_tensor("sh_y1", [NEXP * ISL + 1, D // 2], F32, kind="Internal", addr_space="Shared").ap()
    sh_y2 = nc.dram_tensor("sh_y2", [NEXP * ISL + 1, D // 2], F32, kind="Internal", addr_space="Shared").ap()

    pr_haa = nc.dram_tensor("pr_haa", [NH, D], F32, kind="Internal").ap()
    pr_x = nc.dram_tensor("pr_x", [NH, D], F32, kind="Internal").ap()
    pr_xn = nc.dram_tensor("pr_xn", [NH, D], F32, kind="Internal").ap()
    pr_outp = nc.dram_tensor("pr_outp", [NH, D], F32, kind="Internal").ap()
    pr_norm = nc.dram_tensor("pr_norm", [4 * HID, NH], BF16, kind="Internal").ap()
    pr_gate = nc.dram_tensor("pr_gate", [HID, NH], BF16, kind="Internal").ap()
    pr_quad = nc.dram_tensor("pr_quad", [HID, NH], BF16, kind="Internal").ap()

    dbg = {}
    b_dbg = {}
    if debug:
        for nm, shape in [("xn", (NH, D)), ("attn", (D, NH)), ("haa", (NH, D)),
                          ("logits", (4 * NH, NEXP)), ("cnt", (1, 32)),
                          ("gidx", (128, 32)), ("pk", (128, 32)), ("linkv", (QKD, HID)),
                          ("y1", (4 * ISL, 512)), ("y2", (4 * ISL, 512)),
                          ("ngi", (1, 64))]:
            dbg[nm] = nc.dram_tensor("dbg_" + nm, list(shape), F32, kind="ExternalOutput").ap()
            b_dbg[nm] = Buf("dram:dbg_" + nm, 0, 1 << 40, nm)

    c_idb = nc.inline_tensor(np.eye(128, dtype=BF), "c_idb").ap()
    c_idf = nc.inline_tensor(np.eye(128, dtype=np.float32), "c_idf").ap()
    c_l128 = nc.inline_tensor(_l_strict128(), "c_l128").ap()
    c_relm = nc.inline_tensor(_rel_bias_masks(), "c_relm").ap()
    c_e16 = nc.inline_tensor(_e16_const(), "c_e16").ap()
    c_f16 = nc.inline_tensor(_f16_const(), "c_f16").ap()
    c_t1 = nc.inline_tensor(_t1_const(), "c_t1").ap()
    c_ohe = nc.inline_tensor(_ohe_const(), "c_ohe").ap()
    c_misc = nc.inline_tensor(np.concatenate([
        np.ones((1, 128), np.float32),
        np.arange(128, dtype=np.float32)[None, :],
    ], axis=0), "c_misc").ap()
    c_onescb = nc.inline_tensor(np.ones((128, 1), dtype=BF), "c_onescb").ap()
    c_iota8c = nc.inline_tensor(np.arange(8, dtype=np.float32)[:, None], "c_iota8c").ap()
    c_ones8f = nc.inline_tensor(np.ones((8, 1), np.float32), "c_ones8f").ap()
    c_onesr_b16 = nc.inline_tensor(np.ones((1, 128), dtype=BF), "c_onesr16").ap()
    c_estr = nc.inline_tensor((np.arange(8, dtype=np.float32) * ISL)[:, None], "c_estr").ap()

    P = Prog()
    ar = Arena(nc, "arena", 190 * 1024)

    ps = [nc.alloc_psum_tensor(f"ps{i}", [128, 512], F32).ap() for i in range(4)]
    psb = [Buf(f"psum{i}", 0, 2048) for i in range(4)]
    pd = [nc.alloc_psum_tensor(f"pd{i}", [128, 1024], F32).ap() for i in range(2)]
    pdb = [Buf(f"psumd{i}", 0, 4096) for i in range(2)]
    pst_b = ps[3].bitcast(BF16)
    pdt_b = pd[1].bitcast(BF16)

    def DB(name):
        return Buf("dram:" + name, 0, 1 << 40, name)

    bI = {nm: DB(nm) for nm in ["xs", "w_h", "w_qk", "w_o", "qkv_wT", "proj_wT",
                                "router_w", "moe_w1", "moe_w2", "b_h", "b_qk",
                                "qk_gamma", "qk_beta", "rel_emb", "b_o", "norm_g",
                                "norm_b", "ln1_w", "ln1_b", "ln2_w", "ln2_b",
                                "sel", "nonce", "consts"]}
    b_out = DB("out")
    b_haa = DB("pr_haa")
    b_x = DB("pr_x")
    b_xn = DB("pr_xn")
    b_outp = DB("pr_outp")
    b_norm = DB("pr_norm")
    b_gate = DB("pr_gate")
    b_quad = DB("pr_quad")
    b_shx2 = DB("sh_x2")
    b_shcnt = DB("sh_cnt")
    b_shlinkv = DB("sh_linkv")
    b_shS = DB("sh_S")
    b_shy1 = DB("sh_y1")
    b_shy2 = DB("sh_y2")
    b_flags = DB("sh_flags")
    btok = [DB(f"btok{i}") for i in range(4)]

    chsem = {e: nc.alloc_semaphore("ch_" + e) for e in ENGINES}
    chcnt = {e: 0 for e in ENGINES}

    def mkch(ename):
        def ch(e, inst):
            chcnt[ename] += 1
            inst.then_inc(chsem[ename], 1)
            e.wait_ge(chsem[ename], chcnt[ename])
            return inst
        return ch
    CH = {e: mkch(e) for e in ENGINES}

    def vload(e, ap, lo, hi):
        return nc.s_assert_within(e.value_load(ap), lo, hi, skip_runtime_assert=True)

    rbc_nh = nc.gpsimd.alloc_register("rbc_nh")
    rbc_x2 = nc.gpsimd.alloc_register("rbc_x2")
    rbc_S = nc.gpsimd.alloc_register("rbc_S")
    rbc_y = nc.gpsimd.alloc_register("rbc_y")
    rbc_nrm = nc.gpsimd.alloc_register("rbc_nrm")
    rbc_lkv = nc.gpsimd.alloc_register("rbc_lkv")

    def _set_rbc(e):
        e.reg_mov(rbc_nh, NH - 1)
        e.reg_mov(rbc_x2, 2 * B * NH - 1)
        e.reg_mov(rbc_S, NEXP * ISL - 1)
        e.reg_mov(rbc_nrm, 4 * HID - 1)
        e.reg_mov(rbc_lkv, 2 * QKD - 1)
        return e.reg_mov(rbc_y, NEXP * ISL)
    P.op("pool", _set_rbc, tag="rbc")

    selr, selr_b = ar.alloc(1, 16, F32, "selr")
    tot16k, tot16k_b = ar.alloc(1, 1, F32, "tot16k")
    noncer, noncer_b = ar.alloc(1, 4, F32, "noncer")
    NBAR = 4
    magic, magic_b = ar.alloc(1, 4 * (NBAR + 2), F32, "magic")
    hint, hint_b = ar.alloc(1, 4, I32, "hint")
    pint, pint_b = ar.alloc(1, 4, I32, "pint")
    barsc, barsc_b = ar.alloc(1, 32, F32, "barsc")
    cols, cols_b = ar.alloc(128, 80, F32, "cols")
    selc, selc_b = ar.alloc(128, 8, F32, "selc")
    routw, routw_b = ar.alloc(128, 8 * NEXP, F32, "routw")
    biasT, biasT_b = ar.alloc(128, 2 * GSZ, BF16, "biasT")
    m1o, m1o_b = ar.alloc(128, 16 * NEXP, BF16, "m1o")
    m2o, m2o_b = ar.alloc(128, 16 * NEXP, BF16, "m2o")
    dzo, dzo_b = ar.alloc(128, 16, F32, "dzo")
    cntrow, cntrow_b = ar.alloc(1, 32, F32, "cntrow")
    gidx_all, gidx_all_b = ar.alloc(128, 32, F32, "gidx_all")
    pk_all, pk_all_b = ar.alloc(128, 32, F32, "pk_all")
    ngi, ngi_b = ar.alloc(1, 8, I32, "ngi")
    P.op("dve", lambda e: e.memset(tot16k[0:1, 0:1], 16384.0),
         writes=[tot16k_b], tag="t16k")
    s_idb, s_idb_b = ar.alloc(128, 128, BF16, "s_idb")
    s_idf, s_idf_b = ar.alloc(128, 128, F32, "s_idf")
    s_l128, s_l128_b = ar.alloc(128, 128, BF16, "s_l128")
    s_e16, s_e16_b = ar.alloc(128, 8 * 16, BF16, "s_e16")
    s_f16, s_f16_b = ar.alloc(16, 8 * 128, BF16, "s_f16")
    s_t1, s_t1_b = ar.alloc(128, 16, BF16, "s_t1")
    s_ohe, s_ohe_b = ar.alloc(128, 8, BF16, "s_ohe")
    s_ones1f, s_ones1f_b = ar.alloc(1, 128, F32, "s_ones1f")
    s_iota1f, s_iota1f_b = ar.alloc(1, 128, F32, "s_iota1f")
    s_onescb, s_onescb_b = ar.alloc(128, 1, BF16, "s_onescb")
    s_iota8c, s_iota8c_b = ar.alloc(8, 1, F32, "s_iota8c")
    s_ones8f, s_ones8f_b = ar.alloc(8, 1, F32, "s_ones8f")
    s_estr, s_estr_b = ar.alloc(8, 1, F32, "s_estr")
    rowbuf, rowbuf_b = ar.alloc(1, D, F32, "rowbuf")

    P.dma("sp", lambda e: e.dma_start(out=selr[:, :], in_=sel[:, :]), reads=[bI["sel"]],
          writes=[selr_b], tag="ld_sel")
    P.dma("sp", lambda e: e.dma_start(out=noncer[:, :], in_=nonce[:, :]), reads=[bI["nonce"]],
          writes=[noncer_b], tag="ld_nonce")
    for apx, src, bwr in [(s_idb, c_idb, s_idb_b), (s_idf, c_idf, s_idf_b),
                          (s_l128, c_l128, s_l128_b), (s_t1, c_t1, s_t1_b),
                          (s_ohe, c_ohe, s_ohe_b), (s_onescb, c_onescb, s_onescb_b),
                          (s_iota8c, c_iota8c, s_iota8c_b), (s_estr, c_estr, s_estr_b),
                          (s_ones8f, c_ones8f, s_ones8f_b)]:
        P.dma("sp", lambda e, apx=apx, src=src: e.dma_start(out=apx[:, :], in_=src[:, :]),
              reads=[bI["consts"]], writes=[bwr], tag="ldc")
    P.dma("sp", lambda e: e.dma_start(out=s_e16[:, :], in_=c_e16[:, :]),
          reads=[bI["consts"]], writes=[s_e16_b], tag="ldc_e16")
    P.dma("sp", lambda e: e.dma_start(out=s_f16[:, :], in_=c_f16[:, :]),
          reads=[bI["consts"]], writes=[s_f16_b], tag="ldc_f16")
    P.dma("sp", lambda e: e.dma_start(out=s_ones1f[:, :], in_=c_misc[0:1, :]),
          reads=[bI["consts"]], writes=[s_ones1f_b], tag="ldc_ones")
    P.dma("sp", lambda e: e.dma_start(out=s_iota1f[:, :], in_=c_misc[1:2, :]),
          reads=[bI["consts"]], writes=[s_iota1f_b], tag="ldc_iota")

    def mk_magic(e):
        c = CH["dve"]
        for k in range(NBAR + 2):
            e.tensor_scalar(magic[0:1, 4 * k:4 * k + 4], noncer[0:1, 0:4], float(k),
                            None, ALU.add)
        e.tensor_copy(hint[0:1, 0:1], selr[0:1, 8:9])
        return e.tensor_copy(pint[0:1, 0:2], selr[0:1, 9:11])
    P.op("dve", mk_magic, reads=[noncer_b, selr_b], writes=[magic_b, hint_b, pint_b],
         tag="magics")

    barsem = [nc.alloc_semaphore(f"bsem{i}") for i in range(NBAR)]
    bpol = [nc.alloc_semaphore(f"bpol{i}") for i in range(NBAR)]

    def barrier(k, guards, tag=""):
        def emit(e):
            h = vload(e, hint[0:1, 0:1], 0, 1)
            e.dma_start(out=sh_flags[0:1, bass.ts(h, 4)],
                        in_=magic[0:1, 4 * k:4 * (k + 1)]).then_inc(barsem[k - 1], 16)
            e.wait_ge(barsem[k - 1], 16)
            e.tensor_copy(barsc[0:1, 24:25], tot16k[0:1, 0:1])
            a = barsc[0:1, 8:16]
            b2 = barsc[0:1, 16:24]
            with e.register(f"nd{k}") as ndr, e.register(f"bt{k}") as tgt:
                e.reg_mov(ndr, 1)
                e.reg_mov(tgt, 0)
                with e.While(ndr):
                    e.dma_start(out=barsc[0:1, 0:8], in_=sh_flags[0:1, 0:8]).then_inc(bpol[k - 1], 16)
                    e.reg_add(tgt, tgt, 16)
                    e.wait_ge(bpol[k - 1], tgt)
                    e.tensor_tensor(a[0:1, 0:4], barsc[0:1, 0:4],
                                    magic[0:1, 4 * k:4 * k + 4], ALU.subtract)
                    e.tensor_tensor(a[0:1, 4:8], barsc[0:1, 4:8],
                                    magic[0:1, 4 * k:4 * k + 4], ALU.subtract)
                    e.tensor_tensor(a[0:1, 0:8], a[0:1, 0:8], a[0:1, 0:8], ALU.mult)
                    e.tensor_tensor(b2[0:1, 0:4], barsc[0:1, 0:4],
                                    magic[0:1, 4 * (k + 1):4 * (k + 1) + 4], ALU.subtract)
                    e.tensor_tensor(b2[0:1, 4:8], barsc[0:1, 4:8],
                                    magic[0:1, 4 * (k + 1):4 * (k + 1) + 4], ALU.subtract)
                    e.tensor_tensor(b2[0:1, 0:8], b2[0:1, 0:8], b2[0:1, 0:8], ALU.mult)
                    e.tensor_tensor(a[0:1, 0:8], a[0:1, 0:8], b2[0:1, 0:8], ALU.mult)
                    e.tensor_tensor(a[0:1, 0:4], a[0:1, 0:4], a[0:1, 4:8], ALU.add)
                    e.tensor_tensor(a[0:1, 0:2], a[0:1, 0:2], a[0:1, 2:4], ALU.add)
                    e.tensor_tensor(barsc[0:1, 24:25], a[0:1, 0:1], a[0:1, 1:2], ALU.add)
                    nd = e.value_load(barsc[0:1, 24:25].bitcast(I32))
                    e.reg_mov(ndr, nd)
            return e.engine_nop()
        P.op("pool", emit, reads=list(guards) + [magic_b, hint_b, barsc_b, b_flags,
                                                 tot16k_b],
             writes=[btok[k - 1], barsc_b, b_flags], tag=tag or f"bar{k}")

    def colify(dst_col_ap, dst_buf, src_row_ap, src_buf, tag, scale=None):
        P.op("pe", lambda e: e.matmul(ps[0][:, 0:1], src_row_ap, s_ones1f[0:1, 0:1],
                                      start=True, stop=True),
             reads=[src_buf, s_ones1f_b], writes=[psb[0]], tag="cf_" + tag)
        if scale is None:
            P.op("dve", lambda e: e.tensor_copy(dst_col_ap, ps[0][:, 0:1]),
                 reads=[psb[0]], writes=[dst_buf], tag="cfe_" + tag)
        else:
            P.op("dve", lambda e: e.tensor_scalar(dst_col_ap, ps[0][:, 0:1], scale,
                                                  None, ALU.mult),
                 reads=[psb[0]], writes=[dst_buf], tag="cfe_" + tag)

    def bcast_row(dst_ap, dst_buf, src_row_ap, src_buf, width, tag):
        for f0 in range(0, width, 512):
            w = min(512, width - f0)
            P.op("pe", lambda e, f0=f0, w=w: e.matmul(
                ps[0][:, 0:w], s_ones1f[0:1, 0:128], src_row_ap[0:1, f0:f0 + w],
                start=True, stop=True),
                reads=[src_buf, s_ones1f_b], writes=[psb[0]], tag=f"bc_{tag}{f0}")
            P.op("dve", lambda e, f0=f0, w=w: e.tensor_copy(dst_ap[:, f0:f0 + w],
                                                            ps[0][:, 0:w]),
                 reads=[psb[0]], writes=[dst_buf], tag=f"bce_{tag}{f0}")

    def ld_row(src_ap, src_buf, w, tag):
        P.dma("sp", lambda e: e.dma_start(out=rowbuf[0:1, 0:w], in_=src_ap[:, :]),
              reads=[src_buf], writes=[rowbuf_b.sub(0, w * 4)], tag="ldr_" + tag)

    def cols_from_row(col0, nf, tag, off=0, scale=None):
        for i in range(nf):
            colify(cols[:, col0 + i:col0 + i + 1], cols_b.sub((col0 + i) * 4, (col0 + i + 1) * 4),
                   rowbuf[0:1, off + i * 128:off + (i + 1) * 128],
                   rowbuf_b.sub((off + i * 128) * 4, (off + (i + 1) * 128) * 4),
                   f"{tag}{i}", scale=scale)

    C_LN1W, C_LN1B, C_LN2W, C_LN2B, C_NG, C_NB, C_BQK = 0, 8, 16, 24, 32, 40, 48
    C_G0, C_BT0, C_BHG = 49, 53, 57

    ld_row(ln1_w, bI["ln1_w"], D, "ln1w"); cols_from_row(C_LN1W, 8, "ln1w")
    ld_row(ln1_b, bI["ln1_b"], D, "ln1b"); cols_from_row(C_LN1B, 8, "ln1b")
    ld_row(ln2_w, bI["ln2_w"], D, "ln2w"); cols_from_row(C_LN2W, 8, "ln2w")
    ld_row(ln2_b, bI["ln2_b"], D, "ln2b"); cols_from_row(C_LN2B, 8, "ln2b")
    ld_row(norm_g, bI["norm_g"], D, "ng"); cols_from_row(C_NG, 8, "ng")
    ld_row(norm_b, bI["norm_b"], D, "nb"); cols_from_row(C_NB, 8, "nb")
    ld_row(b_qk, bI["b_qk"], QKD, "bqk"); cols_from_row(C_BQK, 1, "bqk")
    gb_sc = {0: 1.0 / 16, 1: 1.0, 2: 1.0 / 16, 3: 1.0 / 4096}
    for hh in range(4):
        P.dma("sp", lambda e, hh=hh: e.dma_start(out=rowbuf[0:1, 0:QKD],
                                                 in_=qk_gamma[hh:hh + 1, :]),
              reads=[bI["qk_gamma"]], writes=[rowbuf_b.sub(0, QKD * 4)], tag=f"ldg{hh}")
        cols_from_row(C_G0 + hh, 1, f"g{hh}", scale=gb_sc[hh])
        P.dma("sp", lambda e, hh=hh: e.dma_start(out=rowbuf[0:1, 0:QKD],
                                                 in_=qk_beta[hh:hh + 1, :]),
              reads=[bI["qk_beta"]], writes=[rowbuf_b.sub(0, QKD * 4)], tag=f"ldbt{hh}")
        cols_from_row(C_BT0 + hh, 1, f"bt{hh}", scale=gb_sc[hh])
    for piece in range(2):
        P.dma("sp", lambda e, piece=piece: e.dma_start(
            out=rowbuf[0:1, 0:D], in_=b_h[0:1, HID + piece * D:HID + (piece + 1) * D]),
            reads=[bI["b_h"]], writes=[rowbuf_b], tag=f"ldbhg{piece}")
        cols_from_row(C_BHG + piece * 8, 8, f"bhg{piece}")
    P.op("pe", lambda e: e.matmul(ps[0][:, 0:8], s_ones1f[0:1, 0:128], selr[0:1, 0:8],
                                  start=True, stop=True),
         reads=[selr_b, s_ones1f_b], writes=[psb[0]], tag="selbc")
    P.op("dve", lambda e: e.tensor_copy(selc[:, 0:8], ps[0][:, 0:8]),
         reads=[psb[0]], writes=[selc_b], tag="selbc_e")
    iotacol, iotacol_b = ar.alloc(128, 1, F32, "iotacol")
    colify(iotacol[:, 0:1], iotacol_b, s_iota1f[0:1, 0:128], s_iota1f_b, "iota")
    epscol, epscol_b = ar.alloc(128, 1, F32, "epscol")
    P.op("dve", lambda e: e.memset(epscol[:, 0:1], EPS), writes=[epscol_b], tag="eps")
    hp8k, hp8k_b = ar.alloc(128, 1, F32, "hp8k")
    P.op("pe", lambda e: e.matmul(ps[0][:, 0:1], s_ones1f[0:1, 0:128], selr[0:1, 8:9],
                                  start=True, stop=True),
         reads=[selr_b, s_ones1f_b], writes=[psb[0]], tag="hbc")
    P.op("dve", lambda e: e.tensor_scalar(hp8k[:, 0:1], ps[0][:, 0:1], float(B * NH),
                                          None, ALU.mult),
         reads=[psb[0]], writes=[hp8k_b], tag="hbce")

    for f in range(8):
        P.dma("sp", lambda e, f=f: e.dma_start(out=routw[:, f * 8:(f + 1) * 8],
                                               in_=router_w[f * 128:(f + 1) * 128, :]),
              reads=[bI["router_w"]], writes=[routw_b.sub(f * 32, (f + 1) * 32)],
              tag=f"ld_rout{f}")

    ar.mark()
    relrow, relrow_b = ar.alloc(1, 32, F32, "relrow")
    relcols, relcols_b = ar.alloc(128, 32, F32, "relcols")
    mstage, mstage_b = ar.alloc(128, GSZ, BF16, "mstage")
    bacc, bacc_b = ar.alloc(128, 2 * GSZ, F32, "bacc")
    P.dma("sp", lambda e: e.dma_start(out=relrow[0:1, :], in_=rel_emb[:, :]),
          reads=[bI["rel_emb"]], writes=[relrow_b], tag="ld_rel")
    bcast_row(relcols, relcols_b, relrow[0:1, 0:32], relrow_b, 32, "rel")
    P.op("dve", lambda e: e.memset(bacc[:, :], 0.0), writes=[bacc_b], tag="bacc0")
    for k in range(32):
        for jt in range(2):
            P.dma("sp", lambda e, k=k, jt=jt: e.dma_start(
                out=mstage[:, :], in_=c_relm[(2 * k + jt) * 128:(2 * k + jt + 1) * 128, :]),
                reads=[bI["consts"]], writes=[mstage_b], tag=f"ldm{k}_{jt}")
            P.op("dve", lambda e, k=k, jt=jt: e.scalar_tensor_tensor(
                bacc[:, jt * GSZ:(jt + 1) * GSZ], mstage[:, :], relcols[:, k:k + 1],
                bacc[:, jt * GSZ:(jt + 1) * GSZ], ALU.mult, ALU.add),
                reads=[mstage_b, relcols_b, bacc_b.sub(jt * GSZ * 4, (jt + 1) * GSZ * 4)],
                writes=[bacc_b.sub(jt * GSZ * 4, (jt + 1) * GSZ * 4)], tag=f"bst{k}_{jt}")
    P.op("dve", lambda e: e.tensor_copy(biasT[:, :], bacc[:, :]),
         reads=[bacc_b], writes=[biasT_b], tag="biasT")
    ar.release()

    ar.mark()
    zS, zS_b = ar.alloc(128, 64, F32, "zS")
    zrowz, zrowz_b = ar.alloc(1, D, F32, "zrowz")
    P.op("dve", lambda e: e.memset(zrowz[0:1, :], 0.0), writes=[zrowz_b], tag="zr0")
    P.dma("sp", lambda e: e.dma_start(out=sh_y1[ZROW:ZROW + 1, :], in_=zrowz[0:1, 0:D // 2]),
          reads=[zrowz_b], writes=[b_shy1.sub(ZROW * 2048, (ZROW + 1) * 2048)], tag="y_z1")
    P.dma("sp", lambda e: e.dma_start(out=sh_y2[ZROW:ZROW + 1, :], in_=zrowz[0:1, 0:D // 2]),
          reads=[zrowz_b], writes=[b_shy2.sub(ZROW * 2048, (ZROW + 1) * 2048)], tag="y_z2")
    P.op("dve", lambda e: e.memset(zS[:, :], 0.0), writes=[zS_b], tag="zS")
    for i in range(4):
        P.dma("sp", lambda e, i=i: e.dma_start(
            out=sh_S[i * 8192:(i + 1) * 8192, 0:1].rearrange("(p g) one -> p (g one)", p=128),
            in_=zS[:, :]),
            reads=[zS_b], writes=[b_shS.sub(i * 8192 * 4, (i + 1) * 8192 * 4)], tag=f"zS{i}")
    ar.release()

    ar.mark()
    qkvw, qkvw_b = ar.alloc(128, 8 * HID, BF16, "qkvw")
    projw, projw_b = ar.alloc(128, 8 * D, BF16, "projw")
    for f in range(8):
        P.dma("pool", lambda e, f=f: e.dma_start(out=qkvw[:, f * HID:(f + 1) * HID],
                                                 in_=qkv_wT[f * 128:(f + 1) * 128, :]),
              reads=[bI["qkv_wT"]], writes=[qkvw_b.sub(f * HID * 2, (f + 1) * HID * 2)],
              tag=f"ld_qkvw{f}")
        P.dma("pool", lambda e, f=f: e.dma_start(out=projw[:, f * D:(f + 1) * D],
                                                 in_=proj_wT[f * 128:(f + 1) * 128, :]),
              reads=[bI["proj_wT"]], writes=[projw_b.sub(f * D * 2, (f + 1) * D * 2)],
              tag=f"ld_projw{f}")

    xck, xck_b = ar.alloc(128, NSUB * D, F32, "xck")
    xnk, xnk_b = ar.alloc(128, NSUB * D, F32, "xnk")
    scr4, scr4_b = ar.alloc(128, D, F32, "scr4")
    st8, st8_b = ar.alloc(128, 8, F32, "st8")
    ln1T, ln1T_b = ar.alloc(128, 8 * CHK, BF16, "ln1T")
    qkvT, qkvT_b = ar.alloc(128, 8 * CHK, BF16, "qkvT")
    kvs, kvs_b = [], []
    for s in range(4):
        a_, b_ = ar.alloc(128, 8 * CHK, BF16, f"kv{s}")
        kvs.append(a_); kvs_b.append(b_)
    zbuf, zbuf_b = ar.alloc(16, 4 * CHK, F32, "zbuf")
    zmx, zmx_b = ar.alloc(16, CHK, F32, "zmx")
    wrows, wrows_b = ar.alloc(16, 4 * CHK, BF16, "wrows")
    kd, kd_b = ar.alloc(128, CHK, BF16, "kd")
    zp, zp_b = ar.alloc(128, 8 * CHK, BF16, "zp")
    aom, aom_b = ar.alloc(128, CHK, F32, "aom")
    tmpf, tmpf_b = ar.alloc(128, D, F32, "tmpf")
    haak, haak_b = ar.alloc(128, D, F32, "haak")
    x2b, x2b_b = ar.alloc(128, D, BF16, "x2b")
    lgrow, lgrow_b = ar.alloc(128, NSUB * NEXP, F32, "lgrow")
    topsc, topsc_b = ar.alloc(128, 4 * NEXP, F32, "topsc")
    msb, msb_b = ar.alloc(128, NEXP, BF16, "msb")
    idxc, idxc_b = ar.alloc(128, 8, F32, "idxc")
    idxu, idxu_b = ar.alloc(128, 8, U32, "idxu")

    assert ar.off <= ar.cap, ar.off

    def tcol(apx, t, w):
        return apx[:, t * w:(t + 1) * w]

    for c in range(NCHK):
        r0 = c * CHK
        for s in range(4):
            sfx = f"_{c}_{s}"
            for q in range(NSUB):
                P.dma("sp", lambda e, s=s, r0=r0, q=q: e.dma_start(
                    out=xck[:, q * D:(q + 1) * D],
                    in_=xs[s, r0 + q * 128:r0 + (q + 1) * 128, :]),
                    reads=[bI["xs"]], writes=[xck_b], tag=f"ldx{q}" + sfx)

            def haaidx(e, s=s, r0=r0):
                ch = CH["dve"]
                for q in range(NSUB):
                    ch(e, e.tensor_scalar(idxc[:, q:q + 1], selc[:, s:s + 1], -HUGE, HUGE,
                                          ALU.mult, ALU.add))
                    ch(e, e.tensor_tensor(idxc[:, q:q + 1], idxc[:, q:q + 1], iotacol[:, 0:1],
                                          ALU.add))
                    ch(e, e.tensor_scalar(idxc[:, q:q + 1], idxc[:, q:q + 1],
                                          float(r0 + q * 128), None, ALU.add))
                    ch(e, e.tensor_tensor(idxc[:, 4 + q:5 + q], iotacol[:, 0:1],
                                          hp8k[:, 0:1], ALU.add))
                    ch(e, e.tensor_scalar(idxc[:, 4 + q:5 + q], idxc[:, 4 + q:5 + q],
                                          float(s * NH + r0 + q * 128), None, ALU.add))
                return e.tensor_copy(idxu[:, 0:8], idxc[:, 0:8])
            P.op("dve", haaidx, reads=[selc_b, iotacol_b, idxc_b, idxu_b, hp8k_b],
                 writes=[idxc_b, idxu_b], tag="hidx" + sfx)
            for q in range(NSUB):
                P.dma("pool", lambda e, q=q: e.indirect_dma_start(
                    pr_x[:, :], bass.IndirectOffsetOnAxis(idxu[:, q:q + 1], 0),
                    xck[:, q * D:(q + 1) * D], None,
                    bounds_check=rbc_nh, oob_is_err=False),
                    reads=[xck_b, idxu_b], writes=[b_x], tag=f"xsp{q}" + sfx)

            for q in range(NSUB):
                xq = xck[:, q * D:(q + 1) * D]
                xnq = xnk[:, q * D:(q + 1) * D]

                def ln(e, xq=xq, xnq=xnq):
                    ch = CH["dve"]
                    ch(e, e.reduce_sum(st8[:, 0:1], xq, axis=AX.X))
                    ch(e, e.tensor_scalar(st8[:, 0:1], st8[:, 0:1], 1.0 / D, None, ALU.mult))
                    ch(e, e.tensor_scalar(xnq, xq, st8[:, 0:1], None, ALU.subtract))
                    ch(e, e.tensor_tensor(scr4[:, :], xnq, xnq, ALU.mult))
                    ch(e, e.reduce_sum(st8[:, 1:2], scr4[:, :], axis=AX.X))
                    return e.tensor_scalar(st8[:, 1:2], st8[:, 1:2], 1.0 / D, None, ALU.mult)
                P.op("dve", ln, reads=[xck_b, st8_b, xnk_b, scr4_b],
                     writes=[xnk_b, scr4_b, st8_b], tag="ln" + sfx + f"{q}")
                P.op("act", lambda e: e.activation(st8[:, 2:3], st8[:, 1:2], AF.Sqrt,
                                                   bias=epscol[:, 0:1], scale=1.0),
                     reads=[st8_b, epscol_b], writes=[st8_b], tag="lns" + sfx + f"{q}")

                def fin(e, xq=xq, xnq=xnq):
                    ch = CH["dve"]
                    ch(e, e.reciprocal(st8[:, 3:4], st8[:, 2:3]))
                    ch(e, e.tensor_scalar(xnq, xnq, st8[:, 3:4], None, ALU.mult))
                    ch(e, e.tensor_tensor(st8[:, 4:5], st8[:, 3:4], st8[:, 3:4], ALU.mult))
                    return e.tensor_tensor(st8[:, 4:5], st8[:, 4:5], st8[:, 1:2], ALU.mult)
                P.op("dve", fin, reads=[st8_b, xnk_b], writes=[st8_b, xnk_b],
                     tag="lnf" + sfx + f"{q}")
                P.op("act", lambda e: e.activation(st8[:, 5:6], st8[:, 4:5], AF.Sqrt,
                                                   bias=epscol[:, 0:1], scale=1.0),
                     reads=[st8_b, epscol_b], writes=[st8_b], tag="lns2" + sfx + f"{q}")

                def mku(e, xq=xq, xnq=xnq):
                    ch = CH["dve"]
                    ch(e, e.reciprocal(st8[:, 6:7], st8[:, 5:6]))
                    return e.tensor_scalar(xq, xnq, st8[:, 6:7], None, ALU.mult)
                P.op("dve", mku, reads=[st8_b, xnk_b], writes=[xck_b, st8_b],
                     tag="mku" + sfx + f"{q}")

            for q in range(NSUB):
                P.dma("pool", lambda e, q=q: e.indirect_dma_start(
                    pr_xn[:, :], bass.IndirectOffsetOnAxis(idxu[:, q:q + 1], 0),
                    xnk[:, q * D:(q + 1) * D], None,
                    bounds_check=rbc_nh, oob_is_err=False),
                    reads=[xnk_b, idxu_b], writes=[b_xn], tag=f"nsp2{q}" + sfx)

            for f in range(8):
                for q in range(NSUB):
                    P.op("pe", lambda e, f=f, q=q: e.transpose(
                        ps[3][:, q * 128:(q + 1) * 128],
                        xck[:, q * D + f * 128:q * D + (f + 1) * 128], s_idf[:, :]),
                        reads=[xck_b, s_idf_b], writes=[psb[3]], tag=f"trU{f}{q}" + sfx)
                P.op("act", lambda e, f=f: e.activation(
                    tcol(ln1T, f, CHK), ps[3][:, 0:CHK], AF.Identity,
                    bias=cols[:, C_LN1B + f:C_LN1B + f + 1],
                    scale=cols[:, C_LN1W + f:C_LN1W + f + 1]),
                    reads=[psb[3], cols_b], writes=[ln1T_b], tag=f"l1T{f}" + sfx)
                P.op("act", lambda e, f=f: e.activation(
                    tmpf.bitcast(BF16)[:, 0:CHK], ps[3][:, 0:CHK], AF.Identity,
                    bias=cols[:, C_NB + f:C_NB + f + 1],
                    scale=cols[:, C_NG + f:C_NG + f + 1]),
                    reads=[psb[3], cols_b], writes=[tmpf_b], tag=f"nrm{f}" + sfx)
                P.dma("sp", lambda e, f=f, s=s, r0=r0: e.dma_start(
                    out=pr_norm[s * HID + f * 128:s * HID + (f + 1) * 128, r0:r0 + CHK],
                    in_=tmpf.bitcast(BF16)[:, 0:CHK]),
                    reads=[tmpf_b], writes=[b_norm], tag=f"nsp{f}" + sfx)

            for f in range(16):
                for cc in range(8):
                    P.op("pe", lambda e, f=f, cc=cc: e.matmul(
                        ps[0][:, 0:CHK],
                        qkvw[:, cc * HID + f * 128:cc * HID + (f + 1) * 128],
                        tcol(ln1T, cc, CHK), start=(cc == 0), stop=(cc == 7)),
                        reads=[qkvw_b, ln1T_b], writes=[psb[0]], tag=f"qkv{f}" + sfx)
                if f % 2 == 0:
                    P.op("dve", lambda e, f=f: e.tensor_copy(tcol(qkvT, f // 2, CHK),
                                                             ps[0][:, 0:CHK]),
                         reads=[psb[0]], writes=[qkvT_b], tag=f"qkve{f}" + sfx)
                else:
                    P.op("dve", lambda e, f=f, s=s: e.tensor_copy(tcol(kvs[s], f // 2, CHK),
                                                                  ps[0][:, 0:CHK]),
                         reads=[psb[0]], writes=[kvs_b[s]], tag=f"qkvo{f}" + sfx)

            for t in range(s + 1):
                for m in range(8):
                    def zprod(e, m=m, t=t):
                        ch = CH["dve"]
                        ch(e, e.tensor_copy(kd[0:64, :], kvs[t][0:64, m * CHK:(m + 1) * CHK]))
                        ch(e, e.tensor_copy(kd[64:128, :], kvs[t][0:64, m * CHK:(m + 1) * CHK]))
                        return e.tensor_tensor(
                            zp[:, m * CHK:(m + 1) * CHK],
                            qkvT[:, m * CHK:(m + 1) * CHK], kd[:, :], ALU.mult)
                    P.op("dve", zprod, reads=[qkvT_b, kvs_b[t], kd_b],
                         writes=[zp_b, kd_b], tag=f"zp{m}_{t}" + sfx)
                for m in range(8):
                    P.op("pe", lambda e, m=m: e.matmul(
                        ps[1][0:16, 0:CHK], s_e16[:, m * 16:(m + 1) * 16],
                        zp[:, m * CHK:(m + 1) * CHK], start=(m == 0), stop=(m == 7)),
                        reads=[zp_b, s_e16_b], writes=[psb[1]], tag=f"zr{m}_{t}" + sfx)
                P.op("dve", lambda e, t=t: e.tensor_copy(zbuf[0:16, t * CHK:(t + 1) * CHK],
                                                         ps[1][0:16, 0:CHK]),
                     reads=[psb[1]], writes=[zbuf_b], tag=f"ze{t}" + sfx)

            def soft(e, s=s):
                ch = CH["dve"]
                ch(e, e.tensor_copy(zmx[0:16, :], zbuf[0:16, 0:CHK]))
                for t in range(1, s + 1):
                    ch(e, e.tensor_tensor(zmx[0:16, :], zmx[0:16, :],
                                          zbuf[0:16, t * CHK:(t + 1) * CHK], ALU.max))
                for t in range(s + 1):
                    ch(e, e.tensor_tensor(zbuf[0:16, t * CHK:(t + 1) * CHK],
                                          zbuf[0:16, t * CHK:(t + 1) * CHK],
                                          zmx[0:16, :], ALU.subtract))
                return e.engine_nop()
            P.op("dve", soft, reads=[zbuf_b, zmx_b], writes=[zbuf_b, zmx_b], tag="sft" + sfx)

            def expo(e, s=s):
                last = None
                for t in range(s + 1):
                    last = e.activation(zbuf[0:16, t * CHK:(t + 1) * CHK],
                                        zbuf[0:16, t * CHK:(t + 1) * CHK], AF.Exp)
                    if t < s:
                        CH["act"](e, last)
                return last
            P.op("act", expo, reads=[zbuf_b], writes=[zbuf_b], tag="exp" + sfx)

            def den(e, s=s):
                ch = CH["dve"]
                ch(e, e.tensor_copy(zmx[0:16, :], zbuf[0:16, 0:CHK]))
                for t in range(1, s + 1):
                    ch(e, e.tensor_tensor(zmx[0:16, :], zmx[0:16, :],
                                          zbuf[0:16, t * CHK:(t + 1) * CHK], ALU.add))
                ch(e, e.reciprocal(zmx[0:16, :], zmx[0:16, :]))
                for t in range(s + 1):
                    if t < s:
                        ch(e, e.tensor_tensor(wrows[0:16, t * CHK:(t + 1) * CHK],
                                              zbuf[0:16, t * CHK:(t + 1) * CHK],
                                              zmx[0:16, :], ALU.mult))
                    else:
                        return e.tensor_tensor(wrows[0:16, t * CHK:(t + 1) * CHK],
                                               zbuf[0:16, t * CHK:(t + 1) * CHK],
                                               zmx[0:16, :], ALU.mult)
            P.op("dve", den, reads=[zbuf_b, zmx_b, wrows_b], writes=[zmx_b, wrows_b],
                 tag="den" + sfx)

            for m in range(8):
                for t in range(s + 1):
                    P.op("pe", lambda e, m=m, t=t: e.matmul(
                        ps[2][:, 0:CHK], s_f16[0:16, m * 128:(m + 1) * 128],
                        wrows[0:16, t * CHK:(t + 1) * CHK], start=True, stop=True),
                        reads=[wrows_b, s_f16_b], writes=[psb[2]], tag=f"wb{m}{t}" + sfx)
                    for j in range(2):
                        def acc(e, m=m, j=j, t=t):
                            ch = CH["dve"]
                            ch(e, e.tensor_tensor(
                                tmpf[64 * j:64 * j + 64, 0:CHK],
                                ps[2][64 * j:64 * j + 64, 0:CHK],
                                kvs[t][64:128, m * CHK:(m + 1) * CHK], ALU.mult))
                            if t == 0:
                                return e.tensor_copy(aom[64 * j:64 * j + 64, 0:CHK],
                                                     tmpf[64 * j:64 * j + 64, 0:CHK])
                            return e.tensor_tensor(aom[64 * j:64 * j + 64, 0:CHK],
                                                   aom[64 * j:64 * j + 64, 0:CHK],
                                                   tmpf[64 * j:64 * j + 64, 0:CHK], ALU.add)
                        P.op("dve", acc, reads=[psb[2], kvs_b[t], tmpf_b, aom_b],
                             writes=[tmpf_b, aom_b], tag=f"ac{m}{j}{t}" + sfx)
                P.op("dve", lambda e, m=m: e.tensor_copy(tcol(zp, m, CHK), aom[:, 0:CHK]),
                     reads=[aom_b], writes=[zp_b], tag=f"aob{m}" + sfx)
                if debug and s == 0:
                    P.dma("sp", lambda e, m=m, r0=r0: e.dma_start(
                        out=dbg["attn"][m * 128:(m + 1) * 128, r0:r0 + CHK],
                        in_=aom[:, 0:CHK]),
                        reads=[aom_b], writes=[b_dbg["attn"]], tag=f"dba{m}" + sfx)


            for q in range(NSUB):
                qfx = sfx + f"q{q}"
                for half in range(2):
                    for cc in range(8):
                        P.op("pe", lambda e, q=q, half=half, cc=cc: e.matmul(
                            pd[0][:, half * 512:(half + 1) * 512],
                            zp[:, cc * CHK + q * 128:cc * CHK + q * 128 + 128],
                            projw[:, cc * D + half * 512:cc * D + (half + 1) * 512],
                            start=(cc == 0), stop=(cc == 7)),
                            reads=[zp_b, projw_b], writes=[pdb[0]], tag=f"pj{half}{cc}" + qfx)
                P.op("dve", lambda e, q=q: e.tensor_tensor(
                    haak[:, :], pd[0][:, 0:D], xnk[:, q * D:(q + 1) * D], ALU.add),
                    reads=[pdb[0], xnk_b], writes=[haak_b], tag="haa" + qfx)
                P.dma("pool", lambda e, q=q: e.indirect_dma_start(
                    pr_haa[:, :], bass.IndirectOffsetOnAxis(idxu[:, q:q + 1], 0),
                    haak[:, :], None, bounds_check=rbc_nh, oob_is_err=False),
                    reads=[haak_b, idxu_b], writes=[b_haa], tag="hsp" + qfx)

                def ln2(e):
                    ch = CH["dve"]
                    ch(e, e.reduce_sum(st8[:, 0:1], haak[:, :], axis=AX.X))
                    ch(e, e.tensor_scalar(st8[:, 0:1], st8[:, 0:1], 1.0 / D, None, ALU.mult))
                    ch(e, e.tensor_scalar(tmpf[:, 0:D], haak[:, :], st8[:, 0:1],
                                          None, ALU.subtract))
                    ch(e, e.tensor_tensor(scr4[:, :], tmpf[:, 0:D], tmpf[:, 0:D], ALU.mult))
                    ch(e, e.reduce_sum(st8[:, 1:2], scr4[:, :], axis=AX.X))
                    return e.tensor_scalar(st8[:, 1:2], st8[:, 1:2], 1.0 / D, None, ALU.mult)
                P.op("dve", ln2, reads=[haak_b, tmpf_b, scr4_b, st8_b],
                     writes=[tmpf_b, scr4_b, st8_b], tag="ln2" + qfx)
                P.op("act", lambda e: e.activation(st8[:, 2:3], st8[:, 1:2], AF.Sqrt,
                                                   bias=epscol[:, 0:1], scale=1.0),
                     reads=[st8_b, epscol_b], writes=[st8_b], tag="ln2s" + qfx)

                def c2(e):
                    ch = CH["dve"]
                    ch(e, e.reciprocal(st8[:, 3:4], st8[:, 2:3]))
                    ch(e, e.tensor_scalar(x2b[:, :], tmpf[:, 0:D], st8[:, 3:4],
                                          None, ALU.mult))
                    return e.tensor_scalar(tmpf[:, 0:D], tmpf[:, 0:D], st8[:, 3:4],
                                           None, ALU.mult)
                P.op("dve", c2, reads=[tmpf_b, st8_b], writes=[x2b_b, tmpf_b, st8_b],
                     tag="c2" + qfx)

                P.dma("pool", lambda e, q=q: e.indirect_dma_start(
                    sh_x2[:, :], bass.IndirectOffsetOnAxis(idxu[:, 4 + q:5 + q], 0),
                    x2b[:, :], None, bounds_check=rbc_x2, oob_is_err=False),
                    reads=[x2b_b, idxu_b], writes=[b_shx2], tag="wx2" + qfx)

                for f in range(8):
                    P.op("pe", lambda e, f=f: e.transpose(
                        pd[1][:, f * 128:(f + 1) * 128], tmpf[:, f * 128:(f + 1) * 128],
                        s_idf[:, :]),
                        reads=[tmpf_b, s_idf_b], writes=[pdb[1]], tag=f"tx2{f}" + qfx)
                for f in range(8):
                    P.op("act", lambda e, f=f: e.activation(
                        scr4[:, f * 128:(f + 1) * 128], pd[1][:, f * 128:(f + 1) * 128],
                        AF.Identity, bias=cols[:, C_LN2B + f:C_LN2B + f + 1],
                        scale=cols[:, C_LN2W + f:C_LN2W + f + 1]),
                        reads=[pdb[1], cols_b], writes=[scr4_b], tag=f"x2T{f}" + qfx)
                for cc in range(8):
                    P.op("pe", lambda e, cc=cc: e.matmul(
                        ps[1][:, 0:NEXP], scr4[:, cc * 128:(cc + 1) * 128],
                        routw[:, cc * 8:(cc + 1) * 8], start=(cc == 0), stop=(cc == 7)),
                        reads=[scr4_b, routw_b], writes=[psb[1]], tag=f"lg{cc}" + qfx)
                P.op("dve", lambda e, q=q: e.tensor_copy(lgrow[:, q * 8:(q + 1) * 8],
                                                         ps[1][:, 0:NEXP]),
                     reads=[psb[1]], writes=[lgrow_b], tag="lge" + qfx)
                if debug:
                    P.dma("sp", lambda e, q=q, r0=r0, s=s: e.dma_start(
                        out=dbg["logits"][s * NH + r0 + q * 128:s * NH + r0 + (q + 1) * 128, :],
                        in_=lgrow[:, q * 8:(q + 1) * 8]),
                        reads=[lgrow_b], writes=[b_dbg["logits"]], tag="dlg" + qfx)
                if debug and s == 0:
                    P.dma("sp", lambda e, q=q, r0=r0: e.dma_start(
                        out=dbg["haa"][r0 + q * 128:r0 + (q + 1) * 128, :], in_=haak[:, :]),
                        reads=[haak_b], writes=[b_dbg["haa"]], tag="dh" + qfx)
                    P.dma("sp", lambda e, q=q, r0=r0: e.dma_start(
                        out=dbg["xn"][r0 + q * 128:r0 + (q + 1) * 128, :],
                        in_=xnk[:, q * D:(q + 1) * D]),
                        reads=[xnk_b], writes=[b_dbg["xn"]], tag="dxn" + qfx)

                tix = c * NSUB + q
                zq = lgrow[:, q * 8:(q + 1) * 8]

                def top2(e, zq=zq):
                    ch = CH["dve"]
                    ch(e, e.reduce_max(st8[:, 0:1], zq, axis=AX.X))
                    ch(e, e.tensor_scalar(topsc[:, 0:8], zq, st8[:, 0:1], None, ALU.is_equal))
                    ch(e, e.scalar_tensor_tensor(topsc[:, 8:16], topsc[:, 0:8], -HUGE, zq,
                                                 ALU.mult, ALU.add))
                    ch(e, e.reduce_max(st8[:, 1:2], topsc[:, 8:16], axis=AX.X))
                    ch(e, e.tensor_scalar(topsc[:, 16:24], topsc[:, 8:16], st8[:, 1:2],
                                          None, ALU.is_equal))
                    ch(e, e.tensor_tensor(st8[:, 2:3], st8[:, 0:1], st8[:, 1:2], ALU.subtract))
                    ch(e, e.tensor_tensor(topsc[:, 24:32], topsc[:, 0:8], topsc[:, 16:24],
                                          ALU.add))
                    return e.tensor_copy(msb[:, 0:8], topsc[:, 24:32])
                P.op("dve", top2, reads=[lgrow_b, topsc_b, st8_b, msb_b],
                     writes=[topsc_b, st8_b, msb_b], tag="t2" + qfx)

                P.op("pe", lambda e: e.matmul(ps[1][0:1, 0:8], s_onescb[:, 0:1], msb[:, 0:8],
                                              start=True, stop=True),
                     reads=[msb_b, s_onescb_b], writes=[psb[1]], tag="cm" + qfx)
                first = (c == 0 and q == 0)
                P.op("dve", lambda e, s=s, first=first: e.tensor_copy(
                    cntrow[0:1, s * 8:(s + 1) * 8], ps[1][0:1, 0:8]) if first else
                    e.tensor_tensor(cntrow[0:1, s * 8:(s + 1) * 8],
                                    cntrow[0:1, s * 8:(s + 1) * 8], ps[1][0:1, 0:8], ALU.add),
                    reads=[psb[1]] + ([] if first else [cntrow_b]), writes=[cntrow_b],
                    tag="ca" + qfx)

                def ret(e, s=s, tix=tix):
                    ch = CH["dve"]
                    if s == 0:
                        ch(e, e.tensor_scalar(dzo[:, tix:tix + 1], st8[:, 2:3],
                                              selc[:, s:s + 1], None, ALU.mult))
                        ch(e, e.tensor_scalar(m1o[:, tix * 8:(tix + 1) * 8], topsc[:, 0:8],
                                              selc[:, s:s + 1], None, ALU.mult))
                        return e.tensor_scalar(m2o[:, tix * 8:(tix + 1) * 8], topsc[:, 16:24],
                                               selc[:, s:s + 1], None, ALU.mult)
                    ch(e, e.scalar_tensor_tensor(dzo[:, tix:tix + 1], st8[:, 2:3],
                                                 selc[:, s:s + 1], dzo[:, tix:tix + 1],
                                                 ALU.mult, ALU.add))
                    ch(e, e.scalar_tensor_tensor(m1o[:, tix * 8:(tix + 1) * 8], topsc[:, 0:8],
                                                 selc[:, s:s + 1], m1o[:, tix * 8:(tix + 1) * 8],
                                                 ALU.mult, ALU.add))
                    return e.scalar_tensor_tensor(m2o[:, tix * 8:(tix + 1) * 8], topsc[:, 16:24],
                                                  selc[:, s:s + 1], m2o[:, tix * 8:(tix + 1) * 8],
                                                  ALU.mult, ALU.add)
                P.op("dve", ret, reads=[st8_b, topsc_b, selc_b, dzo_b, m1o_b, m2o_b],
                     writes=[dzo_b, m1o_b, m2o_b], tag="ret" + qfx)


    if debug:
        P.dma("sp", lambda e: e.dma_start(out=dbg["cnt"][0:1, :], in_=cntrow[0:1, :]),
              reads=[cntrow_b], writes=[b_dbg["cnt"]], tag="dcnt")

    def wcnt(e):
        hv = vload(e, hint[0:1, 0:1], 0, 1)
        return e.dma_start(out=sh_cnt[bass.ts(hv, 1), :], in_=cntrow[0:1, :])
    P.dma("pool", wcnt, reads=[cntrow_b, hint_b], writes=[b_shcnt], tag="wcnt")

    barrier(1, [b_shcnt, b_shx2, b_haa, b_norm, b_x, b_xn], tag="B1")

    def finish(mini=True):
        if mini:
            mini_tail()
        P.op("pool", lambda e: e.engine_nop(), reads=[b_out], tag="fence")
        stream = P.build_streams(nc)
        with nc.Block() as blk:
            blk.tensor(stream("pe"))
            blk.scalar(stream("act"))
            blk.vector(stream("dve"))
            blk.gpsimd(stream("pool"))
            blk.sync(stream("sp"))
        return nc, dbg

    def mini_tail():
        ar.mark()
        oacx, oacx_b = ar.alloc(128, D, F32, "oacx")
        xre2, xre2_b = ar.alloc(128, D, F32, "xre2")
        hre2, hre2_b = ar.alloc(128, D, F32, "hre2")
        for t in range(16):
            r0 = t * 128
            P.dma("sp", lambda e, r0=r0: e.dma_start(out=xre2[:, :], in_=pr_x[r0:r0 + 128, :]),
                  reads=[b_x, btok[0]], writes=[xre2_b], tag=f"tx{t}")
            P.dma("sp", lambda e, r0=r0: e.dma_start(out=hre2[:, :], in_=pr_haa[r0:r0 + 128, :]),
                  reads=[b_haa, btok[0]], writes=[hre2_b], tag=f"th{t}")
            P.op("dve", lambda e: e.tensor_tensor(oacx[:, :], xre2[:, :], hre2[:, :], ALU.add),
                 reads=[xre2_b, hre2_b, oacx_b], writes=[oacx_b], tag=f"ta{t}")
            P.dma("sp", lambda e, r0=r0: e.dma_start(out=out[r0:r0 + 128, :], in_=oacx[:, :]),
                  reads=[oacx_b], writes=[b_out.sub(r0 * D * 4, (r0 + 128) * D * 4)],
                  tag=f"to{t}")

    if PH <= 1:
        return finish()

    ar.release()
    ar.mark()
    cA, cA_b = ar.alloc(1, 32, F32, "cA")
    cB, cB_b = ar.alloc(1, 32, F32, "cB")
    cG, cG_b = ar.alloc(1, 32, F32, "cG")
    rows8, rows8_b = ar.alloc(1, 64, F32, "rows8")
    colsc, colsc_b = ar.alloc(8, 8, F32, "colsc")
    msum2, msum2_b = ar.alloc(128, 8, BF16, "msum2")
    totrow, totrow_b = ar.alloc(1, 128, F32, "totrow")
    totcol, totcol_b = ar.alloc(128, 1, F32, "totcol")
    totmask, totmask_b = ar.alloc(128, 8, BF16, "totmask")
    basefm, basefm_b = ar.alloc(8, 16, F32, "basefm")
    posw, posw_b = ar.alloc(8, 6 * 128, F32, "posw")
    mkT, mkT_b = ar.alloc(8, 2 * 128, F32, "mkT")
    srow, srow_b = ar.alloc(1, 128, F32, "srow")
    scol, scol_b = ar.alloc(128, 4, F32, "scol")
    scatv, scatv_b = ar.alloc(128, 1, F32, "scatv")
    hpcol, hpcol_b = ar.alloc(128, 1, F32, "hpcol")
    sidxu2, sidxu2_b = ar.alloc(128, 2, U32, "sidxu2")
    p1all, p1all_b = ar.alloc(128, 16, F32, "p1all")

    csum, csum_b = ar.alloc(1, 64, F32, "csum")

    cpol = nc.alloc_semaphore("cpol")

    def ldcnt(e):
        e.tensor_copy(csum[0:1, 63:64], tot16k[0:1, 0:1])
        with e.register("ndc") as ndr, e.register("ctg") as tgt:
            e.reg_mov(ndr, 1)
            e.reg_mov(tgt, 0)
            with e.While(ndr):
                e.dma_start(out=cA[0:1, :], in_=sh_cnt[0:1, :]).then_inc(cpol, 16)
                e.dma_start(out=cB[0:1, :], in_=sh_cnt[1:2, :]).then_inc(cpol, 16)
                e.reg_add(tgt, tgt, 32)
                e.wait_ge(cpol, tgt)
                e.tensor_tensor(csum[0:1, 0:16], cA[0:1, 0:16], cA[0:1, 16:32], ALU.add)
                e.tensor_tensor(csum[0:1, 32:48], cB[0:1, 0:16], cB[0:1, 16:32], ALU.add)
                e.tensor_tensor(csum[0:1, 0:8], csum[0:1, 0:8], csum[0:1, 8:16], ALU.add)
                e.tensor_tensor(csum[0:1, 32:40], csum[0:1, 32:40], csum[0:1, 40:48], ALU.add)
                e.tensor_tensor(csum[0:1, 0:4], csum[0:1, 0:4], csum[0:1, 4:8], ALU.add)
                e.tensor_tensor(csum[0:1, 32:36], csum[0:1, 32:36], csum[0:1, 36:40], ALU.add)
                e.tensor_tensor(csum[0:1, 0:2], csum[0:1, 0:2], csum[0:1, 2:4], ALU.add)
                e.tensor_tensor(csum[0:1, 32:34], csum[0:1, 32:34], csum[0:1, 34:36], ALU.add)
                e.tensor_tensor(csum[0:1, 0:1], csum[0:1, 0:1], csum[0:1, 1:2], ALU.add)
                e.tensor_tensor(csum[0:1, 32:33], csum[0:1, 32:33], csum[0:1, 33:34], ALU.add)
                e.tensor_tensor(csum[0:1, 0:1], csum[0:1, 0:1], tot16k[0:1, 0:1], ALU.subtract)
                e.tensor_tensor(csum[0:1, 32:33], csum[0:1, 32:33], tot16k[0:1, 0:1], ALU.subtract)
                e.tensor_tensor(csum[0:1, 0:1], csum[0:1, 0:1], csum[0:1, 0:1], ALU.mult)
                e.tensor_tensor(csum[0:1, 32:33], csum[0:1, 32:33], csum[0:1, 32:33], ALU.mult)
                e.tensor_tensor(csum[0:1, 63:64], csum[0:1, 0:1], csum[0:1, 32:33], ALU.add)
                nd = e.value_load(csum[0:1, 63:64].bitcast(I32))
                e.reg_mov(ndr, nd)
        return e.engine_nop()
    P.op("pool", ldcnt, reads=[b_shcnt, btok[0], tot16k_b, cA_b, cB_b, csum_b],
         writes=[cA_b, cB_b, csum_b], tag="ldcnt")
    P.op("dve", lambda e: e.tensor_tensor(cG[0:1, :], cA[0:1, :], cB[0:1, :], ALU.add),
         reads=[cA_b, cB_b], writes=[cG_b], tag="cG")
    if debug:
        P.dma("sp", lambda e: e.dma_start(out=dbg["ngi"][0:1, 8:40], in_=cB[0:1, :]),
              reads=[cB_b], writes=[b_dbg["ngi"]], tag="dcB")

    def mkrows(e):
        ch = CH["dve"]
        ch(e, e.tensor_scalar(rows8[0:1, 0:8], cG[0:1, 0:8], selr[0:1, 4:5], None, ALU.mult))
        for s in range(1, 4):
            ch(e, e.scalar_tensor_tensor(rows8[0:1, 0:8], cG[0:1, 8 * s:8 * s + 8],
                                         selr[0:1, 4 + s:5 + s], rows8[0:1, 0:8],
                                         ALU.mult, ALU.add))
        ch(e, e.tensor_scalar(rows8[0:1, 8:16], cA[0:1, 0:8], selr[0:1, 0:1], None, ALU.mult))
        for s in range(1, 4):
            ch(e, e.scalar_tensor_tensor(rows8[0:1, 8:16], cA[0:1, 8 * s:8 * s + 8],
                                         selr[0:1, s:s + 1], rows8[0:1, 8:16],
                                         ALU.mult, ALU.add))
        ch(e, e.tensor_scalar(rows8[0:1, 8:16], rows8[0:1, 8:16], selr[0:1, 8:9],
                              None, ALU.mult))
        ch(e, e.tensor_scalar(rows8[0:1, 16:24], cG[0:1, 0:8], selr[0:1, 0:1], None, ALU.mult))
        for s in range(1, 4):
            ch(e, e.scalar_tensor_tensor(rows8[0:1, 16:24], cG[0:1, 8 * s:8 * s + 8],
                                         selr[0:1, s:s + 1], rows8[0:1, 16:24],
                                         ALU.mult, ALU.add))
        ch(e, e.tensor_scalar(rows8[0:1, 24:28], rows8[0:1, 16:20], selr[0:1, 8:9], None,
                              ALU.mult))
        ch(e, e.tensor_tensor(rows8[0:1, 28:32], rows8[0:1, 16:20], rows8[0:1, 24:28],
                              ALU.subtract))
        ch(e, e.scalar_tensor_tensor(rows8[0:1, 28:32], rows8[0:1, 20:24], selr[0:1, 8:9],
                                     rows8[0:1, 28:32], ALU.mult, ALU.add))
        ch(e, e.tensor_scalar(rows8[0:1, 32:36], rows8[0:1, 28:32], 0.0, None, ALU.is_gt))
        for g in range(1, NGROUP):
            ch(e, e.tensor_scalar(rows8[0:1, 36:40], rows8[0:1, 28:32], float(512 * g),
                                  None, ALU.is_gt))
            ch(e, e.tensor_tensor(rows8[0:1, 32:36], rows8[0:1, 32:36], rows8[0:1, 36:40],
                                  ALU.add))
        return e.tensor_copy(ngi[0:1, 0:4], rows8[0:1, 32:36])
    P.op("dve", mkrows, reads=[cG_b, cA_b, selr_b, rows8_b, ngi_b],
         writes=[rows8_b, ngi_b], tag="mkrows")

    for (cix, off) in [(0, 0), (1, 8)]:
        P.op("pe", lambda e, cix=cix, off=off: e.matmul(
            ps[0][0:8, 0:1], rows8[0:1, off:off + 8], s_ones1f[0:1, 0:1],
            start=True, stop=True),
            reads=[rows8_b, s_ones1f_b], writes=[psb[0]], tag=f"pcol{cix}")
        P.op("dve", lambda e, cix=cix: e.tensor_copy(colsc[0:8, cix:cix + 1],
                                                     ps[0][0:8, 0:1]),
             reads=[psb[0]], writes=[colsc_b], tag=f"pcole{cix}")
    P.op("pe", lambda e: e.matmul(ps[0][:, 0:1], s_ones1f[0:1, 0:128], selr[0:1, 10:11],
                                  start=True, stop=True),
         reads=[selr_b, s_ones1f_b], writes=[psb[0]], tag="hpc")
    P.op("dve", lambda e: e.tensor_tensor(hpcol[:, 0:1], ps[0][:, 0:1], iotacol[:, 0:1],
                                          ALU.add),
         reads=[psb[0], iotacol_b], writes=[hpcol_b], tag="hpce")
    P.op("act", lambda e: e.activation(p1all[:, :], dzo[:, :], AF.Sigmoid),
         reads=[dzo_b], writes=[p1all_b], tag="p1")

    msall, msall_b = ar.alloc(128, 16 * 8, BF16, "msall")

    for t in range(16):
        tfx = f"_t{t}"
        P.op("dve", lambda e, t=t: e.tensor_tensor(msall[:, t * 8:(t + 1) * 8],
                                                   m1o[:, t * 8:(t + 1) * 8],
                                                   m2o[:, t * 8:(t + 1) * 8], ALU.add),
             reads=[m1o_b, m2o_b], writes=[msall_b], tag="ms" + tfx)
        P.op("pe", lambda e, t=t: e.matmul(ps[1][0:1, 0:8], s_onescb[:, 0:1],
                                           msall[:, t * 8:(t + 1) * 8],
                                           start=True, stop=True),
             reads=[msall_b, s_onescb_b], writes=[psb[1]], tag="tot" + tfx)
        P.op("dve", lambda e, t=t: e.tensor_copy(totrow[0:1, t * 8:(t + 1) * 8],
                                                 ps[1][0:1, 0:8]),
             reads=[psb[1]], writes=[totrow_b], tag="tote" + tfx)
    P.op("pe", lambda e: e.matmul(ps[1][:, 0:1], totrow[0:1, :], s_ones1f[0:1, 0:1],
                                  start=True, stop=True),
         reads=[totrow_b, s_ones1f_b], writes=[psb[1]], tag="totc")
    P.op("dve", lambda e: e.tensor_copy(totcol[:, 0:1], ps[1][:, 0:1]),
         reads=[psb[1]], writes=[totcol_b], tag="totce")
    P.op("dve", lambda e: e.tensor_scalar(totmask[:, 0:8], s_ohe[:, :], totcol[:, 0:1],
                                          None, ALU.mult),
         reads=[s_ohe_b, totcol_b], writes=[totmask_b], tag="tm")
    P.op("pe", lambda e: e.matmul(ps[1][0:8, 0:16], totmask[:, 0:8], s_t1[:, :],
                                  start=True, stop=True),
         reads=[totmask_b, s_t1_b], writes=[psb[1]], tag="bfm")
    P.op("dve", lambda e: e.tensor_copy(basefm[0:8, :], ps[1][0:8, 0:16]),
         reads=[psb[1]], writes=[basefm_b], tag="bfme")

    for t in range(16):
        tfx = f"_t{t}"
        P.op("pe", lambda e, t=t: e.matmul(ps[0][0:8, 0:128], msall[:, t * 8:(t + 1) * 8],
                                           s_l128[:, :], start=True, stop=True),
             reads=[msall_b, s_l128_b], writes=[psb[0]], tag="pfx" + tfx)

        def posops(e, t=t):
            ch = CH["dve"]
            ch(e, e.tensor_scalar(posw[0:8, 0:128], ps[0][0:8, 0:128],
                                  basefm[0:8, t:t + 1], None, ALU.add))
            ch(e, e.tensor_scalar(posw[0:8, 0:128], posw[0:8, 0:128],
                                  colsc[0:8, 1:2], None, ALU.add))
            ch(e, e.tensor_scalar(posw[0:8, 128:256], posw[0:8, 0:128],
                                  colsc[0:8, 0:1], None, ALU.add))
            ch(e, e.tensor_scalar(posw[0:8, 256:384], posw[0:8, 128:256], float(CAP),
                                  None, ALU.is_lt))
            ch(e, e.tensor_scalar(posw[0:8, 384:512], posw[0:8, 256:384], -HUGE, HUGE,
                                  ALU.mult, ALU.add))
            ch(e, e.tensor_scalar(posw[0:8, 512:640], posw[0:8, 0:128],
                                  s_estr[0:8, 0:1], None, ALU.add))
            return e.tensor_tensor(posw[0:8, 512:640], posw[0:8, 512:640],
                                   posw[0:8, 384:512], ALU.add)
        P.op("dve", posops, reads=[psb[0], basefm_b, colsc_b, s_estr_b, posw_b],
             writes=[posw_b], tag="pos" + tfx)
        P.op("dve", lambda e, t=t: e.tensor_scalar(scatv[:, 0:1], hpcol[:, 0:1],
                                                   float(t * 128 + 1), None, ALU.add),
             reads=[hpcol_b], writes=[scatv_b], tag="sv" + tfx)
        for kk, (mo, mo_b) in enumerate([(m1o, m1o_b), (m2o, m2o_b)]):
            kfx = tfx + f"k{kk}"
            P.op("pe", lambda e, mo=mo, t=t: e.transpose(
                pst_b[0:8, 0:128], mo[:, t * 8:(t + 1) * 8], s_idb[:, :]),
                reads=[mo_b, s_idb_b], writes=[psb[3]], tag="mkt" + kfx)
            P.op("dve", lambda e, kk=kk: e.tensor_copy(mkT[0:8, kk * 128:(kk + 1) * 128],
                                                       pst_b[0:8, 0:128]),
                 reads=[psb[3]], writes=[mkT_b], tag="mkte" + kfx)
            P.op("dve", lambda e, kk=kk: e.tensor_tensor(
                posw[0:8, 640:768], mkT[0:8, kk * 128:(kk + 1) * 128],
                posw[0:8, 512:640], ALU.mult),
                reads=[mkT_b, posw_b], writes=[posw_b], tag="rx" + kfx)
            P.op("pe", lambda e: e.matmul(ps[2][0:1, 0:128], s_ones8f[0:8, 0:1],
                                          posw[0:8, 640:768], start=True, stop=True),
                 reads=[posw_b, s_ones8f_b], writes=[psb[2]], tag="rs" + kfx)
            P.op("dve", lambda e: e.tensor_copy(srow[0:1, :], ps[2][0:1, 0:128]),
                 reads=[psb[2]], writes=[srow_b], tag="rse" + kfx)
            P.op("pe", lambda e: e.matmul(ps[2][:, 0:1], srow[0:1, :], s_ones1f[0:1, 0:1],
                                          start=True, stop=True),
                 reads=[srow_b, s_ones1f_b], writes=[psb[2]], tag="rc" + kfx)

            def scmake(e, kk=kk, t=t):
                ch = CH["dve"]
                ch(e, e.tensor_copy(scol[:, 0:1], ps[2][:, 0:1]))
                ch(e, e.tensor_copy(sidxu2[:, kk:kk + 1], scol[:, 0:1]))
                ch(e, e.tensor_scalar(scol[:, 1:2], scol[:, 0:1], float(ZROW), None,
                                      ALU.min))
                return e.tensor_copy(gidx_all[:, (t * 2 + kk):(t * 2 + kk) + 1],
                                     scol[:, 1:2])
            P.op("dve", scmake, reads=[psb[2], scol_b, sidxu2_b, gidx_all_b],
                 writes=[scol_b, sidxu2_b, gidx_all_b], tag="sc" + kfx)
            P.dma("pool", lambda e, kk=kk: e.indirect_dma_start(
                sh_S[:, :], bass.IndirectOffsetOnAxis(sidxu2[:, kk:kk + 1], 0),
                scatv[:, 0:1], None, bounds_check=rbc_S, oob_is_err=False),
                reads=[scatv_b, sidxu2_b, btok[0]], writes=[b_shS], tag="ssc" + kfx)
            P.op("dve", lambda e, kk=kk: e.tensor_tensor(
                posw[0:8, 640:768], mkT[0:8, kk * 128:(kk + 1) * 128],
                posw[0:8, 256:384], ALU.mult),
                reads=[mkT_b, posw_b], writes=[posw_b], tag="kx" + kfx)
            P.op("pe", lambda e: e.matmul(ps[2][0:1, 0:128], s_ones8f[0:8, 0:1],
                                          posw[0:8, 640:768], start=True, stop=True),
                 reads=[posw_b, s_ones8f_b], writes=[psb[2]], tag="ks" + kfx)
            P.op("dve", lambda e: e.tensor_copy(srow[0:1, :], ps[2][0:1, 0:128]),
                 reads=[psb[2]], writes=[srow_b], tag="kse" + kfx)
            P.op("pe", lambda e: e.matmul(ps[2][:, 0:1], srow[0:1, :], s_ones1f[0:1, 0:1],
                                          start=True, stop=True),
                 reads=[srow_b, s_ones1f_b], writes=[psb[2]], tag="kc" + kfx)

            def pkmake(e, kk=kk, t=t):
                ch = CH["dve"]
                if kk == 0:
                    ch(e, e.tensor_tensor(scol[:, 2:3], ps[2][:, 0:1], p1all[:, t:t + 1],
                                          ALU.mult))
                else:
                    ch(e, e.tensor_scalar(scol[:, 3:4], p1all[:, t:t + 1], -1.0, 1.0,
                                          ALU.mult, ALU.add))
                    ch(e, e.tensor_tensor(scol[:, 2:3], ps[2][:, 0:1], scol[:, 3:4],
                                          ALU.mult))
                return e.tensor_copy(pk_all[:, (t * 2 + kk):(t * 2 + kk) + 1], scol[:, 2:3])
            P.op("dve", pkmake, reads=[psb[2], p1all_b, scol_b, pk_all_b],
                 writes=[scol_b, pk_all_b], tag="pk" + kfx)

    if debug:
        P.dma("sp", lambda e: e.dma_start(out=dbg["gidx"][:, :], in_=gidx_all[:, :]),
              reads=[gidx_all_b], writes=[b_dbg["gidx"]], tag="dgix")
        P.dma("sp", lambda e: e.dma_start(out=dbg["pk"][:, :], in_=pk_all[:, :]),
              reads=[pk_all_b], writes=[b_dbg["pk"]], tag="dpk")

    if PH <= 2:
        return finish()
    ar.release()
    ACH = 512
    NACH = NH // ACH
    ar.mark()
    normch, normch_b = ar.alloc(128, 8 * ACH, BF16, "normch")
    lqAll, lqAll_b = ar.alloc(128, NH, BF16, "lqAll")
    lkvm, lkvm_b = ar.alloc(128, HID, F32, "lkvm")
    pNcol, pNcol_b = ar.alloc(128, 1, F32, "pNcol")
    h128c, h128c_b = ar.alloc(128, 1, F32, "h128c")
    nidxu, nidxu_b = ar.alloc(128, 8, U32, "nidxu")

    P.op("pe", lambda e: e.matmul(ps[0][:, 0:1], s_ones1f[0:1, 0:128], selr[0:1, 12:13],
                                  start=True, stop=True),
         reads=[selr_b, s_ones1f_b], writes=[psb[0]], tag="pN")
    P.op("dve", lambda e: e.tensor_copy(pNcol[:, 0:1], ps[0][:, 0:1]),
         reads=[psb[0]], writes=[pNcol_b], tag="pNe")

    def mk_h128(e):
        ch = CH["dve"]
        ch(e, e.tensor_scalar(h128c[:, 0:1], hp8k[:, 0:1], 1.0 / 64.0, None, ALU.mult))
        return e.tensor_tensor(h128c[:, 0:1], h128c[:, 0:1], iotacol[:, 0:1], ALU.add)
    P.op("dve", mk_h128, reads=[hp8k_b, iotacol_b, h128c_b], writes=[h128c_b], tag="h128")

    def mk_nidx(e):
        ch = CH["dve"]
        for f in range(8):
            ch(e, e.tensor_tensor(idxc2[:, f:f + 1], pNcol[:, 0:1], iotacol[:, 0:1], ALU.add))
            ch(e, e.tensor_scalar(idxc2[:, f:f + 1], idxc2[:, f:f + 1], float(f * 128),
                                  None, ALU.add))
        return e.tensor_copy(nidxu[:, 0:8], idxc2[:, 0:8])
    idxc2, idxc2_b = ar.alloc(128, 8, F32, "idxc2")
    P.op("dve", mk_nidx, reads=[pNcol_b, iotacol_b, idxc2_b, nidxu_b],
         writes=[idxc2_b, nidxu_b], tag="nidx")

    ar.mark()
    whv, whv_b = ar.alloc(128, 8 * HID, BF16, "whv")
    whg, whg_b = ar.alloc(128, 8 * HID, BF16, "whg")
    wqkb, wqkb_b = ar.alloc(128, 8 * QKD, BF16, "wqkb")
    bhv16, bhv16_b = ar.alloc(1, HID, BF16, "bhv16")
    s1row, s1row_b = ar.alloc(1, 128, BF16, "s1row")
    vtm, vtm_b = ar.alloc(128, NSUB * HID, BF16, "vtm")
    qhead, qhead_b = ar.alloc(128, 5 * ACH, BF16, "qhead")
    attnT, attnT_b = ar.alloc(128, 2 * GSZ, BF16, "attnT")
    lkN, lkN_b = ar.alloc(128, 4 * 128, BF16, "lkN")
    gsc, gsc_b = ar.alloc(128, ACH, BF16, "gsc")

    for ct in range(8):
        P.dma("pool", lambda e, ct=ct: e.dma_start(
            out=whv[:, ct * HID:(ct + 1) * HID], in_=w_h[ct * 128:(ct + 1) * 128, 0:HID]),
            reads=[bI["w_h"]], writes=[whv_b], tag=f"ldwv{ct}")
        P.dma("pool", lambda e, ct=ct: e.dma_start(
            out=whg[:, ct * HID:(ct + 1) * HID], in_=w_h[ct * 128:(ct + 1) * 128, HID:2 * HID]),
            reads=[bI["w_h"]], writes=[whg_b], tag=f"ldwg{ct}")
        P.dma("pool", lambda e, ct=ct: e.dma_start(
            out=wqkb[:, ct * QKD:(ct + 1) * QKD], in_=w_qk[ct * 128:(ct + 1) * 128, :]),
            reads=[bI["w_qk"]], writes=[wqkb_b], tag=f"ldwqk{ct}")
    P.dma("pool", lambda e: e.dma_start(out=bhv16[0:1, :], in_=b_h[0:1, 0:HID]),
          reads=[bI["b_h"]], writes=[bhv16_b], tag="ldbhv")
    P.dma("sp", lambda e: e.dma_start(out=s1row[0:1, :], in_=c_onesr_b16[:, :]),
          reads=[bI["consts"]], writes=[s1row_b], tag="ld1b")

    for c in range(NACH):
        r0 = c * ACH
        afx = f"_a{c}"
        for f in range(8):
            P.dma("pool", lambda e, f=f, r0=r0: e.indirect_dma_start(
                normch[:, f * ACH:(f + 1) * ACH], None,
                pr_norm[:, :], bass.IndirectOffsetOnAxis(nidxu[:, f:f + 1], 0),
                element_offset=r0, bounds_check=rbc_nrm, oob_is_err=False),
                reads=[b_norm, nidxu_b, btok[0]], writes=[normch_b], tag=f"gnm{f}" + afx)

        for q in range(NSUB):
            for oc in range(4):
                for ct in range(8):
                    P.op("pe", lambda e, q=q, oc=oc, ct=ct: e.matmul(
                        ps[0][:, 0:512], normch[:, ct * ACH + q * 128:ct * ACH + q * 128 + 128],
                        whv[:, ct * HID + oc * 512:ct * HID + (oc + 1) * 512],
                        start=(ct == 0), stop=False),
                        reads=[normch_b, whv_b], writes=[psb[0]], tag=f"v{q}{oc}{ct}" + afx)
                P.op("pe", lambda e, oc=oc: e.matmul(
                    ps[0][:, 0:512], s1row[0:1, 0:128], bhv16[0:1, oc * 512:(oc + 1) * 512],
                    start=False, stop=True),
                    reads=[bhv16_b, s1row_b], writes=[psb[0]], tag=f"vb{q}{oc}" + afx)
                P.op("act", lambda e, q=q, oc=oc: e.activation(
                    vtm[:, q * HID + oc * 512:q * HID + (oc + 1) * 512], ps[0][:, 0:512],
                    AF.Silu),
                    reads=[psb[0]], writes=[vtm_b], tag=f"ve{q}{oc}" + afx)

        for f in range(16):
            for ct in range(8):
                P.op("pe", lambda e, f=f, ct=ct: e.matmul(
                    ps[1][:, 0:512], whg[:, ct * HID + f * 128:ct * HID + (f + 1) * 128],
                    normch[:, ct * ACH:ct * ACH + ACH], start=(ct == 0), stop=(ct == 7)),
                    reads=[normch_b, whg_b], writes=[psb[1]], tag=f"g{f}{ct}" + afx)
            P.op("act", lambda e, f=f: e.activation(
                gsc[:, 0:ACH], ps[1][:, 0:512], AF.Silu,
                bias=cols[:, C_BHG + f:C_BHG + f + 1]),
                reads=[psb[1], cols_b], writes=[gsc_b], tag=f"ge{f}" + afx)
            P.dma("sp", lambda e, f=f, r0=r0: e.dma_start(
                out=pr_gate[f * 128:(f + 1) * 128, r0:r0 + ACH], in_=gsc[:, 0:ACH]),
                reads=[gsc_b], writes=[b_gate], tag=f"gs{f}" + afx)

        for ct in range(8):
            P.op("pe", lambda e, ct=ct: e.matmul(
                ps[2][:, 0:512], wqkb[:, ct * QKD:(ct + 1) * QKD],
                normch[:, ct * ACH:ct * ACH + ACH], start=(ct == 0), stop=(ct == 7)),
                reads=[normch_b, wqkb_b], writes=[psb[2]], tag=f"qk{ct}" + afx)
        P.op("act", lambda e: e.activation(qhead[:, 0:ACH], ps[2][:, 0:512], AF.Silu,
                                           bias=cols[:, C_BQK:C_BQK + 1]),
             reads=[psb[2], cols_b], writes=[qhead_b], tag="qke" + afx)
        for hh in range(4):
            P.op("dve", lambda e, hh=hh: e.tensor_scalar(
                qhead[:, (1 + hh) * ACH:(2 + hh) * ACH], qhead[:, 0:ACH],
                cols[:, C_G0 + hh:C_G0 + hh + 1], cols[:, C_BT0 + hh:C_BT0 + hh + 1],
                ALU.mult, ALU.add),
                reads=[qhead_b, cols_b], writes=[qhead_b], tag=f"hd{hh}" + afx)

        for g in range(2):
            gt0 = g * 256
            for jt in range(2):
                P.op("pe", lambda e, g=g, jt=jt: e.matmul(
                    ps[3][:, 0:256], s_idb[:, :], biasT[:, jt * 256:(jt + 1) * 256],
                    start=True, stop=False),
                    reads=[biasT_b, s_idb_b], writes=[psb[3]], tag=f"sb{g}{jt}" + afx)
                P.op("pe", lambda e, g=g, jt=jt: e.matmul(
                    ps[3][:, 0:256], qhead[:, 3 * ACH + gt0 + jt * 128:3 * ACH + gt0 + jt * 128 + 128],
                    qhead[:, 1 * ACH + gt0:1 * ACH + gt0 + 256], start=False, stop=True),
                    reads=[qhead_b], writes=[psb[3]], tag=f"sm{g}{jt}" + afx)

                def relusq(e, jt=jt):
                    ch = CH["dve"]
                    ch(e, e.tensor_scalar(gsc[:, 0:256], ps[3][:, 0:256], 0.0, None, ALU.max))
                    return e.tensor_tensor(attnT[:, jt * 256:(jt + 1) * 256],
                                           gsc[:, 0:256], gsc[:, 0:256], ALU.mult)
                P.op("dve", relusq, reads=[psb[3], gsc_b, attnT_b],
                     writes=[gsc_b, attnT_b], tag=f"rq{g}{jt}" + afx)
            for f in range(16):
                for jt in range(2):
                    P.op("pe", lambda e, f=f, g=g, jt=jt: e.matmul(
                        ps[3][:, 256:512],
                        vtm[:, (2 * g + jt) * HID + f * 128:(2 * g + jt) * HID + (f + 1) * 128],
                        attnT[:, jt * 256:(jt + 1) * 256], start=(jt == 0), stop=(jt == 1)),
                        reads=[vtm_b, attnT_b], writes=[psb[3]], tag=f"qo{f}{g}{jt}" + afx)
                P.op("dve", lambda e, f=f: e.tensor_copy(gsc[:, 0:256], ps[3][:, 256:512]),
                     reads=[psb[3]], writes=[gsc_b], tag=f"qoe{f}{g}" + afx)
                P.dma("sp", lambda e, f=f, g=g, r0=r0: e.dma_start(
                    out=pr_quad[f * 128:(f + 1) * 128, r0 + g * 256:r0 + (g + 1) * 256],
                    in_=gsc[:, 0:256]),
                    reads=[gsc_b], writes=[b_quad], tag=f"qs{f}{g}" + afx)

        P.op("dve", lambda e, r0=r0: e.tensor_copy(lqAll[:, r0:r0 + ACH],
                                                   qhead[:, 2 * ACH:3 * ACH]),
             reads=[qhead_b], writes=[lqAll_b], tag="lq" + afx)
        for q in range(NSUB):
            P.op("pe", lambda e, q=q: e.transpose(
                pst_b[:, 0:128], qhead[:, 4 * ACH + q * 128:4 * ACH + (q + 1) * 128],
                s_idb[:, :]),
                reads=[qhead_b, s_idb_b], writes=[psb[3]], tag=f"lkT{q}" + afx)
            P.op("dve", lambda e, q=q: e.tensor_copy(lkN[:, q * 128:(q + 1) * 128],
                                                     pst_b[:, 0:128]),
                 reads=[psb[3]], writes=[lkN_b], tag=f"lke{q}" + afx)
        for q in range(NSUB):
            for oc in range(4):
                tgt = pd[0] if oc < 2 else pd[1]
                tb = pdb[0] if oc < 2 else pdb[1]
                off = (oc % 2) * 512
                P.op("pe", lambda e, q=q, oc=oc, tgt=tgt, off=off, c=c: e.matmul(
                    tgt[:, off:off + 512], lkN[:, q * 128:(q + 1) * 128],
                    vtm[:, q * HID + oc * 512:q * HID + (oc + 1) * 512],
                    start=(c == 0 and q == 0), stop=(c == NACH - 1 and q == NSUB - 1)),
                    reads=[lkN_b, vtm_b], writes=[tb], tag=f"lkv{q}{oc}" + afx)

    for half in range(2):
        P.op("dve", lambda e, half=half: e.tensor_copy(
            lkvm[:, half * 1024:(half + 1) * 1024], pd[half][:, 0:1024]),
            reads=[pdb[half]], writes=[lkvm_b], tag=f"lkve{half}")
    lkvi, lkvi_b = ar.alloc(128, 1, U32, "lkvi")
    P.op("dve", lambda e: e.tensor_copy(lkvi[:, 0:1], h128c[:, 0:1]),
         reads=[h128c_b], writes=[lkvi_b], tag="lkvi")
    P.dma("pool", lambda e: e.indirect_dma_start(
        sh_linkv.rearrange("a b c -> (a b) c")[:, :],
        bass.IndirectOffsetOnAxis(lkvi[:, 0:1], 0), lkvm[:, :], None,
        bounds_check=rbc_lkv, oob_is_err=False),
        reads=[lkvm_b, lkvi_b], writes=[b_shlinkv], tag="wlkv")

    if debug:
        P.dma("sp", lambda e: e.dma_start(out=dbg["linkv"][:, :], in_=lkvm[:, :]),
              reads=[lkvm_b], writes=[b_dbg["linkv"]], tag="dlkv")

    barrier(2, [b_shS, b_shlinkv], tag="B2")
    if PH <= 3:
        return finish()

    ar.release()
    ar.mark()
    wob, wob_b = ar.alloc(128, 16 * D, BF16, "wob")
    lkvf, lkvf_b = ar.alloc(128, HID, F32, "lkvf")
    lkvb2, lkvb2_b = ar.alloc(128, HID, BF16, "lkvb2")
    qre, qre_b = ar.alloc(128, 16 * 512, BF16, "qre")
    gre, gre_b = ar.alloc(128, 16 * 512, BF16, "gre")
    Gch, Gch_b = ar.alloc(128, 16 * 512, BF16, "Gch")
    Aev, Aev_b = ar.alloc(128, D, F32, "Aev")

    for ct in range(16):
        P.dma("pool", lambda e, ct=ct: e.dma_start(
            out=wob[:, ct * D:(ct + 1) * D], in_=w_o[ct * 128:(ct + 1) * 128, :]),
            reads=[bI["w_o"]], writes=[wob_b], tag=f"ldwo{ct}")
    P.dma("sp", lambda e: e.dma_start(out=lkvf[:, :],
                                      in_=sh_linkv[0, :, :]),
          reads=[b_shlinkv, btok[1]], writes=[lkvf_b], tag="lkv0")

    lkvp, lkvp_b = ar.alloc(128, HID, F32, "lkvp")
    P.dma("sp", lambda e: e.dma_start(out=lkvp[:, :], in_=sh_linkv[1, :, :]),
          reads=[b_shlinkv, btok[1]], writes=[lkvp_b], tag="lkv1")

    def lkvadd(e):
        ch = CH["dve"]
        ch(e, e.tensor_tensor(lkvf[:, :], lkvf[:, :], lkvp[:, :], ALU.add))
        return e.tensor_copy(lkvb2[:, :], lkvf[:, :])
    P.op("dve", lkvadd, reads=[lkvf_b, lkvp_b, lkvb2_b], writes=[lkvf_b, lkvb2_b],
         tag="lkvadd")

    for c in range(NACH):
        r0 = c * ACH
        afx = f"_p5{c}"
        for f in range(16):
            P.dma("sp", lambda e, f=f, r0=r0: e.dma_start(
                out=qre[:, f * 512:(f + 1) * 512],
                in_=pr_quad[f * 128:(f + 1) * 128, r0:r0 + ACH]),
                reads=[b_quad], writes=[qre_b], tag=f"qr{f}" + afx)
            P.dma("sp", lambda e, f=f, r0=r0: e.dma_start(
                out=gre[:, f * 512:(f + 1) * 512],
                in_=pr_gate[f * 128:(f + 1) * 128, r0:r0 + ACH]),
                reads=[b_gate], writes=[gre_b], tag=f"gr{f}" + afx)
        for f in range(16):
            P.op("pe", lambda e, f=f, r0=r0: e.matmul(
                ps[0][:, 0:512], lkvb2[:, f * 128:(f + 1) * 128], lqAll[:, r0:r0 + ACH],
                start=True, stop=True),
                reads=[lkvb2_b, lqAll_b], writes=[psb[0]], tag=f"lo{f}" + afx)

            def mkG(e, f=f):
                ch = CH["dve"]
                ch(e, e.tensor_tensor(Gch[:, f * 512:(f + 1) * 512], ps[0][:, 0:512],
                                      qre[:, f * 512:(f + 1) * 512], ALU.add))
                return e.tensor_tensor(Gch[:, f * 512:(f + 1) * 512],
                                       Gch[:, f * 512:(f + 1) * 512],
                                       gre[:, f * 512:(f + 1) * 512], ALU.mult)
            P.op("dve", mkG, reads=[psb[0], qre_b, gre_b, Gch_b], writes=[Gch_b],
                 tag=f"G{f}" + afx)
        for q in range(NSUB):
            for half in range(2):
                for ct in range(16):
                    P.op("pe", lambda e, q=q, half=half, ct=ct: e.matmul(
                        pd[0][:, half * 512:(half + 1) * 512],
                        Gch[:, ct * 512 + q * 128:ct * 512 + q * 128 + 128],
                        wob[:, ct * D + half * 512:ct * D + (half + 1) * 512],
                        start=(ct == 0), stop=(ct == 15)),
                        reads=[Gch_b, wob_b], writes=[pdb[0]], tag=f"A{q}{half}{ct}" + afx)
            P.op("dve", lambda e, q=q: e.tensor_copy(Aev[:, :], pd[0][:, 0:D]),
                 reads=[pdb[0]], writes=[Aev_b], tag=f"Ae{q}" + afx)
            P.dma("sp", lambda e, q=q, r0=r0: e.dma_start(
                out=pr_outp[r0 + q * 128:r0 + (q + 1) * 128, :], in_=Aev[:, :]),
                reads=[Aev_b], writes=[b_outp], tag=f"As{q}" + afx)

    if PH <= 4:
        return finish()
    ar.release()
    ar.release()
    ar.mark()
    w1sb, w1sb_b = ar.alloc(128, 8 * FFNH, BF16, "w1sb")
    w2sb2 = [ar.alloc(128, 32 * 512, BF16, f"w2sb{i}") for i in range(2)]
    actb, actb_b = ar.alloc(128, 32 * 512, BF16, "actb")
    x2g2 = [ar.alloc(128, D, BF16, f"x2g{i}") for i in range(2)]
    x2gT, x2gT_b = ar.alloc(128, 8 * 512, BF16, "x2gT")
    ysb2 = [ar.alloc(128, 512, F32, f"ysb{i}") for i in range(2)]
    Sg, Sg_b = ar.alloc(128, 16, F32, "Sg")
    h16c, h16c_b = ar.alloc(128, 1, F32, "h16c")
    eidx, eidx_b = ar.alloc(128, 8, F32, "eidx")
    eidxu, eidxu_b = ar.alloc(128, 8, U32, "eidxu")

    P.op("pe", lambda e: e.matmul(ps[0][:, 0:1], s_ones1f[0:1, 0:128], selr[0:1, 14:15],
                                  start=True, stop=True),
         reads=[selr_b, s_ones1f_b], writes=[psb[0]], tag="h16")
    P.op("dve", lambda e: e.tensor_tensor(h16c[:, 0:1], ps[0][:, 0:1], iotacol[:, 0:1],
                                          ALU.add),
         reads=[psb[0], iotacol_b], writes=[h16c_b], tag="h16e")

    for j in range(4):
        jfx = f"_e{j}"
        for ct in range(8):
            P.dma("pool", lambda e, j=j, ct=ct: e.dma_start(
                out=w1sb[:, ct * FFNH:(ct + 1) * FFNH],
                in_=moe_w1[j, ct * 128:(ct + 1) * 128, :]),
                reads=[bI["moe_w1"]], writes=[w1sb_b], tag=f"lw1{ct}" + jfx)
        for half in range(2):
            w2s, w2s_b = w2sb2[half]
            for ct in range(32):
                P.dma("pool", lambda e, j=j, ct=ct, half=half, w2s=w2s: e.dma_start(
                    out=w2s[:, ct * 512:(ct + 1) * 512],
                    in_=moe_w2[j, ct * 128:(ct + 1) * 128,
                               half * 512:(half + 1) * 512]),
                    reads=[bI["moe_w2"]], writes=[w2s_b], tag=f"lw2{ct}{half}" + jfx)
        for grp in range(NGROUP):
            gfx = jfx + f"g{grp}"
            _m = os.environ.get("KP4MODE", "real")
            if _m == "real":
                _gth = grp + 1
            elif _m == "mix2":
                _gth = 0 if grp == 0 else 9999
            elif _m == "mix1":
                _gth = 0 if j == 0 else 9999
            elif _m == "one_skip":
                _gth = 9999 if (j == 0 and grp == 1) else 0
            elif _m == "one_take":
                _gth = 0 if (j == 0 and grp == 0) else 9999
            elif _m == "jreal":
                _gth = (grp + 1) if j == 0 else 0
            elif _m == "realskip1":
                _gth = 9999 if (j == 0 and grp == 7) else 0
            else:
                _gth = {"open": 0, "else": 9999}.get(_m, 0)
            with P.gate(ngi[0:1, j:j + 1], ngi_b, _gth, key=("ng", j)):
                def mkeidx(e, j=j, grp=grp):
                    for tt in range(4):
                        inst = e.tensor_scalar(eidx[:, tt:tt + 1], h16c[:, 0:1],
                                               float(j * ISL + grp * EG + tt * 128),
                                               None, ALU.add)
                    return inst
                P.op("dve", mkeidx, reads=[h16c_b, eidx_b],
                     writes=[eidx_b], tag="eix" + gfx)
                P.op("dve", lambda e: e.tensor_copy(eidxu[:, 0:4], eidx[:, 0:4]),
                     reads=[eidx_b, eidxu_b], writes=[eidxu_b], tag="eixc" + gfx)
                for tt in range(4):
                    P.dma("pool", lambda e, tt=tt: e.indirect_dma_start(
                        Sg[:, tt:tt + 1], None, sh_S[:, :],
                        bass.IndirectOffsetOnAxis(eidxu[:, tt:tt + 1], 0),
                        bounds_check=rbc_S, oob_is_err=False),
                        reads=[b_shS, eidxu_b, btok[1]], writes=[Sg_b], tag=f"gS{tt}" + gfx)
                def mktid1(e):
                    for tt in range(4):
                        e.tensor_scalar(Sg[:, 4 + tt:5 + tt], Sg[:, tt:tt + 1], 0.0,
                                        None, ALU.is_equal)
                    for tt in range(4):
                        inst = e.tensor_scalar(Sg[:, 8 + tt:9 + tt], Sg[:, tt:tt + 1],
                                               -1.0, None, ALU.add)
                    return inst
                P.op("dve", mktid1, reads=[Sg_b], writes=[Sg_b], tag="tid1" + gfx)

                def mktid2(e):
                    for tt in range(4):
                        inst = e.scalar_tensor_tensor(
                            Sg[:, 12 + tt:13 + tt], Sg[:, 4 + tt:5 + tt], HUGE,
                            Sg[:, 8 + tt:9 + tt], ALU.mult, ALU.add)
                    return inst
                P.op("dve", mktid2, reads=[Sg_b], writes=[Sg_b], tag="tid2" + gfx)
                P.op("dve", lambda e: e.tensor_copy(eidxu[:, 4:8], Sg[:, 12:16]),
                     reads=[Sg_b, eidxu_b], writes=[eidxu_b], tag="tid3" + gfx)
                for tt in range(4):
                    x2g, x2g_b = x2g2[tt % 2]
                    P.op("dve", lambda e, x2g=x2g: e.memset(x2g[:, :], 0.0),
                         writes=[x2g_b], tag=f"z{tt}" + gfx)
                    P.dma("pool", lambda e, tt=tt, x2g=x2g: e.indirect_dma_start(
                        x2g[:, :], None, sh_x2[:, :],
                        bass.IndirectOffsetOnAxis(eidxu[:, 4 + tt:5 + tt], 0),
                        bounds_check=rbc_x2, oob_is_err=False),
                        reads=[b_shx2, eidxu_b, x2g_b, btok[1]], writes=[x2g_b],
                        tag=f"gx{tt}" + gfx)
                    for f in range(8):
                        tb = pst_b if f % 2 == 0 else pdt_b
                        tbb = psb[3] if f % 2 == 0 else pdb[1]
                        P.op("pe", lambda e, f=f, tb=tb, x2g=x2g: e.transpose(
                            tb[:, 0:128], x2g[:, f * 128:(f + 1) * 128], s_idb[:, :]),
                            reads=[x2g_b, s_idb_b], writes=[tbb], tag=f"tg{tt}{f}" + gfx)
                        P.op("act", lambda e, f=f, tt=tt, tb=tb: e.activation(
                            x2gT[:, f * 512 + tt * 128:f * 512 + tt * 128 + 128],
                            tb[:, 0:128], AF.Identity,
                            bias=cols[:, C_LN2B + f:C_LN2B + f + 1],
                            scale=cols[:, C_LN2W + f:C_LN2W + f + 1]),
                            reads=[tbb, cols_b], writes=[x2gT_b], tag=f"tge{tt}{f}" + gfx)
                for f4 in range(32):
                    apk = f4 % 2 * 2
                    for ct in range(8):
                        P.op("pe", lambda e, f4=f4, ct=ct, apk=apk: e.matmul(
                            ps[apk][:, 0:512],
                            w1sb[:, ct * FFNH + f4 * 128:ct * FFNH + (f4 + 1) * 128],
                            x2gT[:, ct * 512:(ct + 1) * 512],
                            start=(ct == 0), stop=(ct == 7)),
                            reads=[w1sb_b, x2gT_b], writes=[psb[apk]], tag=f"a{f4}{ct}" + gfx)
                    P.op("act", lambda e, f4=f4, apk=apk: e.activation(
                        actb[:, f4 * 512:(f4 + 1) * 512], ps[apk][:, 0:512], AF.Gelu),
                        reads=[psb[apk]], writes=[actb_b], tag=f"ag{f4}" + gfx)
                for half in range(2):
                    w2s, w2s_b = w2sb2[half]
                    for tt in range(4):
                        ypb = ps[1][:, 0:512] if tt % 2 == 0 else pd[0][:, 0:512]
                        ypbb = psb[1] if tt % 2 == 0 else pdb[0]
                        for ct in range(32):
                            P.op("pe", lambda e, tt=tt, ct=ct, ypb=ypb, w2s=w2s: e.matmul(
                                ypb,
                                actb[:, ct * 512 + tt * 128:ct * 512 + tt * 128 + 128],
                                w2s[:, ct * 512:(ct + 1) * 512],
                                start=(ct == 0), stop=(ct == 31)),
                                reads=[actb_b, w2s_b], writes=[ypbb], tag=f"y{tt}{ct}{half}" + gfx)
                        ysb, ysb_b = ysb2[tt % 2]
                        P.op("dve", lambda e, ypb=ypb, ysb=ysb: e.tensor_copy(
                            ysb[:, 0:512], ypb),
                            reads=[ypbb], writes=[ysb_b], tag=f"ye{tt}{half}" + gfx)

                        P.op("dve", lambda e, j=j, grp=grp, tt=tt: e.tensor_scalar(
                            eidx[:, 4:5], h16c[:, 0:1],
                            float(j * ISL + grp * EG + tt * 128), None, ALU.add),
                            reads=[h16c_b, eidx_b], writes=[eidx_b],
                            tag=f"ywi{tt}{half}" + gfx)
                        P.op("dve", lambda e: e.tensor_copy(eidxu[:, 4:5], eidx[:, 4:5]),
                             reads=[eidx_b, eidxu_b], writes=[eidxu_b],
                             tag=f"ywc{tt}{half}" + gfx)
                        P.dma("pool", lambda e, half=half, ysb=ysb: e.indirect_dma_start(
                            (sh_y1 if half == 0 else sh_y2)[:, :],
                            bass.IndirectOffsetOnAxis(eidxu[:, 4:5], 0),
                            ysb[:, 0:512], None, bounds_check=rbc_y, oob_is_err=False),
                            reads=[ysb_b, eidxu_b],
                            writes=[b_shy1 if half == 0 else b_shy2],
                            tag=f"yw{tt}{half}" + gfx)

    barrier(3, [b_shy1, b_shy2], tag="B3")

    if debug:
        ydmp, ydmp_b = ysb2[0]
        P.op("dve", lambda e: e.tensor_copy(ydmp[0:1, 0:8], ngi[0:1, 0:8]),
             reads=[ngi_b], writes=[ydmp_b], tag="dngi")
        P.dma("sp", lambda e: e.dma_start(out=dbg["ngi"][0:1, 0:8], in_=ydmp[0:1, 0:8]),
              reads=[ydmp_b], writes=[b_dbg["ngi"]], tag="dngis")
        for blk in range(4 * ISL // 128):
            for nm, sh in [("y1", sh_y1), ("y2", sh_y2)]:
                P.dma("sp", lambda e, blk=blk, sh=sh: e.dma_start(
                    out=ydmp[:, :], in_=sh[blk * 128:(blk + 1) * 128, :]),
                    reads=[b_shy1, b_shy2, b_dbg["ngi"]], writes=[ydmp_b],
                    tag=f"yd{nm}{blk}")
                P.dma("sp", lambda e, blk=blk, nm=nm: e.dma_start(
                    out=dbg[nm][blk * 128:(blk + 1) * 128, :], in_=ydmp[:, :]),
                    reads=[ydmp_b], writes=[b_dbg[nm]], tag=f"ye{nm}{blk}")

    if PH <= 5:
        return finish()

    ar.release()
    ar.mark()
    oach, oach_b = ar.alloc(128, D, F32, "oach")
    xrel, xrel_b = ar.alloc(128, D, F32, "xrel")
    hrel, hrel_b = ar.alloc(128, D, F32, "hrel")
    nrel, nrel_b = ar.alloc(128, D, F32, "nrel")
    arel, arel_b = ar.alloc(128, D, F32, "arel")
    y1t, y1t_b = ar.alloc(128, D, F32, "y1t")
    y2t, y2t_b = ar.alloc(128, D, F32, "y2t")
    mte, mte_b = ar.alloc(128, D, F32, "mte")
    gix, gix_b = ar.alloc(128, 2, U32, "gix")
    bobc, bobc_b = ar.alloc(128, D, F32, "bobc")
    ld_row(b_o, bI["b_o"], D, "bo")
    bcast_row(bobc, bobc_b, rowbuf[0:1, 0:D], rowbuf_b.sub(0, D * 4), D, "bo")

    for t in range(16):
        r0 = t * 128
        tfx = f"_o{t}"
        P.dma("sp", lambda e, r0=r0: e.dma_start(out=xrel[:, :], in_=pr_x[r0:r0 + 128, :]),
              reads=[b_x, btok[0]], writes=[xrel_b], tag="ox" + tfx)
        P.dma("sp", lambda e, r0=r0: e.dma_start(out=hrel[:, :], in_=pr_haa[r0:r0 + 128, :]),
              reads=[b_haa, btok[0]], writes=[hrel_b], tag="oh" + tfx)
        P.dma("sp", lambda e, r0=r0: e.dma_start(out=nrel[:, :], in_=pr_xn[r0:r0 + 128, :]),
              reads=[b_xn, btok[0]], writes=[nrel_b], tag="on" + tfx)
        P.dma("sp", lambda e, r0=r0: e.dma_start(out=arel[:, :], in_=pr_outp[r0:r0 + 128, :]),
              reads=[b_outp], writes=[arel_b], tag="oa" + tfx)
        P.op("dve", lambda e, t=t: e.tensor_copy(gix[:, 0:2], gidx_all[:, 2 * t:2 * t + 2]),
             reads=[gidx_all_b, gix_b], writes=[gix_b], tag="og" + tfx)
        for kk, (yt, yt_b) in enumerate([(y1t, y1t_b), (y2t, y2t_b)]):
            P.op("dve", lambda e, yt=yt: e.memset(yt[:, :], 0.0), writes=[yt_b],
                 tag=f"yz{kk}" + tfx)
            for half in range(2):
                P.dma("pool", lambda e, kk=kk, yt=yt, half=half: e.indirect_dma_start(
                    yt[:, half * 512:(half + 1) * 512], None,
                    (sh_y1 if half == 0 else sh_y2)[:, :],
                    bass.IndirectOffsetOnAxis(gix[:, kk:kk + 1], 0),
                    bounds_check=rbc_y, oob_is_err=False),
                    reads=[b_shy1 if half == 0 else b_shy2, gix_b, yt_b, btok[2]],
                    writes=[yt_b], tag=f"yg{kk}{half}" + tfx)

        def comb(e, t=t):
            ch = CH["dve"]
            ch(e, e.tensor_scalar(mte[:, :], y1t[:, :], pk_all[:, 2 * t:2 * t + 1],
                                  None, ALU.mult))
            ch(e, e.scalar_tensor_tensor(mte[:, :], y2t[:, :],
                                         pk_all[:, 2 * t + 1:2 * t + 2], mte[:, :],
                                         ALU.mult, ALU.add))
            ch(e, e.tensor_tensor(oach[:, :], xrel[:, :], hrel[:, :], ALU.add))
            ch(e, e.tensor_tensor(oach[:, :], oach[:, :], nrel[:, :], ALU.add))
            ch(e, e.tensor_tensor(oach[:, :], oach[:, :], arel[:, :], ALU.add))
            ch(e, e.tensor_tensor(oach[:, :], oach[:, :], bobc[:, :], ALU.add))
            return e.tensor_tensor(oach[:, :], oach[:, :], mte[:, :], ALU.add)
        P.op("dve", comb,
             reads=[y1t_b, y2t_b, pk_all_b, xrel_b, hrel_b, nrel_b, arel_b, bobc_b,
                    mte_b, oach_b],
             writes=[mte_b, oach_b], tag="oc" + tfx)
        P.dma("sp", lambda e, r0=r0: e.dma_start(out=out[r0:r0 + 128, :], in_=oach[:, :]),
              reads=[oach_b], writes=[b_out.sub(r0 * D * 4, (r0 + 128) * D * 4)],
              tag="os" + tfx)

    return finish(mini=False)


def _in_maps(inputs, rng_nonce):
    x = np.ascontiguousarray(np.asarray(inputs["L0_attn_x"], dtype=np.float32))
    qkv_wT = np.ascontiguousarray(np.asarray(inputs["L0_ffn_qkv_weight"], np.float32).T)
    proj_wT = np.ascontiguousarray(np.asarray(inputs["L0_ffn_proj_weight"], np.float32).T)
    w1 = np.ascontiguousarray(np.asarray(inputs["L0_ffn_moe_w1"], np.float32))
    w2 = np.ascontiguousarray(np.asarray(inputs["L0_ffn_moe_w2"], np.float32))

    def row(name, w):
        return np.asarray(inputs[name], np.float32).reshape(1, w)

    maps = []
    for core in range(8):
        p, h = core // 2, core % 2
        selv = np.zeros((1, 16), np.float32)
        selv[0, p] = 1.0
        selv[0, 4:4 + p] = 1.0
        selv[0, 8] = h
        selv[0, 9] = p
        selv[0, 10] = h * (B * NH) + p * NH
        selv[0, 11] = h * 4
        selv[0, 12] = p * HID
        selv[0, 14] = h * 4 * ISL
        m = {
            "xs": np.ascontiguousarray(x[:, h * NH:(h + 1) * NH, :]),
            "sel": selv,
            "nonce": rng_nonce,
            "norm_g": row("L0_attn_norm_g", D),
            "norm_b": row("L0_attn_norm_b", D),
            "w_h": np.ascontiguousarray(np.asarray(inputs["L0_attn_W_h"], np.float32)),
            "b_h": row("L0_attn_b_h", 2 * HID),
            "w_qk": np.ascontiguousarray(np.asarray(inputs["L0_attn_W_qk"], np.float32)),
            "b_qk": row("L0_attn_b_qk", QKD),
            "qk_gamma": np.ascontiguousarray(np.asarray(inputs["L0_attn_qk_gamma"], np.float32)),
            "qk_beta": np.ascontiguousarray(np.asarray(inputs["L0_attn_qk_beta"], np.float32)),
            "rel_emb": row("L0_attn_rel_emb", 32),
            "w_o": np.ascontiguousarray(np.asarray(inputs["L0_attn_W_o"], np.float32)),
            "b_o": row("L0_attn_b_o", D),
            "ln1_w": row("L0_ffn_ln1_weight", D),
            "ln1_b": row("L0_ffn_ln1_bias", D),
            "ln2_w": row("L0_ffn_ln2_weight", D),
            "ln2_b": row("L0_ffn_ln2_bias", D),
            "qkv_wT": qkv_wT,
            "proj_wT": proj_wT,
            "router_w": np.ascontiguousarray(np.asarray(inputs["L0_ffn_router_weight"], np.float32)),
            "moe_w1": np.ascontiguousarray(w1[4 * h:4 * h + 4]),
            "moe_w2": np.ascontiguousarray(w2[4 * h:4 * h + 4]),
        }
        maps.append(m)
    return maps


_CACHE = {}


def _build(debug=0):
    key = ("nc", debug)
    if key not in _CACHE:
        _CACHE[key] = build_program(debug=debug)
    return _CACHE[key]


def run_device(inputs, debug=0):
    nc, dbg = _build(debug)
    nonce_v = np.random.default_rng().standard_normal((1, 4)).astype(np.float32)
    maps = _in_maps(inputs, nonce_v)
    res = run_bass_kernel_spmd(nc, maps, core_ids=list(range(8)))
    return res


def kernel(**inputs):
    res = run_device(inputs, debug=0)
    outf = np.zeros((B, N, D), np.float32)
    for core in range(8):
        p, h = core // 2, core % 2
        outf[p, h * NH:(h + 1) * NH, :] = res.results[core]["out"]
    return outf


# version 05
# speedup vs baseline: 2.8088x; 1.0045x over previous
import math
import os

import numpy as np
import ml_dtypes

import concourse.bass as bass
import concourse.mybir as mybir
from concourse.bass_utils import run_bass_kernel_spmd

F32 = mybir.dt.float32
BF16 = mybir.dt.bfloat16
I32 = mybir.dt.int32
U32 = mybir.dt.uint32
U8 = mybir.dt.uint8
ALU = mybir.AluOpType
AF = mybir.ActivationFunctionType
AX = mybir.AxisListType
BF = ml_dtypes.bfloat16

B = 4
N = 4096
D = 1024
HID = 2048
QKD = 128
GSZ = 256
NEXP = 8
FFNH = 4096
CAP = 5120
NH = 2048
CHK = 512
NCHK = NH // CHK
NSUB = CHK // 128
EPS = 1e-5
ISL = 4096
EG = 512
ZROW = NEXP * ISL
NGROUP = ISL // EG
HUGE = 1.0e9

ENGINES = ("pe", "act", "dve", "pool", "sp")


class Buf:
    __slots__ = ("space", "lo", "hi", "name")

    def __init__(self, space, lo, hi, name=""):
        self.space, self.lo, self.hi, self.name = space, lo, hi, name

    def sub(self, lo_b, hi_b):
        return Buf(self.space, self.lo + lo_b, self.lo + hi_b, self.name)


class _Seg:
    __slots__ = ("lo", "hi", "writer", "readers")

    def __init__(self, lo, hi, writer, readers):
        self.lo, self.hi, self.writer, self.readers = lo, hi, writer, readers


class Op:
    __slots__ = ("idx", "engine", "emit", "reads", "writes", "is_dma", "gate",
                 "edges", "needs_inc", "rank", "dma_sem", "dma_target", "tag")

    def __init__(self, idx, engine, emit, reads, writes, is_dma, gate, tag):
        self.idx, self.engine, self.emit = idx, engine, emit
        self.reads, self.writes = list(reads), list(writes)
        self.is_dma, self.gate, self.tag = is_dma, gate, tag
        self.edges = {}
        self.needs_inc = False
        self.rank = None
        self.dma_sem = None
        self.dma_target = None


class Gate:
    __slots__ = ("gid", "cond_ap", "cond_buf", "thresh", "key")

    def __init__(self, gid, cond_ap, cond_buf, thresh, key):
        self.gid, self.cond_ap, self.cond_buf, self.thresh = gid, cond_ap, cond_buf, thresh
        self.key = key


class Prog:
    def __init__(self, n_dma_sems=20):
        self.ops = []
        self._cur_gate = None
        self._gid = 0
        self.n_dma = n_dma_sems
        self._rr = 0

    def op(self, engine, emit, reads=(), writes=(), tag=""):
        o = Op(len(self.ops), engine, emit, reads, writes, False, self._cur_gate, tag)
        self.ops.append(o)
        return o

    def dma(self, engine, emit, reads=(), writes=(), tag=""):
        o = Op(len(self.ops), engine, emit, reads, writes, True, self._cur_gate, tag)
        self.ops.append(o)
        return o

    class _G:
        def __init__(s, p, g):
            s.p, s.g = p, g

        def __enter__(s):
            assert s.p._cur_gate is None
            s.p._cur_gate = s.g
            return s.g

        def __exit__(s, *a):
            s.p._cur_gate = None
            return False

    def gate(self, cond_ap, cond_buf, thresh, key=None):
        g = Gate(self._gid, cond_ap, cond_buf, thresh,
                 key if key is not None else ("g", self._gid))
        self._gid += 1
        return Prog._G(self, g)

    def _edges(self):
        spaces = {}
        gate_first = set()
        for o in self.ops:
            if o.gate is not None:
                key = (o.gate.gid, o.engine)
                if key not in gate_first:
                    gate_first.add(key)
                    o.reads = o.reads + [o.gate.cond_buf]
        for o in self.ops:
            def add(j):
                if j is not None and j != o.idx:
                    o.edges[j] = True
            for b in o.reads:
                for s in spaces.setdefault(b.space, []):
                    if s.lo < b.hi and b.lo < s.hi:
                        add(s.writer)
                        s.readers.append(o.idx)
            for b in o.writes:
                ss = spaces.setdefault(b.space, [])
                new = []
                for s in ss:
                    if s.lo < b.hi and b.lo < s.hi:
                        add(s.writer)
                        for r in s.readers:
                            add(r)
                        if s.lo < b.lo:
                            new.append(_Seg(s.lo, b.lo, s.writer, list(s.readers)))
                        if b.hi < s.hi:
                            new.append(_Seg(b.hi, s.hi, s.writer, list(s.readers)))
                    else:
                        new.append(s)
                new.append(_Seg(b.lo, b.hi, o.idx, []))
                spaces[b.space] = new
        for o in self.ops:
            keep = {}
            for j in o.edges:
                pj = self.ops[j]
                if (not pj.is_dma) and (not o.is_dma) and pj.engine == "pe" \
                        and o.engine == "pe":
                    continue
                keep[j] = True
            o.edges = keep
            for j in keep:
                self.ops[j].needs_inc = True
        cnt = {e: 0 for e in ENGINES}
        dma_t = [0] * self.n_dma
        for o in self.ops:
            if o.is_dma:
                o.needs_inc = True
                o.dma_sem = self._rr
                self._rr = (self._rr + 1) % self.n_dma
                o.dma_target = dma_t[o.dma_sem] + 16
                dma_t[o.dma_sem] = o.dma_target
            elif o.needs_inc:
                cnt[o.engine] += 1
                o.rank = cnt[o.engine]
        self.final_dma = dma_t

    def build_streams(self, nc):
        self._edges()
        self.sem_eng = {e: nc.alloc_semaphore(f"se_{e}") for e in ENGINES}
        self.sem_dma = [nc.alloc_semaphore(f"sd_{i}") for i in range(self.n_dma)]

        def sem_of(key):
            return self.sem_eng[key[1]] if key[0] == "e" else self.sem_dma[key[1]]

        def waits_of(o):
            w = {}
            for j in o.edges:
                pj = self.ops[j]
                if pj.is_dma:
                    k, v = ("d", pj.dma_sem), pj.dma_target
                else:
                    if pj.rank is None:
                        continue
                    k, v = ("e", pj.engine), pj.rank
                if w.get(k, 0) < v:
                    w[k] = v
            if o.is_dma and o.dma_target > 16:
                k = ("d", o.dma_sem)
                if w.get(k, 0) < o.dma_target - 16:
                    w[k] = o.dma_target - 16
            return w

        def emit_one(eng, o, sat):
            for k, v in sorted(waits_of(o).items()):
                if sat.get(k, 0) < v:
                    eng.wait_ge(sem_of(k), v)
                    sat[k] = v
            inst = o.emit(eng)
            if o.is_dma:
                inst.then_inc(self.sem_dma[o.dma_sem], 16)
            elif o.rank is not None:
                inst.then_inc(self.sem_eng[o.engine], 1)

        def stream(ename):
            def run(eng):
                sat = {}
                cond_regs = {}
                ops = [o for o in self.ops if o.engine == ename]
                i = 0
                while i < len(ops):
                    o = ops[i]
                    if o.gate is None:
                        emit_one(eng, o, sat)
                        i += 1
                        continue
                    g = o.gate
                    group = []
                    while i < len(ops) and ops[i].gate is g:
                        group.append(ops[i])
                        i += 1
                    for k, v in sorted(waits_of(group[0]).items()):
                        if sat.get(k, 0) < v:
                            eng.wait_ge(sem_of(k), v)
                            sat[k] = v
                    if g.key in cond_regs:
                        greg = cond_regs[g.key]
                    else:
                        greg = eng.alloc_register(f"greg_{ename}_{len(cond_regs)}")
                        cond_regs[g.key] = greg
                        cv = eng.value_load(g.cond_ap)
                        eng.reg_mov(greg, cv)
                    with eng.If_cmp(greg, g.thresh, "IS_GE"):
                        gsat = dict(sat)
                        for oo in group:
                            emit_one(eng, oo, gsat)
                    with eng.Else():
                        incs = {}
                        first_t = {}
                        first_rank = None
                        for oo in group:
                            if oo.is_dma:
                                k = ("d", oo.dma_sem)
                                incs[k] = incs.get(k, 0) + 16
                                if k not in first_t:
                                    first_t[k] = oo.dma_target - 16
                            elif oo.rank is not None:
                                k = ("e", ename)
                                incs[k] = incs.get(k, 0) + 1
                                if first_rank is None:
                                    first_rank = oo.rank
                        for k, v in sorted(incs.items()):
                            if k[0] == "d" and first_t.get(k, 0) > 0:
                                eng.wait_ge(sem_of(k), first_t[k])
                            elif k[0] == "e" and first_rank and first_rank > 1:
                                eng.wait_ge(sem_of(k), first_rank - 1)
                            eng.sem_inc(sem_of(k), v)
            return run
        return stream


class Arena:
    def __init__(self, nc, name, part_bytes):
        self.t = nc.alloc_sbuf_tensor(name, [128, part_bytes], U8)
        self.ap_all = self.t.ap()
        self.space = "sb:" + name
        self.cap = part_bytes
        self.off = 0
        self.marks = []
        self.peak = 0

    def alloc(self, rows, cols, dt, name="", align=64):
        esz = mybir.dt.size(dt)
        off = (self.off + align - 1) // align * align
        sz = cols * esz
        assert off + sz <= self.cap, f"arena OOM at {name}: {off}+{sz} > {self.cap}"
        self.off = off + sz
        self.peak = max(self.peak, self.off)
        ap = self.ap_all[0:rows, off:off + sz].bitcast(dt)
        return ap, Buf(self.space, off, off + sz, name)

    def mark(self):
        self.marks.append(self.off)

    def release(self):
        self.off = self.marks.pop()


def _np_bf16(a):
    return np.ascontiguousarray(np.asarray(a).astype(BF))


def _rel_bias_masks():
    pos = np.arange(GSZ)
    n = -(pos[None, :] - pos[:, None])
    nb = 16
    ret = (n < 0).astype(np.int64) * nb
    n = np.abs(n)
    mx = nb // 2
    small = n < mx
    vl = mx + (np.log(np.maximum(n, 1) / mx) / math.log(128 / mx) * (nb - mx)).astype(np.int64)
    vl = np.minimum(vl, nb - 1)
    bucket = ret + np.where(small, n, vl)
    bt = bucket.T
    masks = np.zeros((32, GSZ, GSZ), np.float32)
    sc = QKD ** 0.5
    for k in range(32):
        masks[k] = (bt == k) * sc
    return _np_bf16(masks.reshape(32 * GSZ, GSZ))


def _l_strict128():
    return _np_bf16((np.arange(128)[:, None] < np.arange(128)[None, :]).astype(np.float32))


def _e16_const():
    e = np.zeros((128, 8 * 16), np.float32)
    for m in range(8):
        for j in range(2):
            e[j * 64:(j + 1) * 64, m * 16 + 2 * m + j] = 0.125
    return _np_bf16(e)


def _f16_const():
    f = np.zeros((16, 8 * 128), np.float32)
    for m in range(8):
        for j in range(2):
            f[2 * m + j, m * 128 + j * 64:m * 128 + (j + 1) * 64] = 1.0
    return _np_bf16(f)


def _t1_const():
    t1 = np.zeros((128, 16), np.float32)
    for tp in range(16):
        t1[tp * 8:(tp + 1) * 8, tp + 1:] = 1.0
    return _np_bf16(t1)


def _ohe_const():
    o = np.zeros((128, 8), np.float32)
    for t in range(16):
        for e in range(8):
            o[t * 8 + e, e] = 1.0
    return _np_bf16(o)


def build_program(debug=0):
    PH = int(os.environ.get("KPHASE", "6"))
    nc = bass.Bass("TRN2", target_bir_lowering=False, enable_asserts=False)

    def ein(name, shape, dt=F32):
        return nc.dram_tensor(name, list(shape), dt, kind="ExternalInput").ap()

    xs = ein("xs", (B, NH, D))
    sel = ein("sel", (1, 16))
    nonce = ein("nonce", (1, 4))
    norm_g = ein("norm_g", (1, D))
    norm_b = ein("norm_b", (1, D))
    w_h = ein("w_h", (D, 2 * HID))
    b_h = ein("b_h", (1, 2 * HID))
    w_qk = ein("w_qk", (D, QKD))
    b_qk = ein("b_qk", (1, QKD))
    qk_gamma = ein("qk_gamma", (4, QKD))
    qk_beta = ein("qk_beta", (4, QKD))
    rel_emb = ein("rel_emb", (1, 32))
    w_o = ein("w_o", (HID, D))
    b_o = ein("b_o", (1, D))
    ln1_w = ein("ln1_w", (1, D))
    ln1_b = ein("ln1_b", (1, D))
    ln2_w = ein("ln2_w", (1, D))
    ln2_b = ein("ln2_b", (1, D))
    qkv_wT = ein("qkv_wT", (D, HID))
    proj_wT = ein("proj_wT", (D, D))
    router_w = ein("router_w", (D, NEXP))
    moe_w1 = ein("moe_w1", (4, D, FFNH))
    moe_w2 = ein("moe_w2", (4, FFNH, D))

    out = nc.dram_tensor("out", [NH, D], F32, kind="ExternalOutput").ap()

    sh_flags = nc.dram_tensor("sh_flags", [1, 64], F32, kind="Internal", addr_space="Shared").ap()
    sh_cnt = nc.dram_tensor("sh_cnt", [2, 32], F32, kind="Internal", addr_space="Shared").ap()
    sh_x2 = nc.dram_tensor("sh_x2", [2 * B * NH, D], BF16, kind="Internal", addr_space="Shared").ap()
    sh_linkv = nc.dram_tensor("sh_linkv", [2, QKD, HID], F32, kind="Internal", addr_space="Shared").ap()
    sh_S = nc.dram_tensor("sh_S", [NEXP * ISL, 1], F32, kind="Internal", addr_space="Shared").ap()
    sh_y1 = nc.dram_tensor("sh_y1", [NEXP * ISL + 1, D // 2], F32, kind="Internal", addr_space="Shared").ap()
    sh_y2 = nc.dram_tensor("sh_y2", [NEXP * ISL + 1, D // 2], F32, kind="Internal", addr_space="Shared").ap()

    pr_haa = nc.dram_tensor("pr_haa", [NH, D], F32, kind="Internal").ap()
    pr_x = nc.dram_tensor("pr_x", [NH, D], F32, kind="Internal").ap()
    pr_xn = nc.dram_tensor("pr_xn", [NH, D], F32, kind="Internal").ap()
    pr_outp = nc.dram_tensor("pr_outp", [NH, D], F32, kind="Internal").ap()
    pr_norm = nc.dram_tensor("pr_norm", [4 * HID, NH], BF16, kind="Internal").ap()
    pr_gate = nc.dram_tensor("pr_gate", [HID, NH], BF16, kind="Internal").ap()
    pr_quad = nc.dram_tensor("pr_quad", [HID, NH], BF16, kind="Internal").ap()

    dbg = {}
    b_dbg = {}
    if debug:
        for nm, shape in [("xn", (NH, D)), ("attn", (D, NH)), ("haa", (NH, D)),
                          ("logits", (4 * NH, NEXP)), ("cnt", (1, 32)),
                          ("gidx", (128, 32)), ("pk", (128, 32)), ("linkv", (QKD, HID)),
                          ("y1", (4 * ISL, 512)), ("y2", (4 * ISL, 512)),
                          ("ngi", (1, 64))]:
            dbg[nm] = nc.dram_tensor("dbg_" + nm, list(shape), F32, kind="ExternalOutput").ap()
            b_dbg[nm] = Buf("dram:dbg_" + nm, 0, 1 << 40, nm)

    c_idb = nc.inline_tensor(np.eye(128, dtype=BF), "c_idb").ap()
    c_idf = nc.inline_tensor(np.eye(128, dtype=np.float32), "c_idf").ap()
    c_l128 = nc.inline_tensor(_l_strict128(), "c_l128").ap()
    c_relm = nc.inline_tensor(_rel_bias_masks(), "c_relm").ap()
    c_e16 = nc.inline_tensor(_e16_const(), "c_e16").ap()
    c_f16 = nc.inline_tensor(_f16_const(), "c_f16").ap()
    c_t1 = nc.inline_tensor(_t1_const(), "c_t1").ap()
    c_ohe = nc.inline_tensor(_ohe_const(), "c_ohe").ap()
    c_misc = nc.inline_tensor(np.concatenate([
        np.ones((1, 128), np.float32),
        np.arange(128, dtype=np.float32)[None, :],
    ], axis=0), "c_misc").ap()
    c_onescb = nc.inline_tensor(np.ones((128, 1), dtype=BF), "c_onescb").ap()
    c_iota8c = nc.inline_tensor(np.arange(8, dtype=np.float32)[:, None], "c_iota8c").ap()
    c_ones8f = nc.inline_tensor(np.ones((8, 1), np.float32), "c_ones8f").ap()
    c_onesr_b16 = nc.inline_tensor(np.ones((1, 128), dtype=BF), "c_onesr16").ap()
    c_estr = nc.inline_tensor((np.arange(8, dtype=np.float32) * ISL)[:, None], "c_estr").ap()

    P = Prog()
    ar = Arena(nc, "arena", 190 * 1024)

    ps = [nc.alloc_psum_tensor(f"ps{i}", [128, 512], F32).ap() for i in range(4)]
    psb = [Buf(f"psum{i}", 0, 2048) for i in range(4)]
    pd = [nc.alloc_psum_tensor(f"pd{i}", [128, 1024], F32).ap() for i in range(2)]
    pdb = [Buf(f"psumd{i}", 0, 4096) for i in range(2)]
    pst_b = ps[3].bitcast(BF16)
    pdt_b = pd[1].bitcast(BF16)

    def DB(name):
        return Buf("dram:" + name, 0, 1 << 40, name)

    bI = {nm: DB(nm) for nm in ["xs", "w_h", "w_qk", "w_o", "qkv_wT", "proj_wT",
                                "router_w", "moe_w1", "moe_w2", "b_h", "b_qk",
                                "qk_gamma", "qk_beta", "rel_emb", "b_o", "norm_g",
                                "norm_b", "ln1_w", "ln1_b", "ln2_w", "ln2_b",
                                "sel", "nonce", "consts"]}
    b_out = DB("out")
    b_haa = DB("pr_haa")
    b_x = DB("pr_x")
    b_xn = DB("pr_xn")
    b_outp = DB("pr_outp")
    b_norm = DB("pr_norm")
    b_gate = DB("pr_gate")
    b_quad = DB("pr_quad")
    b_shx2 = DB("sh_x2")
    b_shcnt = DB("sh_cnt")
    b_shlinkv = DB("sh_linkv")
    b_shS = DB("sh_S")
    b_shy1 = DB("sh_y1")
    b_shy2 = DB("sh_y2")
    b_flags = DB("sh_flags")
    btok = [DB(f"btok{i}") for i in range(4)]

    chsem = {e: nc.alloc_semaphore("ch_" + e) for e in ENGINES}
    chcnt = {e: 0 for e in ENGINES}

    def mkch(ename):
        def ch(e, inst):
            chcnt[ename] += 1
            inst.then_inc(chsem[ename], 1)
            e.wait_ge(chsem[ename], chcnt[ename])
            return inst
        return ch
    CH = {e: mkch(e) for e in ENGINES}

    def vload(e, ap, lo, hi):
        return nc.s_assert_within(e.value_load(ap), lo, hi, skip_runtime_assert=True)

    rbc_nh = nc.gpsimd.alloc_register("rbc_nh")
    rbc_x2 = nc.gpsimd.alloc_register("rbc_x2")
    rbc_S = nc.gpsimd.alloc_register("rbc_S")
    rbc_y = nc.gpsimd.alloc_register("rbc_y")
    rbc_nrm = nc.gpsimd.alloc_register("rbc_nrm")
    rbc_lkv = nc.gpsimd.alloc_register("rbc_lkv")

    def _set_rbc(e):
        e.reg_mov(rbc_nh, NH - 1)
        e.reg_mov(rbc_x2, 2 * B * NH - 1)
        e.reg_mov(rbc_S, NEXP * ISL - 1)
        e.reg_mov(rbc_nrm, 4 * HID - 1)
        e.reg_mov(rbc_lkv, 2 * QKD - 1)
        return e.reg_mov(rbc_y, NEXP * ISL)
    P.op("pool", _set_rbc, tag="rbc")

    selr, selr_b = ar.alloc(1, 16, F32, "selr")
    tot16k, tot16k_b = ar.alloc(1, 1, F32, "tot16k")
    noncer, noncer_b = ar.alloc(1, 4, F32, "noncer")
    NBAR = 4
    magic, magic_b = ar.alloc(1, 4 * (NBAR + 2), F32, "magic")
    hint, hint_b = ar.alloc(1, 4, I32, "hint")
    pint, pint_b = ar.alloc(1, 4, I32, "pint")
    barsc, barsc_b = ar.alloc(1, 32, F32, "barsc")
    cols, cols_b = ar.alloc(128, 80, F32, "cols")
    selc, selc_b = ar.alloc(128, 8, F32, "selc")
    routw, routw_b = ar.alloc(128, 8 * NEXP, F32, "routw")
    biasT, biasT_b = ar.alloc(128, 2 * GSZ, BF16, "biasT")
    m1o, m1o_b = ar.alloc(128, 16 * NEXP, BF16, "m1o")
    m2o, m2o_b = ar.alloc(128, 16 * NEXP, BF16, "m2o")
    dzo, dzo_b = ar.alloc(128, 16, F32, "dzo")
    cntrow, cntrow_b = ar.alloc(1, 32, F32, "cntrow")
    gidx_all, gidx_all_b = ar.alloc(128, 32, F32, "gidx_all")
    pk_all, pk_all_b = ar.alloc(128, 32, F32, "pk_all")
    ngi, ngi_b = ar.alloc(1, 8, I32, "ngi")
    P.op("dve", lambda e: e.memset(tot16k[0:1, 0:1], 16384.0),
         writes=[tot16k_b], tag="t16k")
    s_idb, s_idb_b = ar.alloc(128, 128, BF16, "s_idb")
    s_idf, s_idf_b = ar.alloc(128, 128, F32, "s_idf")
    s_l128, s_l128_b = ar.alloc(128, 128, BF16, "s_l128")
    s_e16, s_e16_b = ar.alloc(128, 8 * 16, BF16, "s_e16")
    s_f16, s_f16_b = ar.alloc(16, 8 * 128, BF16, "s_f16")
    s_t1, s_t1_b = ar.alloc(128, 16, BF16, "s_t1")
    s_ohe, s_ohe_b = ar.alloc(128, 8, BF16, "s_ohe")
    s_ones1f, s_ones1f_b = ar.alloc(1, 128, F32, "s_ones1f")
    s_iota1f, s_iota1f_b = ar.alloc(1, 128, F32, "s_iota1f")
    s_onescb, s_onescb_b = ar.alloc(128, 1, BF16, "s_onescb")
    s_iota8c, s_iota8c_b = ar.alloc(8, 1, F32, "s_iota8c")
    s_ones8f, s_ones8f_b = ar.alloc(8, 1, F32, "s_ones8f")
    s_estr, s_estr_b = ar.alloc(8, 1, F32, "s_estr")
    rowbuf, rowbuf_b = ar.alloc(1, D, F32, "rowbuf")

    P.dma("sp", lambda e: e.dma_start(out=selr[:, :], in_=sel[:, :]), reads=[bI["sel"]],
          writes=[selr_b], tag="ld_sel")
    P.dma("sp", lambda e: e.dma_start(out=noncer[:, :], in_=nonce[:, :]), reads=[bI["nonce"]],
          writes=[noncer_b], tag="ld_nonce")
    for apx, src, bwr in [(s_idb, c_idb, s_idb_b), (s_idf, c_idf, s_idf_b),
                          (s_l128, c_l128, s_l128_b), (s_t1, c_t1, s_t1_b),
                          (s_ohe, c_ohe, s_ohe_b), (s_onescb, c_onescb, s_onescb_b),
                          (s_iota8c, c_iota8c, s_iota8c_b), (s_estr, c_estr, s_estr_b),
                          (s_ones8f, c_ones8f, s_ones8f_b)]:
        P.dma("sp", lambda e, apx=apx, src=src: e.dma_start(out=apx[:, :], in_=src[:, :]),
              reads=[bI["consts"]], writes=[bwr], tag="ldc")
    P.dma("sp", lambda e: e.dma_start(out=s_e16[:, :], in_=c_e16[:, :]),
          reads=[bI["consts"]], writes=[s_e16_b], tag="ldc_e16")
    P.dma("sp", lambda e: e.dma_start(out=s_f16[:, :], in_=c_f16[:, :]),
          reads=[bI["consts"]], writes=[s_f16_b], tag="ldc_f16")
    P.dma("sp", lambda e: e.dma_start(out=s_ones1f[:, :], in_=c_misc[0:1, :]),
          reads=[bI["consts"]], writes=[s_ones1f_b], tag="ldc_ones")
    P.dma("sp", lambda e: e.dma_start(out=s_iota1f[:, :], in_=c_misc[1:2, :]),
          reads=[bI["consts"]], writes=[s_iota1f_b], tag="ldc_iota")

    def mk_magic(e):
        c = CH["dve"]
        for k in range(NBAR + 2):
            e.tensor_scalar(magic[0:1, 4 * k:4 * k + 4], noncer[0:1, 0:4], float(k),
                            None, ALU.add)
        e.tensor_copy(hint[0:1, 0:1], selr[0:1, 8:9])
        return e.tensor_copy(pint[0:1, 0:2], selr[0:1, 9:11])
    P.op("dve", mk_magic, reads=[noncer_b, selr_b], writes=[magic_b, hint_b, pint_b],
         tag="magics")

    barsem = [nc.alloc_semaphore(f"bsem{i}") for i in range(NBAR)]
    bpol = [nc.alloc_semaphore(f"bpol{i}") for i in range(NBAR)]

    def barrier(k, guards, tag=""):
        def emit(e):
            h = vload(e, hint[0:1, 0:1], 0, 1)
            e.dma_start(out=sh_flags[0:1, bass.ts(h, 4)],
                        in_=magic[0:1, 4 * k:4 * (k + 1)]).then_inc(barsem[k - 1], 16)
            e.wait_ge(barsem[k - 1], 16)
            e.tensor_copy(barsc[0:1, 24:25], tot16k[0:1, 0:1])
            a = barsc[0:1, 8:16]
            b2 = barsc[0:1, 16:24]
            with e.register(f"nd{k}") as ndr, e.register(f"bt{k}") as tgt:
                e.reg_mov(ndr, 1)
                e.reg_mov(tgt, 0)
                with e.While(ndr):
                    e.dma_start(out=barsc[0:1, 0:8], in_=sh_flags[0:1, 0:8]).then_inc(bpol[k - 1], 16)
                    e.reg_add(tgt, tgt, 16)
                    e.wait_ge(bpol[k - 1], tgt)
                    e.tensor_tensor(a[0:1, 0:4], barsc[0:1, 0:4],
                                    magic[0:1, 4 * k:4 * k + 4], ALU.subtract)
                    e.tensor_tensor(a[0:1, 4:8], barsc[0:1, 4:8],
                                    magic[0:1, 4 * k:4 * k + 4], ALU.subtract)
                    e.tensor_tensor(a[0:1, 0:8], a[0:1, 0:8], a[0:1, 0:8], ALU.mult)
                    e.tensor_tensor(b2[0:1, 0:4], barsc[0:1, 0:4],
                                    magic[0:1, 4 * (k + 1):4 * (k + 1) + 4], ALU.subtract)
                    e.tensor_tensor(b2[0:1, 4:8], barsc[0:1, 4:8],
                                    magic[0:1, 4 * (k + 1):4 * (k + 1) + 4], ALU.subtract)
                    e.tensor_tensor(b2[0:1, 0:8], b2[0:1, 0:8], b2[0:1, 0:8], ALU.mult)
                    e.tensor_tensor(a[0:1, 0:8], a[0:1, 0:8], b2[0:1, 0:8], ALU.mult)
                    e.tensor_tensor(a[0:1, 0:4], a[0:1, 0:4], a[0:1, 4:8], ALU.add)
                    e.tensor_tensor(a[0:1, 0:2], a[0:1, 0:2], a[0:1, 2:4], ALU.add)
                    e.tensor_tensor(barsc[0:1, 24:25], a[0:1, 0:1], a[0:1, 1:2], ALU.add)
                    nd = e.value_load(barsc[0:1, 24:25].bitcast(I32))
                    e.reg_mov(ndr, nd)
            return e.engine_nop()
        P.op("pool", emit, reads=list(guards) + [magic_b, hint_b, barsc_b, b_flags,
                                                 tot16k_b],
             writes=[btok[k - 1], barsc_b, b_flags], tag=tag or f"bar{k}")

    def colify(dst_col_ap, dst_buf, src_row_ap, src_buf, tag, scale=None):
        P.op("pe", lambda e: e.matmul(ps[0][:, 0:1], src_row_ap, s_ones1f[0:1, 0:1],
                                      start=True, stop=True),
             reads=[src_buf, s_ones1f_b], writes=[psb[0]], tag="cf_" + tag)
        if scale is None:
            P.op("dve", lambda e: e.tensor_copy(dst_col_ap, ps[0][:, 0:1]),
                 reads=[psb[0]], writes=[dst_buf], tag="cfe_" + tag)
        else:
            P.op("dve", lambda e: e.tensor_scalar(dst_col_ap, ps[0][:, 0:1], scale,
                                                  None, ALU.mult),
                 reads=[psb[0]], writes=[dst_buf], tag="cfe_" + tag)

    def bcast_row(dst_ap, dst_buf, src_row_ap, src_buf, width, tag):
        for f0 in range(0, width, 512):
            w = min(512, width - f0)
            P.op("pe", lambda e, f0=f0, w=w: e.matmul(
                ps[0][:, 0:w], s_ones1f[0:1, 0:128], src_row_ap[0:1, f0:f0 + w],
                start=True, stop=True),
                reads=[src_buf, s_ones1f_b], writes=[psb[0]], tag=f"bc_{tag}{f0}")
            P.op("dve", lambda e, f0=f0, w=w: e.tensor_copy(dst_ap[:, f0:f0 + w],
                                                            ps[0][:, 0:w]),
                 reads=[psb[0]], writes=[dst_buf], tag=f"bce_{tag}{f0}")

    def ld_row(src_ap, src_buf, w, tag):
        P.dma("sp", lambda e: e.dma_start(out=rowbuf[0:1, 0:w], in_=src_ap[:, :]),
              reads=[src_buf], writes=[rowbuf_b.sub(0, w * 4)], tag="ldr_" + tag)

    def cols_from_row(col0, nf, tag, off=0, scale=None):
        for i in range(nf):
            colify(cols[:, col0 + i:col0 + i + 1], cols_b.sub((col0 + i) * 4, (col0 + i + 1) * 4),
                   rowbuf[0:1, off + i * 128:off + (i + 1) * 128],
                   rowbuf_b.sub((off + i * 128) * 4, (off + (i + 1) * 128) * 4),
                   f"{tag}{i}", scale=scale)

    C_LN1W, C_LN1B, C_LN2W, C_LN2B, C_NG, C_NB, C_BQK = 0, 8, 16, 24, 32, 40, 48
    C_G0, C_BT0, C_BHG = 49, 53, 57

    ld_row(ln1_w, bI["ln1_w"], D, "ln1w"); cols_from_row(C_LN1W, 8, "ln1w")
    ld_row(ln1_b, bI["ln1_b"], D, "ln1b"); cols_from_row(C_LN1B, 8, "ln1b")
    ld_row(ln2_w, bI["ln2_w"], D, "ln2w"); cols_from_row(C_LN2W, 8, "ln2w")
    ld_row(ln2_b, bI["ln2_b"], D, "ln2b"); cols_from_row(C_LN2B, 8, "ln2b")
    ld_row(norm_g, bI["norm_g"], D, "ng"); cols_from_row(C_NG, 8, "ng")
    ld_row(norm_b, bI["norm_b"], D, "nb"); cols_from_row(C_NB, 8, "nb")
    ld_row(b_qk, bI["b_qk"], QKD, "bqk"); cols_from_row(C_BQK, 1, "bqk")
    gb_sc = {0: 1.0 / 16, 1: 1.0, 2: 1.0 / 16, 3: 1.0 / 4096}
    for hh in range(4):
        P.dma("sp", lambda e, hh=hh: e.dma_start(out=rowbuf[0:1, 0:QKD],
                                                 in_=qk_gamma[hh:hh + 1, :]),
              reads=[bI["qk_gamma"]], writes=[rowbuf_b.sub(0, QKD * 4)], tag=f"ldg{hh}")
        cols_from_row(C_G0 + hh, 1, f"g{hh}", scale=gb_sc[hh])
        P.dma("sp", lambda e, hh=hh: e.dma_start(out=rowbuf[0:1, 0:QKD],
                                                 in_=qk_beta[hh:hh + 1, :]),
              reads=[bI["qk_beta"]], writes=[rowbuf_b.sub(0, QKD * 4)], tag=f"ldbt{hh}")
        cols_from_row(C_BT0 + hh, 1, f"bt{hh}", scale=gb_sc[hh])
    for piece in range(2):
        P.dma("sp", lambda e, piece=piece: e.dma_start(
            out=rowbuf[0:1, 0:D], in_=b_h[0:1, HID + piece * D:HID + (piece + 1) * D]),
            reads=[bI["b_h"]], writes=[rowbuf_b], tag=f"ldbhg{piece}")
        cols_from_row(C_BHG + piece * 8, 8, f"bhg{piece}")
    P.op("pe", lambda e: e.matmul(ps[0][:, 0:8], s_ones1f[0:1, 0:128], selr[0:1, 0:8],
                                  start=True, stop=True),
         reads=[selr_b, s_ones1f_b], writes=[psb[0]], tag="selbc")
    P.op("dve", lambda e: e.tensor_copy(selc[:, 0:8], ps[0][:, 0:8]),
         reads=[psb[0]], writes=[selc_b], tag="selbc_e")
    iotacol, iotacol_b = ar.alloc(128, 1, F32, "iotacol")
    colify(iotacol[:, 0:1], iotacol_b, s_iota1f[0:1, 0:128], s_iota1f_b, "iota")
    epscol, epscol_b = ar.alloc(128, 1, F32, "epscol")
    P.op("dve", lambda e: e.memset(epscol[:, 0:1], EPS), writes=[epscol_b], tag="eps")
    hp8k, hp8k_b = ar.alloc(128, 1, F32, "hp8k")
    P.op("pe", lambda e: e.matmul(ps[0][:, 0:1], s_ones1f[0:1, 0:128], selr[0:1, 8:9],
                                  start=True, stop=True),
         reads=[selr_b, s_ones1f_b], writes=[psb[0]], tag="hbc")
    P.op("dve", lambda e: e.tensor_scalar(hp8k[:, 0:1], ps[0][:, 0:1], float(B * NH),
                                          None, ALU.mult),
         reads=[psb[0]], writes=[hp8k_b], tag="hbce")

    for f in range(8):
        P.dma("sp", lambda e, f=f: e.dma_start(out=routw[:, f * 8:(f + 1) * 8],
                                               in_=router_w[f * 128:(f + 1) * 128, :]),
              reads=[bI["router_w"]], writes=[routw_b.sub(f * 32, (f + 1) * 32)],
              tag=f"ld_rout{f}")

    ar.mark()
    relrow, relrow_b = ar.alloc(1, 32, F32, "relrow")
    relcols, relcols_b = ar.alloc(128, 32, F32, "relcols")
    mstage, mstage_b = ar.alloc(128, GSZ, BF16, "mstage")
    bacc, bacc_b = ar.alloc(128, 2 * GSZ, F32, "bacc")
    P.dma("sp", lambda e: e.dma_start(out=relrow[0:1, :], in_=rel_emb[:, :]),
          reads=[bI["rel_emb"]], writes=[relrow_b], tag="ld_rel")
    bcast_row(relcols, relcols_b, relrow[0:1, 0:32], relrow_b, 32, "rel")
    P.op("dve", lambda e: e.memset(bacc[:, :], 0.0), writes=[bacc_b], tag="bacc0")
    for k in range(32):
        for jt in range(2):
            P.dma("sp", lambda e, k=k, jt=jt: e.dma_start(
                out=mstage[:, :], in_=c_relm[(2 * k + jt) * 128:(2 * k + jt + 1) * 128, :]),
                reads=[bI["consts"]], writes=[mstage_b], tag=f"ldm{k}_{jt}")
            P.op("dve", lambda e, k=k, jt=jt: e.scalar_tensor_tensor(
                bacc[:, jt * GSZ:(jt + 1) * GSZ], mstage[:, :], relcols[:, k:k + 1],
                bacc[:, jt * GSZ:(jt + 1) * GSZ], ALU.mult, ALU.add),
                reads=[mstage_b, relcols_b, bacc_b.sub(jt * GSZ * 4, (jt + 1) * GSZ * 4)],
                writes=[bacc_b.sub(jt * GSZ * 4, (jt + 1) * GSZ * 4)], tag=f"bst{k}_{jt}")
    P.op("dve", lambda e: e.tensor_copy(biasT[:, :], bacc[:, :]),
         reads=[bacc_b], writes=[biasT_b], tag="biasT")
    ar.release()

    ar.mark()
    zS, zS_b = ar.alloc(128, 64, F32, "zS")
    zrowz, zrowz_b = ar.alloc(1, D, F32, "zrowz")
    P.op("dve", lambda e: e.memset(zrowz[0:1, :], 0.0), writes=[zrowz_b], tag="zr0")
    P.dma("sp", lambda e: e.dma_start(out=sh_y1[ZROW:ZROW + 1, :], in_=zrowz[0:1, 0:D // 2]),
          reads=[zrowz_b], writes=[b_shy1.sub(ZROW * 2048, (ZROW + 1) * 2048)], tag="y_z1")
    P.dma("sp", lambda e: e.dma_start(out=sh_y2[ZROW:ZROW + 1, :], in_=zrowz[0:1, 0:D // 2]),
          reads=[zrowz_b], writes=[b_shy2.sub(ZROW * 2048, (ZROW + 1) * 2048)], tag="y_z2")
    P.op("dve", lambda e: e.memset(zS[:, :], 0.0), writes=[zS_b], tag="zS")
    for i in range(4):
        P.dma("sp", lambda e, i=i: e.dma_start(
            out=sh_S[i * 8192:(i + 1) * 8192, 0:1].rearrange("(p g) one -> p (g one)", p=128),
            in_=zS[:, :]),
            reads=[zS_b], writes=[b_shS.sub(i * 8192 * 4, (i + 1) * 8192 * 4)], tag=f"zS{i}")
    ar.release()

    ar.mark()
    qkvw, qkvw_b = ar.alloc(128, 8 * HID, BF16, "qkvw")
    projw, projw_b = ar.alloc(128, 8 * D, BF16, "projw")
    for f in range(8):
        P.dma("pool", lambda e, f=f: e.dma_start(out=qkvw[:, f * HID:(f + 1) * HID],
                                                 in_=qkv_wT[f * 128:(f + 1) * 128, :]),
              reads=[bI["qkv_wT"]], writes=[qkvw_b.sub(f * HID * 2, (f + 1) * HID * 2)],
              tag=f"ld_qkvw{f}")
        P.dma("pool", lambda e, f=f: e.dma_start(out=projw[:, f * D:(f + 1) * D],
                                                 in_=proj_wT[f * 128:(f + 1) * 128, :]),
              reads=[bI["proj_wT"]], writes=[projw_b.sub(f * D * 2, (f + 1) * D * 2)],
              tag=f"ld_projw{f}")

    xck, xck_b = ar.alloc(128, NSUB * D, F32, "xck")
    xnk, xnk_b = ar.alloc(128, NSUB * D, F32, "xnk")
    scr4, scr4_b = ar.alloc(128, D, F32, "scr4")
    st8, st8_b = ar.alloc(128, 8, F32, "st8")
    ln1T, ln1T_b = ar.alloc(128, 8 * CHK, BF16, "ln1T")
    qkvT, qkvT_b = ar.alloc(128, 8 * CHK, BF16, "qkvT")
    kvs, kvs_b = [], []
    for s in range(4):
        a_, b_ = ar.alloc(128, 8 * CHK, BF16, f"kv{s}")
        kvs.append(a_); kvs_b.append(b_)
    zbuf, zbuf_b = ar.alloc(16, 4 * CHK, F32, "zbuf")
    zmx, zmx_b = ar.alloc(16, CHK, F32, "zmx")
    wrows, wrows_b = ar.alloc(16, 4 * CHK, BF16, "wrows")
    kd, kd_b = ar.alloc(128, CHK, BF16, "kd")
    zp, zp_b = ar.alloc(128, 8 * CHK, BF16, "zp")
    aom, aom_b = ar.alloc(128, CHK, F32, "aom")
    tmpf, tmpf_b = ar.alloc(128, D, F32, "tmpf")
    haak, haak_b = ar.alloc(128, D, F32, "haak")
    x2b, x2b_b = ar.alloc(128, D, BF16, "x2b")
    lgrow, lgrow_b = ar.alloc(128, NSUB * NEXP, F32, "lgrow")
    topsc, topsc_b = ar.alloc(128, 4 * NEXP, F32, "topsc")
    msb, msb_b = ar.alloc(128, NEXP, BF16, "msb")
    idxc, idxc_b = ar.alloc(128, 8, F32, "idxc")
    idxu, idxu_b = ar.alloc(128, 8, U32, "idxu")

    assert ar.off <= ar.cap, ar.off

    def tcol(apx, t, w):
        return apx[:, t * w:(t + 1) * w]

    for c in range(NCHK):
        r0 = c * CHK
        for s in range(4):
            sfx = f"_{c}_{s}"
            for q in range(NSUB):
                P.dma("sp", lambda e, s=s, r0=r0, q=q: e.dma_start(
                    out=xck[:, q * D:(q + 1) * D],
                    in_=xs[s, r0 + q * 128:r0 + (q + 1) * 128, :]),
                    reads=[bI["xs"]], writes=[xck_b], tag=f"ldx{q}" + sfx)

            def haaidx(e, s=s, r0=r0):
                ch = CH["dve"]
                for q in range(NSUB):
                    ch(e, e.tensor_scalar(idxc[:, q:q + 1], selc[:, s:s + 1], -HUGE, HUGE,
                                          ALU.mult, ALU.add))
                    ch(e, e.tensor_tensor(idxc[:, q:q + 1], idxc[:, q:q + 1], iotacol[:, 0:1],
                                          ALU.add))
                    ch(e, e.tensor_scalar(idxc[:, q:q + 1], idxc[:, q:q + 1],
                                          float(r0 + q * 128), None, ALU.add))
                    ch(e, e.tensor_tensor(idxc[:, 4 + q:5 + q], iotacol[:, 0:1],
                                          hp8k[:, 0:1], ALU.add))
                    ch(e, e.tensor_scalar(idxc[:, 4 + q:5 + q], idxc[:, 4 + q:5 + q],
                                          float(s * NH + r0 + q * 128), None, ALU.add))
                return e.tensor_copy(idxu[:, 0:8], idxc[:, 0:8])
            P.op("dve", haaidx, reads=[selc_b, iotacol_b, idxc_b, idxu_b, hp8k_b],
                 writes=[idxc_b, idxu_b], tag="hidx" + sfx)
            for q in range(NSUB):
                P.dma("pool", lambda e, q=q: e.indirect_dma_start(
                    pr_x[:, :], bass.IndirectOffsetOnAxis(idxu[:, q:q + 1], 0),
                    xck[:, q * D:(q + 1) * D], None,
                    bounds_check=rbc_nh, oob_is_err=False),
                    reads=[xck_b, idxu_b], writes=[b_x], tag=f"xsp{q}" + sfx)

            for q in range(NSUB):
                xq = xck[:, q * D:(q + 1) * D]
                xnq = xnk[:, q * D:(q + 1) * D]

                def ln(e, xq=xq, xnq=xnq):
                    ch = CH["dve"]
                    ch(e, e.reduce_sum(st8[:, 0:1], xq, axis=AX.X))
                    ch(e, e.tensor_scalar(st8[:, 0:1], st8[:, 0:1], 1.0 / D, None, ALU.mult))
                    ch(e, e.tensor_scalar(xnq, xq, st8[:, 0:1], None, ALU.subtract))
                    ch(e, e.tensor_tensor(scr4[:, :], xnq, xnq, ALU.mult))
                    ch(e, e.reduce_sum(st8[:, 1:2], scr4[:, :], axis=AX.X))
                    return e.tensor_scalar(st8[:, 1:2], st8[:, 1:2], 1.0 / D, None, ALU.mult)
                P.op("dve", ln, reads=[xck_b, st8_b, xnk_b, scr4_b],
                     writes=[xnk_b, scr4_b, st8_b], tag="ln" + sfx + f"{q}")
                P.op("act", lambda e: e.activation(st8[:, 2:3], st8[:, 1:2], AF.Sqrt,
                                                   bias=epscol[:, 0:1], scale=1.0),
                     reads=[st8_b, epscol_b], writes=[st8_b], tag="lns" + sfx + f"{q}")

                def fin(e, xq=xq, xnq=xnq):
                    ch = CH["dve"]
                    ch(e, e.reciprocal(st8[:, 3:4], st8[:, 2:3]))
                    ch(e, e.tensor_scalar(xnq, xnq, st8[:, 3:4], None, ALU.mult))
                    ch(e, e.tensor_tensor(st8[:, 4:5], st8[:, 3:4], st8[:, 3:4], ALU.mult))
                    return e.tensor_tensor(st8[:, 4:5], st8[:, 4:5], st8[:, 1:2], ALU.mult)
                P.op("dve", fin, reads=[st8_b, xnk_b], writes=[st8_b, xnk_b],
                     tag="lnf" + sfx + f"{q}")
                P.op("act", lambda e: e.activation(st8[:, 5:6], st8[:, 4:5], AF.Sqrt,
                                                   bias=epscol[:, 0:1], scale=1.0),
                     reads=[st8_b, epscol_b], writes=[st8_b], tag="lns2" + sfx + f"{q}")

                def mku(e, xq=xq, xnq=xnq):
                    ch = CH["dve"]
                    ch(e, e.reciprocal(st8[:, 6:7], st8[:, 5:6]))
                    return e.tensor_scalar(xq, xnq, st8[:, 6:7], None, ALU.mult)
                P.op("dve", mku, reads=[st8_b, xnk_b], writes=[xck_b, st8_b],
                     tag="mku" + sfx + f"{q}")

            for q in range(NSUB):
                P.dma("pool", lambda e, q=q: e.indirect_dma_start(
                    pr_xn[:, :], bass.IndirectOffsetOnAxis(idxu[:, q:q + 1], 0),
                    xnk[:, q * D:(q + 1) * D], None,
                    bounds_check=rbc_nh, oob_is_err=False),
                    reads=[xnk_b, idxu_b], writes=[b_xn], tag=f"nsp2{q}" + sfx)

            for f in range(8):
                for q in range(NSUB):
                    P.op("pe", lambda e, f=f, q=q: e.transpose(
                        ps[3][:, q * 128:(q + 1) * 128],
                        xck[:, q * D + f * 128:q * D + (f + 1) * 128], s_idf[:, :]),
                        reads=[xck_b, s_idf_b], writes=[psb[3]], tag=f"trU{f}{q}" + sfx)
                P.op("act", lambda e, f=f: e.activation(
                    tcol(ln1T, f, CHK), ps[3][:, 0:CHK], AF.Identity,
                    bias=cols[:, C_LN1B + f:C_LN1B + f + 1],
                    scale=cols[:, C_LN1W + f:C_LN1W + f + 1]),
                    reads=[psb[3], cols_b], writes=[ln1T_b], tag=f"l1T{f}" + sfx)
                P.op("act", lambda e, f=f: e.activation(
                    tmpf.bitcast(BF16)[:, 0:CHK], ps[3][:, 0:CHK], AF.Identity,
                    bias=cols[:, C_NB + f:C_NB + f + 1],
                    scale=cols[:, C_NG + f:C_NG + f + 1]),
                    reads=[psb[3], cols_b], writes=[tmpf_b], tag=f"nrm{f}" + sfx)
                P.dma("sp", lambda e, f=f, s=s, r0=r0: e.dma_start(
                    out=pr_norm[s * HID + f * 128:s * HID + (f + 1) * 128, r0:r0 + CHK],
                    in_=tmpf.bitcast(BF16)[:, 0:CHK]),
                    reads=[tmpf_b], writes=[b_norm], tag=f"nsp{f}" + sfx)

            for f in range(16):
                for cc in range(8):
                    P.op("pe", lambda e, f=f, cc=cc: e.matmul(
                        ps[0][:, 0:CHK],
                        qkvw[:, cc * HID + f * 128:cc * HID + (f + 1) * 128],
                        tcol(ln1T, cc, CHK), start=(cc == 0), stop=(cc == 7)),
                        reads=[qkvw_b, ln1T_b], writes=[psb[0]], tag=f"qkv{f}" + sfx)
                if f % 2 == 0:
                    P.op("dve", lambda e, f=f: e.tensor_copy(tcol(qkvT, f // 2, CHK),
                                                             ps[0][:, 0:CHK]),
                         reads=[psb[0]], writes=[qkvT_b], tag=f"qkve{f}" + sfx)
                else:
                    P.op("dve", lambda e, f=f, s=s: e.tensor_copy(tcol(kvs[s], f // 2, CHK),
                                                                  ps[0][:, 0:CHK]),
                         reads=[psb[0]], writes=[kvs_b[s]], tag=f"qkvo{f}" + sfx)

            for t in range(s + 1):
                for m in range(8):
                    def zprod(e, m=m, t=t):
                        ch = CH["dve"]
                        ch(e, e.tensor_copy(kd[0:64, :], kvs[t][0:64, m * CHK:(m + 1) * CHK]))
                        ch(e, e.tensor_copy(kd[64:128, :], kvs[t][0:64, m * CHK:(m + 1) * CHK]))
                        return e.tensor_tensor(
                            zp[:, m * CHK:(m + 1) * CHK],
                            qkvT[:, m * CHK:(m + 1) * CHK], kd[:, :], ALU.mult)
                    P.op("dve", zprod, reads=[qkvT_b, kvs_b[t], kd_b],
                         writes=[zp_b, kd_b], tag=f"zp{m}_{t}" + sfx)
                for m in range(8):
                    P.op("pe", lambda e, m=m: e.matmul(
                        ps[1][0:16, 0:CHK], s_e16[:, m * 16:(m + 1) * 16],
                        zp[:, m * CHK:(m + 1) * CHK], start=(m == 0), stop=(m == 7)),
                        reads=[zp_b, s_e16_b], writes=[psb[1]], tag=f"zr{m}_{t}" + sfx)
                P.op("dve", lambda e, t=t: e.tensor_copy(zbuf[0:16, t * CHK:(t + 1) * CHK],
                                                         ps[1][0:16, 0:CHK]),
                     reads=[psb[1]], writes=[zbuf_b], tag=f"ze{t}" + sfx)

            def soft(e, s=s):
                ch = CH["dve"]
                ch(e, e.tensor_copy(zmx[0:16, :], zbuf[0:16, 0:CHK]))
                for t in range(1, s + 1):
                    ch(e, e.tensor_tensor(zmx[0:16, :], zmx[0:16, :],
                                          zbuf[0:16, t * CHK:(t + 1) * CHK], ALU.max))
                for t in range(s + 1):
                    ch(e, e.tensor_tensor(zbuf[0:16, t * CHK:(t + 1) * CHK],
                                          zbuf[0:16, t * CHK:(t + 1) * CHK],
                                          zmx[0:16, :], ALU.subtract))
                return e.engine_nop()
            P.op("dve", soft, reads=[zbuf_b, zmx_b], writes=[zbuf_b, zmx_b], tag="sft" + sfx)

            def expo(e, s=s):
                last = None
                for t in range(s + 1):
                    last = e.activation(zbuf[0:16, t * CHK:(t + 1) * CHK],
                                        zbuf[0:16, t * CHK:(t + 1) * CHK], AF.Exp)
                    if t < s:
                        CH["act"](e, last)
                return last
            P.op("act", expo, reads=[zbuf_b], writes=[zbuf_b], tag="exp" + sfx)

            def den(e, s=s):
                ch = CH["dve"]
                ch(e, e.tensor_copy(zmx[0:16, :], zbuf[0:16, 0:CHK]))
                for t in range(1, s + 1):
                    ch(e, e.tensor_tensor(zmx[0:16, :], zmx[0:16, :],
                                          zbuf[0:16, t * CHK:(t + 1) * CHK], ALU.add))
                ch(e, e.reciprocal(zmx[0:16, :], zmx[0:16, :]))
                for t in range(s + 1):
                    if t < s:
                        ch(e, e.tensor_tensor(wrows[0:16, t * CHK:(t + 1) * CHK],
                                              zbuf[0:16, t * CHK:(t + 1) * CHK],
                                              zmx[0:16, :], ALU.mult))
                    else:
                        return e.tensor_tensor(wrows[0:16, t * CHK:(t + 1) * CHK],
                                               zbuf[0:16, t * CHK:(t + 1) * CHK],
                                               zmx[0:16, :], ALU.mult)
            P.op("dve", den, reads=[zbuf_b, zmx_b, wrows_b], writes=[zmx_b, wrows_b],
                 tag="den" + sfx)

            for m in range(8):
                for t in range(s + 1):
                    P.op("pe", lambda e, m=m, t=t: e.matmul(
                        ps[2][:, 0:CHK], s_f16[0:16, m * 128:(m + 1) * 128],
                        wrows[0:16, t * CHK:(t + 1) * CHK], start=True, stop=True),
                        reads=[wrows_b, s_f16_b], writes=[psb[2]], tag=f"wb{m}{t}" + sfx)
                    for j in range(2):
                        def acc(e, m=m, j=j, t=t):
                            ch = CH["dve"]
                            ch(e, e.tensor_tensor(
                                tmpf[64 * j:64 * j + 64, 0:CHK],
                                ps[2][64 * j:64 * j + 64, 0:CHK],
                                kvs[t][64:128, m * CHK:(m + 1) * CHK], ALU.mult))
                            if t == 0:
                                return e.tensor_copy(aom[64 * j:64 * j + 64, 0:CHK],
                                                     tmpf[64 * j:64 * j + 64, 0:CHK])
                            return e.tensor_tensor(aom[64 * j:64 * j + 64, 0:CHK],
                                                   aom[64 * j:64 * j + 64, 0:CHK],
                                                   tmpf[64 * j:64 * j + 64, 0:CHK], ALU.add)
                        P.op("dve", acc, reads=[psb[2], kvs_b[t], tmpf_b, aom_b],
                             writes=[tmpf_b, aom_b], tag=f"ac{m}{j}{t}" + sfx)
                P.op("dve", lambda e, m=m: e.tensor_copy(tcol(zp, m, CHK), aom[:, 0:CHK]),
                     reads=[aom_b], writes=[zp_b], tag=f"aob{m}" + sfx)
                if debug and s == 0:
                    P.dma("sp", lambda e, m=m, r0=r0: e.dma_start(
                        out=dbg["attn"][m * 128:(m + 1) * 128, r0:r0 + CHK],
                        in_=aom[:, 0:CHK]),
                        reads=[aom_b], writes=[b_dbg["attn"]], tag=f"dba{m}" + sfx)


            for q in range(NSUB):
                qfx = sfx + f"q{q}"
                for half in range(2):
                    for cc in range(8):
                        P.op("pe", lambda e, q=q, half=half, cc=cc: e.matmul(
                            pd[0][:, half * 512:(half + 1) * 512],
                            zp[:, cc * CHK + q * 128:cc * CHK + q * 128 + 128],
                            projw[:, cc * D + half * 512:cc * D + (half + 1) * 512],
                            start=(cc == 0), stop=(cc == 7)),
                            reads=[zp_b, projw_b], writes=[pdb[0]], tag=f"pj{half}{cc}" + qfx)
                P.op("dve", lambda e, q=q: e.tensor_tensor(
                    haak[:, :], pd[0][:, 0:D], xnk[:, q * D:(q + 1) * D], ALU.add),
                    reads=[pdb[0], xnk_b], writes=[haak_b], tag="haa" + qfx)
                P.dma("pool", lambda e, q=q: e.indirect_dma_start(
                    pr_haa[:, :], bass.IndirectOffsetOnAxis(idxu[:, q:q + 1], 0),
                    haak[:, :], None, bounds_check=rbc_nh, oob_is_err=False),
                    reads=[haak_b, idxu_b], writes=[b_haa], tag="hsp" + qfx)

                def ln2(e):
                    ch = CH["dve"]
                    ch(e, e.reduce_sum(st8[:, 0:1], haak[:, :], axis=AX.X))
                    ch(e, e.tensor_scalar(st8[:, 0:1], st8[:, 0:1], 1.0 / D, None, ALU.mult))
                    ch(e, e.tensor_scalar(tmpf[:, 0:D], haak[:, :], st8[:, 0:1],
                                          None, ALU.subtract))
                    ch(e, e.tensor_tensor(scr4[:, :], tmpf[:, 0:D], tmpf[:, 0:D], ALU.mult))
                    ch(e, e.reduce_sum(st8[:, 1:2], scr4[:, :], axis=AX.X))
                    return e.tensor_scalar(st8[:, 1:2], st8[:, 1:2], 1.0 / D, None, ALU.mult)
                P.op("dve", ln2, reads=[haak_b, tmpf_b, scr4_b, st8_b],
                     writes=[tmpf_b, scr4_b, st8_b], tag="ln2" + qfx)
                P.op("act", lambda e: e.activation(st8[:, 2:3], st8[:, 1:2], AF.Sqrt,
                                                   bias=epscol[:, 0:1], scale=1.0),
                     reads=[st8_b, epscol_b], writes=[st8_b], tag="ln2s" + qfx)

                def c2(e):
                    ch = CH["dve"]
                    ch(e, e.reciprocal(st8[:, 3:4], st8[:, 2:3]))
                    ch(e, e.tensor_scalar(x2b[:, :], tmpf[:, 0:D], st8[:, 3:4],
                                          None, ALU.mult))
                    return e.tensor_scalar(tmpf[:, 0:D], tmpf[:, 0:D], st8[:, 3:4],
                                           None, ALU.mult)
                P.op("dve", c2, reads=[tmpf_b, st8_b], writes=[x2b_b, tmpf_b, st8_b],
                     tag="c2" + qfx)

                P.dma("pool", lambda e, q=q: e.indirect_dma_start(
                    sh_x2[:, :], bass.IndirectOffsetOnAxis(idxu[:, 4 + q:5 + q], 0),
                    x2b[:, :], None, bounds_check=rbc_x2, oob_is_err=False),
                    reads=[x2b_b, idxu_b], writes=[b_shx2], tag="wx2" + qfx)

                for f in range(8):
                    P.op("pe", lambda e, f=f: e.transpose(
                        pd[1][:, f * 128:(f + 1) * 128], tmpf[:, f * 128:(f + 1) * 128],
                        s_idf[:, :]),
                        reads=[tmpf_b, s_idf_b], writes=[pdb[1]], tag=f"tx2{f}" + qfx)
                for f in range(8):
                    P.op("act", lambda e, f=f: e.activation(
                        scr4[:, f * 128:(f + 1) * 128], pd[1][:, f * 128:(f + 1) * 128],
                        AF.Identity, bias=cols[:, C_LN2B + f:C_LN2B + f + 1],
                        scale=cols[:, C_LN2W + f:C_LN2W + f + 1]),
                        reads=[pdb[1], cols_b], writes=[scr4_b], tag=f"x2T{f}" + qfx)
                for cc in range(8):
                    P.op("pe", lambda e, cc=cc: e.matmul(
                        ps[1][:, 0:NEXP], scr4[:, cc * 128:(cc + 1) * 128],
                        routw[:, cc * 8:(cc + 1) * 8], start=(cc == 0), stop=(cc == 7)),
                        reads=[scr4_b, routw_b], writes=[psb[1]], tag=f"lg{cc}" + qfx)
                P.op("dve", lambda e, q=q: e.tensor_copy(lgrow[:, q * 8:(q + 1) * 8],
                                                         ps[1][:, 0:NEXP]),
                     reads=[psb[1]], writes=[lgrow_b], tag="lge" + qfx)
                if debug:
                    P.dma("sp", lambda e, q=q, r0=r0, s=s: e.dma_start(
                        out=dbg["logits"][s * NH + r0 + q * 128:s * NH + r0 + (q + 1) * 128, :],
                        in_=lgrow[:, q * 8:(q + 1) * 8]),
                        reads=[lgrow_b], writes=[b_dbg["logits"]], tag="dlg" + qfx)
                if debug and s == 0:
                    P.dma("sp", lambda e, q=q, r0=r0: e.dma_start(
                        out=dbg["haa"][r0 + q * 128:r0 + (q + 1) * 128, :], in_=haak[:, :]),
                        reads=[haak_b], writes=[b_dbg["haa"]], tag="dh" + qfx)
                    P.dma("sp", lambda e, q=q, r0=r0: e.dma_start(
                        out=dbg["xn"][r0 + q * 128:r0 + (q + 1) * 128, :],
                        in_=xnk[:, q * D:(q + 1) * D]),
                        reads=[xnk_b], writes=[b_dbg["xn"]], tag="dxn" + qfx)

                tix = c * NSUB + q
                zq = lgrow[:, q * 8:(q + 1) * 8]

                def top2(e, zq=zq):
                    ch = CH["dve"]
                    ch(e, e.reduce_max(st8[:, 0:1], zq, axis=AX.X))
                    ch(e, e.tensor_scalar(topsc[:, 0:8], zq, st8[:, 0:1], None, ALU.is_equal))
                    ch(e, e.scalar_tensor_tensor(topsc[:, 8:16], topsc[:, 0:8], -HUGE, zq,
                                                 ALU.mult, ALU.add))
                    ch(e, e.reduce_max(st8[:, 1:2], topsc[:, 8:16], axis=AX.X))
                    ch(e, e.tensor_scalar(topsc[:, 16:24], topsc[:, 8:16], st8[:, 1:2],
                                          None, ALU.is_equal))
                    ch(e, e.tensor_tensor(st8[:, 2:3], st8[:, 0:1], st8[:, 1:2], ALU.subtract))
                    ch(e, e.tensor_tensor(topsc[:, 24:32], topsc[:, 0:8], topsc[:, 16:24],
                                          ALU.add))
                    return e.tensor_copy(msb[:, 0:8], topsc[:, 24:32])
                P.op("dve", top2, reads=[lgrow_b, topsc_b, st8_b, msb_b],
                     writes=[topsc_b, st8_b, msb_b], tag="t2" + qfx)

                P.op("pe", lambda e: e.matmul(ps[1][0:1, 0:8], s_onescb[:, 0:1], msb[:, 0:8],
                                              start=True, stop=True),
                     reads=[msb_b, s_onescb_b], writes=[psb[1]], tag="cm" + qfx)
                first = (c == 0 and q == 0)
                P.op("dve", lambda e, s=s, first=first: e.tensor_copy(
                    cntrow[0:1, s * 8:(s + 1) * 8], ps[1][0:1, 0:8]) if first else
                    e.tensor_tensor(cntrow[0:1, s * 8:(s + 1) * 8],
                                    cntrow[0:1, s * 8:(s + 1) * 8], ps[1][0:1, 0:8], ALU.add),
                    reads=[psb[1]] + ([] if first else [cntrow_b]), writes=[cntrow_b],
                    tag="ca" + qfx)

                def ret(e, s=s, tix=tix):
                    ch = CH["dve"]
                    if s == 0:
                        ch(e, e.tensor_scalar(dzo[:, tix:tix + 1], st8[:, 2:3],
                                              selc[:, s:s + 1], None, ALU.mult))
                        ch(e, e.tensor_scalar(m1o[:, tix * 8:(tix + 1) * 8], topsc[:, 0:8],
                                              selc[:, s:s + 1], None, ALU.mult))
                        return e.tensor_scalar(m2o[:, tix * 8:(tix + 1) * 8], topsc[:, 16:24],
                                               selc[:, s:s + 1], None, ALU.mult)
                    ch(e, e.scalar_tensor_tensor(dzo[:, tix:tix + 1], st8[:, 2:3],
                                                 selc[:, s:s + 1], dzo[:, tix:tix + 1],
                                                 ALU.mult, ALU.add))
                    ch(e, e.scalar_tensor_tensor(m1o[:, tix * 8:(tix + 1) * 8], topsc[:, 0:8],
                                                 selc[:, s:s + 1], m1o[:, tix * 8:(tix + 1) * 8],
                                                 ALU.mult, ALU.add))
                    return e.scalar_tensor_tensor(m2o[:, tix * 8:(tix + 1) * 8], topsc[:, 16:24],
                                                  selc[:, s:s + 1], m2o[:, tix * 8:(tix + 1) * 8],
                                                  ALU.mult, ALU.add)
                P.op("dve", ret, reads=[st8_b, topsc_b, selc_b, dzo_b, m1o_b, m2o_b],
                     writes=[dzo_b, m1o_b, m2o_b], tag="ret" + qfx)


    if debug:
        P.dma("sp", lambda e: e.dma_start(out=dbg["cnt"][0:1, :], in_=cntrow[0:1, :]),
              reads=[cntrow_b], writes=[b_dbg["cnt"]], tag="dcnt")

    def wcnt(e):
        hv = vload(e, hint[0:1, 0:1], 0, 1)
        return e.dma_start(out=sh_cnt[bass.ts(hv, 1), :], in_=cntrow[0:1, :])
    P.dma("pool", wcnt, reads=[cntrow_b, hint_b], writes=[b_shcnt], tag="wcnt")

    barrier(1, [b_shcnt, b_shx2, b_haa, b_norm, b_x, b_xn], tag="B1")

    def finish(mini=True):
        if mini:
            mini_tail()
        P.op("pool", lambda e: e.engine_nop(), reads=[b_out], tag="fence")
        stream = P.build_streams(nc)
        with nc.Block() as blk:
            blk.tensor(stream("pe"))
            blk.scalar(stream("act"))
            blk.vector(stream("dve"))
            blk.gpsimd(stream("pool"))
            blk.sync(stream("sp"))
        return nc, dbg

    def mini_tail():
        ar.mark()
        oacx, oacx_b = ar.alloc(128, D, F32, "oacx")
        xre2, xre2_b = ar.alloc(128, D, F32, "xre2")
        hre2, hre2_b = ar.alloc(128, D, F32, "hre2")
        for t in range(16):
            r0 = t * 128
            P.dma("sp", lambda e, r0=r0: e.dma_start(out=xre2[:, :], in_=pr_x[r0:r0 + 128, :]),
                  reads=[b_x, btok[0]], writes=[xre2_b], tag=f"tx{t}")
            P.dma("sp", lambda e, r0=r0: e.dma_start(out=hre2[:, :], in_=pr_haa[r0:r0 + 128, :]),
                  reads=[b_haa, btok[0]], writes=[hre2_b], tag=f"th{t}")
            P.op("dve", lambda e: e.tensor_tensor(oacx[:, :], xre2[:, :], hre2[:, :], ALU.add),
                 reads=[xre2_b, hre2_b, oacx_b], writes=[oacx_b], tag=f"ta{t}")
            P.dma("sp", lambda e, r0=r0: e.dma_start(out=out[r0:r0 + 128, :], in_=oacx[:, :]),
                  reads=[oacx_b], writes=[b_out.sub(r0 * D * 4, (r0 + 128) * D * 4)],
                  tag=f"to{t}")

    if PH <= 1:
        return finish()

    ar.release()
    ar.mark()
    cA, cA_b = ar.alloc(1, 32, F32, "cA")
    cB, cB_b = ar.alloc(1, 32, F32, "cB")
    cG, cG_b = ar.alloc(1, 32, F32, "cG")
    rows8, rows8_b = ar.alloc(1, 64, F32, "rows8")
    colsc, colsc_b = ar.alloc(8, 8, F32, "colsc")
    msum2, msum2_b = ar.alloc(128, 8, BF16, "msum2")
    totrow, totrow_b = ar.alloc(1, 128, F32, "totrow")
    totcol, totcol_b = ar.alloc(128, 1, F32, "totcol")
    totmask, totmask_b = ar.alloc(128, 8, BF16, "totmask")
    basefm, basefm_b = ar.alloc(8, 16, F32, "basefm")
    posw, posw_b = ar.alloc(8, 6 * 128, F32, "posw")
    mkT, mkT_b = ar.alloc(8, 2 * 128, F32, "mkT")
    srow, srow_b = ar.alloc(1, 128, F32, "srow")
    scol, scol_b = ar.alloc(128, 4, F32, "scol")
    scatv, scatv_b = ar.alloc(128, 1, F32, "scatv")
    hpcol, hpcol_b = ar.alloc(128, 1, F32, "hpcol")
    sidxu2, sidxu2_b = ar.alloc(128, 2, U32, "sidxu2")
    p1all, p1all_b = ar.alloc(128, 16, F32, "p1all")

    csum, csum_b = ar.alloc(1, 64, F32, "csum")

    cpol = nc.alloc_semaphore("cpol")

    def ldcnt(e):
        e.tensor_copy(csum[0:1, 63:64], tot16k[0:1, 0:1])
        with e.register("ndc") as ndr, e.register("ctg") as tgt:
            e.reg_mov(ndr, 1)
            e.reg_mov(tgt, 0)
            with e.While(ndr):
                e.dma_start(out=cA[0:1, :], in_=sh_cnt[0:1, :]).then_inc(cpol, 16)
                e.dma_start(out=cB[0:1, :], in_=sh_cnt[1:2, :]).then_inc(cpol, 16)
                e.reg_add(tgt, tgt, 32)
                e.wait_ge(cpol, tgt)
                e.tensor_tensor(csum[0:1, 0:16], cA[0:1, 0:16], cA[0:1, 16:32], ALU.add)
                e.tensor_tensor(csum[0:1, 32:48], cB[0:1, 0:16], cB[0:1, 16:32], ALU.add)
                e.tensor_tensor(csum[0:1, 0:8], csum[0:1, 0:8], csum[0:1, 8:16], ALU.add)
                e.tensor_tensor(csum[0:1, 32:40], csum[0:1, 32:40], csum[0:1, 40:48], ALU.add)
                e.tensor_tensor(csum[0:1, 0:4], csum[0:1, 0:4], csum[0:1, 4:8], ALU.add)
                e.tensor_tensor(csum[0:1, 32:36], csum[0:1, 32:36], csum[0:1, 36:40], ALU.add)
                e.tensor_tensor(csum[0:1, 0:2], csum[0:1, 0:2], csum[0:1, 2:4], ALU.add)
                e.tensor_tensor(csum[0:1, 32:34], csum[0:1, 32:34], csum[0:1, 34:36], ALU.add)
                e.tensor_tensor(csum[0:1, 0:1], csum[0:1, 0:1], csum[0:1, 1:2], ALU.add)
                e.tensor_tensor(csum[0:1, 32:33], csum[0:1, 32:33], csum[0:1, 33:34], ALU.add)
                e.tensor_tensor(csum[0:1, 0:1], csum[0:1, 0:1], tot16k[0:1, 0:1], ALU.subtract)
                e.tensor_tensor(csum[0:1, 32:33], csum[0:1, 32:33], tot16k[0:1, 0:1], ALU.subtract)
                e.tensor_tensor(csum[0:1, 0:1], csum[0:1, 0:1], csum[0:1, 0:1], ALU.mult)
                e.tensor_tensor(csum[0:1, 32:33], csum[0:1, 32:33], csum[0:1, 32:33], ALU.mult)
                e.tensor_tensor(csum[0:1, 63:64], csum[0:1, 0:1], csum[0:1, 32:33], ALU.add)
                nd = e.value_load(csum[0:1, 63:64].bitcast(I32))
                e.reg_mov(ndr, nd)
        return e.engine_nop()
    P.op("pool", ldcnt, reads=[b_shcnt, btok[0], tot16k_b, cA_b, cB_b, csum_b],
         writes=[cA_b, cB_b, csum_b], tag="ldcnt")
    P.op("dve", lambda e: e.tensor_tensor(cG[0:1, :], cA[0:1, :], cB[0:1, :], ALU.add),
         reads=[cA_b, cB_b], writes=[cG_b], tag="cG")
    if debug:
        P.dma("sp", lambda e: e.dma_start(out=dbg["ngi"][0:1, 8:40], in_=cB[0:1, :]),
              reads=[cB_b], writes=[b_dbg["ngi"]], tag="dcB")

    def mkrows(e):
        ch = CH["dve"]
        ch(e, e.tensor_scalar(rows8[0:1, 0:8], cG[0:1, 0:8], selr[0:1, 4:5], None, ALU.mult))
        for s in range(1, 4):
            ch(e, e.scalar_tensor_tensor(rows8[0:1, 0:8], cG[0:1, 8 * s:8 * s + 8],
                                         selr[0:1, 4 + s:5 + s], rows8[0:1, 0:8],
                                         ALU.mult, ALU.add))
        ch(e, e.tensor_scalar(rows8[0:1, 8:16], cA[0:1, 0:8], selr[0:1, 0:1], None, ALU.mult))
        for s in range(1, 4):
            ch(e, e.scalar_tensor_tensor(rows8[0:1, 8:16], cA[0:1, 8 * s:8 * s + 8],
                                         selr[0:1, s:s + 1], rows8[0:1, 8:16],
                                         ALU.mult, ALU.add))
        ch(e, e.tensor_scalar(rows8[0:1, 8:16], rows8[0:1, 8:16], selr[0:1, 8:9],
                              None, ALU.mult))
        ch(e, e.tensor_scalar(rows8[0:1, 16:24], cG[0:1, 0:8], selr[0:1, 0:1], None, ALU.mult))
        for s in range(1, 4):
            ch(e, e.scalar_tensor_tensor(rows8[0:1, 16:24], cG[0:1, 8 * s:8 * s + 8],
                                         selr[0:1, s:s + 1], rows8[0:1, 16:24],
                                         ALU.mult, ALU.add))
        ch(e, e.tensor_scalar(rows8[0:1, 24:28], rows8[0:1, 16:20], selr[0:1, 8:9], None,
                              ALU.mult))
        ch(e, e.tensor_tensor(rows8[0:1, 28:32], rows8[0:1, 16:20], rows8[0:1, 24:28],
                              ALU.subtract))
        ch(e, e.scalar_tensor_tensor(rows8[0:1, 28:32], rows8[0:1, 20:24], selr[0:1, 8:9],
                                     rows8[0:1, 28:32], ALU.mult, ALU.add))
        ch(e, e.tensor_scalar(rows8[0:1, 32:36], rows8[0:1, 28:32], 0.0, None, ALU.is_gt))
        for g in range(1, NGROUP):
            ch(e, e.tensor_scalar(rows8[0:1, 36:40], rows8[0:1, 28:32], float(512 * g),
                                  None, ALU.is_gt))
            ch(e, e.tensor_tensor(rows8[0:1, 32:36], rows8[0:1, 32:36], rows8[0:1, 36:40],
                                  ALU.add))
        return e.tensor_copy(ngi[0:1, 0:4], rows8[0:1, 32:36])
    P.op("dve", mkrows, reads=[cG_b, cA_b, selr_b, rows8_b, ngi_b],
         writes=[rows8_b, ngi_b], tag="mkrows")

    for (cix, off) in [(0, 0), (1, 8)]:
        P.op("pe", lambda e, cix=cix, off=off: e.matmul(
            ps[0][0:8, 0:1], rows8[0:1, off:off + 8], s_ones1f[0:1, 0:1],
            start=True, stop=True),
            reads=[rows8_b, s_ones1f_b], writes=[psb[0]], tag=f"pcol{cix}")
        P.op("dve", lambda e, cix=cix: e.tensor_copy(colsc[0:8, cix:cix + 1],
                                                     ps[0][0:8, 0:1]),
             reads=[psb[0]], writes=[colsc_b], tag=f"pcole{cix}")
    P.op("pe", lambda e: e.matmul(ps[0][:, 0:1], s_ones1f[0:1, 0:128], selr[0:1, 10:11],
                                  start=True, stop=True),
         reads=[selr_b, s_ones1f_b], writes=[psb[0]], tag="hpc")
    P.op("dve", lambda e: e.tensor_tensor(hpcol[:, 0:1], ps[0][:, 0:1], iotacol[:, 0:1],
                                          ALU.add),
         reads=[psb[0], iotacol_b], writes=[hpcol_b], tag="hpce")
    P.op("act", lambda e: e.activation(p1all[:, :], dzo[:, :], AF.Sigmoid),
         reads=[dzo_b], writes=[p1all_b], tag="p1")

    msall, msall_b = ar.alloc(128, 16 * 8, BF16, "msall")

    for t in range(16):
        tfx = f"_t{t}"
        P.op("dve", lambda e, t=t: e.tensor_tensor(msall[:, t * 8:(t + 1) * 8],
                                                   m1o[:, t * 8:(t + 1) * 8],
                                                   m2o[:, t * 8:(t + 1) * 8], ALU.add),
             reads=[m1o_b, m2o_b], writes=[msall_b], tag="ms" + tfx)
        P.op("pe", lambda e, t=t: e.matmul(ps[1][0:1, 0:8], s_onescb[:, 0:1],
                                           msall[:, t * 8:(t + 1) * 8],
                                           start=True, stop=True),
             reads=[msall_b, s_onescb_b], writes=[psb[1]], tag="tot" + tfx)
        P.op("dve", lambda e, t=t: e.tensor_copy(totrow[0:1, t * 8:(t + 1) * 8],
                                                 ps[1][0:1, 0:8]),
             reads=[psb[1]], writes=[totrow_b], tag="tote" + tfx)
    P.op("pe", lambda e: e.matmul(ps[1][:, 0:1], totrow[0:1, :], s_ones1f[0:1, 0:1],
                                  start=True, stop=True),
         reads=[totrow_b, s_ones1f_b], writes=[psb[1]], tag="totc")
    P.op("dve", lambda e: e.tensor_copy(totcol[:, 0:1], ps[1][:, 0:1]),
         reads=[psb[1]], writes=[totcol_b], tag="totce")
    P.op("dve", lambda e: e.tensor_scalar(totmask[:, 0:8], s_ohe[:, :], totcol[:, 0:1],
                                          None, ALU.mult),
         reads=[s_ohe_b, totcol_b], writes=[totmask_b], tag="tm")
    P.op("pe", lambda e: e.matmul(ps[1][0:8, 0:16], totmask[:, 0:8], s_t1[:, :],
                                  start=True, stop=True),
         reads=[totmask_b, s_t1_b], writes=[psb[1]], tag="bfm")
    P.op("dve", lambda e: e.tensor_copy(basefm[0:8, :], ps[1][0:8, 0:16]),
         reads=[psb[1]], writes=[basefm_b], tag="bfme")

    for t in range(16):
        tfx = f"_t{t}"
        P.op("pe", lambda e, t=t: e.matmul(ps[0][0:8, 0:128], msall[:, t * 8:(t + 1) * 8],
                                           s_l128[:, :], start=True, stop=True),
             reads=[msall_b, s_l128_b], writes=[psb[0]], tag="pfx" + tfx)

        def posops(e, t=t):
            ch = CH["dve"]
            ch(e, e.tensor_scalar(posw[0:8, 0:128], ps[0][0:8, 0:128],
                                  basefm[0:8, t:t + 1], None, ALU.add))
            ch(e, e.tensor_scalar(posw[0:8, 0:128], posw[0:8, 0:128],
                                  colsc[0:8, 1:2], None, ALU.add))
            ch(e, e.tensor_scalar(posw[0:8, 128:256], posw[0:8, 0:128],
                                  colsc[0:8, 0:1], None, ALU.add))
            ch(e, e.tensor_scalar(posw[0:8, 256:384], posw[0:8, 128:256], float(CAP),
                                  None, ALU.is_lt))
            ch(e, e.tensor_scalar(posw[0:8, 384:512], posw[0:8, 256:384], -HUGE, HUGE,
                                  ALU.mult, ALU.add))
            ch(e, e.tensor_scalar(posw[0:8, 512:640], posw[0:8, 0:128],
                                  s_estr[0:8, 0:1], None, ALU.add))
            return e.tensor_tensor(posw[0:8, 512:640], posw[0:8, 512:640],
                                   posw[0:8, 384:512], ALU.add)
        P.op("dve", posops, reads=[psb[0], basefm_b, colsc_b, s_estr_b, posw_b],
             writes=[posw_b], tag="pos" + tfx)
        P.op("dve", lambda e, t=t: e.tensor_scalar(scatv[:, 0:1], hpcol[:, 0:1],
                                                   float(t * 128 + 1), None, ALU.add),
             reads=[hpcol_b], writes=[scatv_b], tag="sv" + tfx)
        for kk, (mo, mo_b) in enumerate([(m1o, m1o_b), (m2o, m2o_b)]):
            kfx = tfx + f"k{kk}"
            P.op("pe", lambda e, mo=mo, t=t: e.transpose(
                pst_b[0:8, 0:128], mo[:, t * 8:(t + 1) * 8], s_idb[:, :]),
                reads=[mo_b, s_idb_b], writes=[psb[3]], tag="mkt" + kfx)
            P.op("dve", lambda e, kk=kk: e.tensor_copy(mkT[0:8, kk * 128:(kk + 1) * 128],
                                                       pst_b[0:8, 0:128]),
                 reads=[psb[3]], writes=[mkT_b], tag="mkte" + kfx)
            P.op("dve", lambda e, kk=kk: e.tensor_tensor(
                posw[0:8, 640:768], mkT[0:8, kk * 128:(kk + 1) * 128],
                posw[0:8, 512:640], ALU.mult),
                reads=[mkT_b, posw_b], writes=[posw_b], tag="rx" + kfx)
            P.op("pe", lambda e: e.matmul(ps[2][0:1, 0:128], s_ones8f[0:8, 0:1],
                                          posw[0:8, 640:768], start=True, stop=True),
                 reads=[posw_b, s_ones8f_b], writes=[psb[2]], tag="rs" + kfx)
            P.op("dve", lambda e: e.tensor_copy(srow[0:1, :], ps[2][0:1, 0:128]),
                 reads=[psb[2]], writes=[srow_b], tag="rse" + kfx)
            P.op("pe", lambda e: e.matmul(ps[2][:, 0:1], srow[0:1, :], s_ones1f[0:1, 0:1],
                                          start=True, stop=True),
                 reads=[srow_b, s_ones1f_b], writes=[psb[2]], tag="rc" + kfx)

            def scmake(e, kk=kk, t=t):
                ch = CH["dve"]
                ch(e, e.tensor_copy(scol[:, 0:1], ps[2][:, 0:1]))
                ch(e, e.tensor_copy(sidxu2[:, kk:kk + 1], scol[:, 0:1]))
                ch(e, e.tensor_scalar(scol[:, 1:2], scol[:, 0:1], float(ZROW), None,
                                      ALU.min))
                return e.tensor_copy(gidx_all[:, (t * 2 + kk):(t * 2 + kk) + 1],
                                     scol[:, 1:2])
            P.op("dve", scmake, reads=[psb[2], scol_b, sidxu2_b, gidx_all_b],
                 writes=[scol_b, sidxu2_b, gidx_all_b], tag="sc" + kfx)
            P.dma("pool", lambda e, kk=kk: e.indirect_dma_start(
                sh_S[:, :], bass.IndirectOffsetOnAxis(sidxu2[:, kk:kk + 1], 0),
                scatv[:, 0:1], None, bounds_check=rbc_S, oob_is_err=False),
                reads=[scatv_b, sidxu2_b, btok[0]], writes=[b_shS], tag="ssc" + kfx)
            P.op("dve", lambda e, kk=kk: e.tensor_tensor(
                posw[0:8, 640:768], mkT[0:8, kk * 128:(kk + 1) * 128],
                posw[0:8, 256:384], ALU.mult),
                reads=[mkT_b, posw_b], writes=[posw_b], tag="kx" + kfx)
            P.op("pe", lambda e: e.matmul(ps[2][0:1, 0:128], s_ones8f[0:8, 0:1],
                                          posw[0:8, 640:768], start=True, stop=True),
                 reads=[posw_b, s_ones8f_b], writes=[psb[2]], tag="ks" + kfx)
            P.op("dve", lambda e: e.tensor_copy(srow[0:1, :], ps[2][0:1, 0:128]),
                 reads=[psb[2]], writes=[srow_b], tag="kse" + kfx)
            P.op("pe", lambda e: e.matmul(ps[2][:, 0:1], srow[0:1, :], s_ones1f[0:1, 0:1],
                                          start=True, stop=True),
                 reads=[srow_b, s_ones1f_b], writes=[psb[2]], tag="kc" + kfx)

            def pkmake(e, kk=kk, t=t):
                ch = CH["dve"]
                if kk == 0:
                    ch(e, e.tensor_tensor(scol[:, 2:3], ps[2][:, 0:1], p1all[:, t:t + 1],
                                          ALU.mult))
                else:
                    ch(e, e.tensor_scalar(scol[:, 3:4], p1all[:, t:t + 1], -1.0, 1.0,
                                          ALU.mult, ALU.add))
                    ch(e, e.tensor_tensor(scol[:, 2:3], ps[2][:, 0:1], scol[:, 3:4],
                                          ALU.mult))
                return e.tensor_copy(pk_all[:, (t * 2 + kk):(t * 2 + kk) + 1], scol[:, 2:3])
            P.op("dve", pkmake, reads=[psb[2], p1all_b, scol_b, pk_all_b],
                 writes=[scol_b, pk_all_b], tag="pk" + kfx)

    if debug:
        P.dma("sp", lambda e: e.dma_start(out=dbg["gidx"][:, :], in_=gidx_all[:, :]),
              reads=[gidx_all_b], writes=[b_dbg["gidx"]], tag="dgix")
        P.dma("sp", lambda e: e.dma_start(out=dbg["pk"][:, :], in_=pk_all[:, :]),
              reads=[pk_all_b], writes=[b_dbg["pk"]], tag="dpk")

    if PH <= 2:
        return finish()
    ar.release()
    ACH = 512
    NACH = NH // ACH
    ar.mark()
    normch, normch_b = ar.alloc(128, 8 * ACH, BF16, "normch")
    lqAll, lqAll_b = ar.alloc(128, NH, BF16, "lqAll")
    lkvm, lkvm_b = ar.alloc(128, HID, F32, "lkvm")
    pNcol, pNcol_b = ar.alloc(128, 1, F32, "pNcol")
    h128c, h128c_b = ar.alloc(128, 1, F32, "h128c")
    nidxu, nidxu_b = ar.alloc(128, 8, U32, "nidxu")

    P.op("pe", lambda e: e.matmul(ps[0][:, 0:1], s_ones1f[0:1, 0:128], selr[0:1, 12:13],
                                  start=True, stop=True),
         reads=[selr_b, s_ones1f_b], writes=[psb[0]], tag="pN")
    P.op("dve", lambda e: e.tensor_copy(pNcol[:, 0:1], ps[0][:, 0:1]),
         reads=[psb[0]], writes=[pNcol_b], tag="pNe")

    def mk_h128(e):
        ch = CH["dve"]
        ch(e, e.tensor_scalar(h128c[:, 0:1], hp8k[:, 0:1], 1.0 / 64.0, None, ALU.mult))
        return e.tensor_tensor(h128c[:, 0:1], h128c[:, 0:1], iotacol[:, 0:1], ALU.add)
    P.op("dve", mk_h128, reads=[hp8k_b, iotacol_b, h128c_b], writes=[h128c_b], tag="h128")

    def mk_nidx(e):
        ch = CH["dve"]
        for f in range(8):
            ch(e, e.tensor_tensor(idxc2[:, f:f + 1], pNcol[:, 0:1], iotacol[:, 0:1], ALU.add))
            ch(e, e.tensor_scalar(idxc2[:, f:f + 1], idxc2[:, f:f + 1], float(f * 128),
                                  None, ALU.add))
        return e.tensor_copy(nidxu[:, 0:8], idxc2[:, 0:8])
    idxc2, idxc2_b = ar.alloc(128, 8, F32, "idxc2")
    P.op("dve", mk_nidx, reads=[pNcol_b, iotacol_b, idxc2_b, nidxu_b],
         writes=[idxc2_b, nidxu_b], tag="nidx")

    ar.mark()
    whv, whv_b = ar.alloc(128, 8 * HID, BF16, "whv")
    whg, whg_b = ar.alloc(128, 8 * HID, BF16, "whg")
    wqkb, wqkb_b = ar.alloc(128, 8 * QKD, BF16, "wqkb")
    bhv16, bhv16_b = ar.alloc(1, HID, BF16, "bhv16")
    s1row, s1row_b = ar.alloc(1, 128, BF16, "s1row")
    vtm, vtm_b = ar.alloc(128, NSUB * HID, BF16, "vtm")
    qhead, qhead_b = ar.alloc(128, 5 * ACH, BF16, "qhead")
    attnT, attnT_b = ar.alloc(128, 2 * GSZ, BF16, "attnT")
    lkN, lkN_b = ar.alloc(128, 4 * 128, BF16, "lkN")
    gsc, gsc_b = ar.alloc(128, ACH, BF16, "gsc")

    for ct in range(8):
        P.dma("pool", lambda e, ct=ct: e.dma_start(
            out=whv[:, ct * HID:(ct + 1) * HID], in_=w_h[ct * 128:(ct + 1) * 128, 0:HID]),
            reads=[bI["w_h"]], writes=[whv_b], tag=f"ldwv{ct}")
        P.dma("pool", lambda e, ct=ct: e.dma_start(
            out=whg[:, ct * HID:(ct + 1) * HID], in_=w_h[ct * 128:(ct + 1) * 128, HID:2 * HID]),
            reads=[bI["w_h"]], writes=[whg_b], tag=f"ldwg{ct}")
        P.dma("pool", lambda e, ct=ct: e.dma_start(
            out=wqkb[:, ct * QKD:(ct + 1) * QKD], in_=w_qk[ct * 128:(ct + 1) * 128, :]),
            reads=[bI["w_qk"]], writes=[wqkb_b], tag=f"ldwqk{ct}")
    P.dma("pool", lambda e: e.dma_start(out=bhv16[0:1, :], in_=b_h[0:1, 0:HID]),
          reads=[bI["b_h"]], writes=[bhv16_b], tag="ldbhv")
    P.dma("sp", lambda e: e.dma_start(out=s1row[0:1, :], in_=c_onesr_b16[:, :]),
          reads=[bI["consts"]], writes=[s1row_b], tag="ld1b")

    for c in range(NACH):
        r0 = c * ACH
        afx = f"_a{c}"
        for f in range(8):
            P.dma("pool", lambda e, f=f, r0=r0: e.indirect_dma_start(
                normch[:, f * ACH:(f + 1) * ACH], None,
                pr_norm[:, :], bass.IndirectOffsetOnAxis(nidxu[:, f:f + 1], 0),
                element_offset=r0, bounds_check=rbc_nrm, oob_is_err=False),
                reads=[b_norm, nidxu_b, btok[0]], writes=[normch_b], tag=f"gnm{f}" + afx)

        for q in range(NSUB):
            for oc in range(4):
                for ct in range(8):
                    P.op("pe", lambda e, q=q, oc=oc, ct=ct: e.matmul(
                        ps[0][:, 0:512], normch[:, ct * ACH + q * 128:ct * ACH + q * 128 + 128],
                        whv[:, ct * HID + oc * 512:ct * HID + (oc + 1) * 512],
                        start=(ct == 0), stop=False),
                        reads=[normch_b, whv_b], writes=[psb[0]], tag=f"v{q}{oc}{ct}" + afx)
                P.op("pe", lambda e, oc=oc: e.matmul(
                    ps[0][:, 0:512], s1row[0:1, 0:128], bhv16[0:1, oc * 512:(oc + 1) * 512],
                    start=False, stop=True),
                    reads=[bhv16_b, s1row_b], writes=[psb[0]], tag=f"vb{q}{oc}" + afx)
                P.op("act", lambda e, q=q, oc=oc: e.activation(
                    vtm[:, q * HID + oc * 512:q * HID + (oc + 1) * 512], ps[0][:, 0:512],
                    AF.Silu),
                    reads=[psb[0]], writes=[vtm_b], tag=f"ve{q}{oc}" + afx)

        for f in range(16):
            for ct in range(8):
                P.op("pe", lambda e, f=f, ct=ct: e.matmul(
                    ps[1][:, 0:512], whg[:, ct * HID + f * 128:ct * HID + (f + 1) * 128],
                    normch[:, ct * ACH:ct * ACH + ACH], start=(ct == 0), stop=(ct == 7)),
                    reads=[normch_b, whg_b], writes=[psb[1]], tag=f"g{f}{ct}" + afx)
            P.op("act", lambda e, f=f: e.activation(
                gsc[:, 0:ACH], ps[1][:, 0:512], AF.Silu,
                bias=cols[:, C_BHG + f:C_BHG + f + 1]),
                reads=[psb[1], cols_b], writes=[gsc_b], tag=f"ge{f}" + afx)
            P.dma("sp", lambda e, f=f, r0=r0: e.dma_start(
                out=pr_gate[f * 128:(f + 1) * 128, r0:r0 + ACH], in_=gsc[:, 0:ACH]),
                reads=[gsc_b], writes=[b_gate], tag=f"gs{f}" + afx)

        for ct in range(8):
            P.op("pe", lambda e, ct=ct: e.matmul(
                ps[2][:, 0:512], wqkb[:, ct * QKD:(ct + 1) * QKD],
                normch[:, ct * ACH:ct * ACH + ACH], start=(ct == 0), stop=(ct == 7)),
                reads=[normch_b, wqkb_b], writes=[psb[2]], tag=f"qk{ct}" + afx)
        P.op("act", lambda e: e.activation(qhead[:, 0:ACH], ps[2][:, 0:512], AF.Silu,
                                           bias=cols[:, C_BQK:C_BQK + 1]),
             reads=[psb[2], cols_b], writes=[qhead_b], tag="qke" + afx)
        for hh in range(4):
            P.op("dve", lambda e, hh=hh: e.tensor_scalar(
                qhead[:, (1 + hh) * ACH:(2 + hh) * ACH], qhead[:, 0:ACH],
                cols[:, C_G0 + hh:C_G0 + hh + 1], cols[:, C_BT0 + hh:C_BT0 + hh + 1],
                ALU.mult, ALU.add),
                reads=[qhead_b, cols_b], writes=[qhead_b], tag=f"hd{hh}" + afx)

        for g in range(2):
            gt0 = g * 256
            for jt in range(2):
                P.op("pe", lambda e, g=g, jt=jt: e.matmul(
                    ps[3][:, 0:256], s_idb[:, :], biasT[:, jt * 256:(jt + 1) * 256],
                    start=True, stop=False),
                    reads=[biasT_b, s_idb_b], writes=[psb[3]], tag=f"sb{g}{jt}" + afx)
                P.op("pe", lambda e, g=g, jt=jt: e.matmul(
                    ps[3][:, 0:256], qhead[:, 3 * ACH + gt0 + jt * 128:3 * ACH + gt0 + jt * 128 + 128],
                    qhead[:, 1 * ACH + gt0:1 * ACH + gt0 + 256], start=False, stop=True),
                    reads=[qhead_b], writes=[psb[3]], tag=f"sm{g}{jt}" + afx)

                def relusq(e, jt=jt):
                    ch = CH["dve"]
                    ch(e, e.tensor_scalar(gsc[:, 0:256], ps[3][:, 0:256], 0.0, None, ALU.max))
                    return e.tensor_tensor(attnT[:, jt * 256:(jt + 1) * 256],
                                           gsc[:, 0:256], gsc[:, 0:256], ALU.mult)
                P.op("dve", relusq, reads=[psb[3], gsc_b, attnT_b],
                     writes=[gsc_b, attnT_b], tag=f"rq{g}{jt}" + afx)
            for f in range(16):
                for jt in range(2):
                    P.op("pe", lambda e, f=f, g=g, jt=jt: e.matmul(
                        ps[3][:, 256:512],
                        vtm[:, (2 * g + jt) * HID + f * 128:(2 * g + jt) * HID + (f + 1) * 128],
                        attnT[:, jt * 256:(jt + 1) * 256], start=(jt == 0), stop=(jt == 1)),
                        reads=[vtm_b, attnT_b], writes=[psb[3]], tag=f"qo{f}{g}{jt}" + afx)
                P.op("dve", lambda e, f=f: e.tensor_copy(gsc[:, 0:256], ps[3][:, 256:512]),
                     reads=[psb[3]], writes=[gsc_b], tag=f"qoe{f}{g}" + afx)
                P.dma("sp", lambda e, f=f, g=g, r0=r0: e.dma_start(
                    out=pr_quad[f * 128:(f + 1) * 128, r0 + g * 256:r0 + (g + 1) * 256],
                    in_=gsc[:, 0:256]),
                    reads=[gsc_b], writes=[b_quad], tag=f"qs{f}{g}" + afx)

        P.op("dve", lambda e, r0=r0: e.tensor_copy(lqAll[:, r0:r0 + ACH],
                                                   qhead[:, 2 * ACH:3 * ACH]),
             reads=[qhead_b], writes=[lqAll_b], tag="lq" + afx)
        for q in range(NSUB):
            P.op("pe", lambda e, q=q: e.transpose(
                pst_b[:, 0:128], qhead[:, 4 * ACH + q * 128:4 * ACH + (q + 1) * 128],
                s_idb[:, :]),
                reads=[qhead_b, s_idb_b], writes=[psb[3]], tag=f"lkT{q}" + afx)
            P.op("dve", lambda e, q=q: e.tensor_copy(lkN[:, q * 128:(q + 1) * 128],
                                                     pst_b[:, 0:128]),
                 reads=[psb[3]], writes=[lkN_b], tag=f"lke{q}" + afx)
        for q in range(NSUB):
            for oc in range(4):
                tgt = pd[0] if oc < 2 else pd[1]
                tb = pdb[0] if oc < 2 else pdb[1]
                off = (oc % 2) * 512
                P.op("pe", lambda e, q=q, oc=oc, tgt=tgt, off=off, c=c: e.matmul(
                    tgt[:, off:off + 512], lkN[:, q * 128:(q + 1) * 128],
                    vtm[:, q * HID + oc * 512:q * HID + (oc + 1) * 512],
                    start=(c == 0 and q == 0), stop=(c == NACH - 1 and q == NSUB - 1)),
                    reads=[lkN_b, vtm_b], writes=[tb], tag=f"lkv{q}{oc}" + afx)

    for half in range(2):
        P.op("dve", lambda e, half=half: e.tensor_copy(
            lkvm[:, half * 1024:(half + 1) * 1024], pd[half][:, 0:1024]),
            reads=[pdb[half]], writes=[lkvm_b], tag=f"lkve{half}")
    lkvi, lkvi_b = ar.alloc(128, 1, U32, "lkvi")
    P.op("dve", lambda e: e.tensor_copy(lkvi[:, 0:1], h128c[:, 0:1]),
         reads=[h128c_b], writes=[lkvi_b], tag="lkvi")
    P.dma("pool", lambda e: e.indirect_dma_start(
        sh_linkv.rearrange("a b c -> (a b) c")[:, :],
        bass.IndirectOffsetOnAxis(lkvi[:, 0:1], 0), lkvm[:, :], None,
        bounds_check=rbc_lkv, oob_is_err=False),
        reads=[lkvm_b, lkvi_b], writes=[b_shlinkv], tag="wlkv")

    if debug:
        P.dma("sp", lambda e: e.dma_start(out=dbg["linkv"][:, :], in_=lkvm[:, :]),
              reads=[lkvm_b], writes=[b_dbg["linkv"]], tag="dlkv")

    barrier(2, [b_shS, b_shlinkv], tag="B2")
    if PH <= 3:
        return finish()

    ar.release()
    ar.mark()
    wob, wob_b = ar.alloc(128, 16 * D, BF16, "wob")
    lkvf, lkvf_b = ar.alloc(128, HID, F32, "lkvf")
    lkvb2, lkvb2_b = ar.alloc(128, HID, BF16, "lkvb2")
    qre, qre_b = ar.alloc(128, 16 * 512, BF16, "qre")
    gre, gre_b = ar.alloc(128, 16 * 512, BF16, "gre")
    Gch, Gch_b = ar.alloc(128, 16 * 512, BF16, "Gch")
    Aev, Aev_b = ar.alloc(128, D, F32, "Aev")

    for ct in range(16):
        P.dma("pool", lambda e, ct=ct: e.dma_start(
            out=wob[:, ct * D:(ct + 1) * D], in_=w_o[ct * 128:(ct + 1) * 128, :]),
            reads=[bI["w_o"]], writes=[wob_b], tag=f"ldwo{ct}")
    P.dma("sp", lambda e: e.dma_start(out=lkvf[:, :],
                                      in_=sh_linkv[0, :, :]),
          reads=[b_shlinkv, btok[1]], writes=[lkvf_b], tag="lkv0")

    lkvp, lkvp_b = ar.alloc(128, HID, F32, "lkvp")
    P.dma("sp", lambda e: e.dma_start(out=lkvp[:, :], in_=sh_linkv[1, :, :]),
          reads=[b_shlinkv, btok[1]], writes=[lkvp_b], tag="lkv1")

    def lkvadd(e):
        ch = CH["dve"]
        ch(e, e.tensor_tensor(lkvf[:, :], lkvf[:, :], lkvp[:, :], ALU.add))
        return e.tensor_copy(lkvb2[:, :], lkvf[:, :])
    P.op("dve", lkvadd, reads=[lkvf_b, lkvp_b, lkvb2_b], writes=[lkvf_b, lkvb2_b],
         tag="lkvadd")

    bobc5, bobc5_b = ar.alloc(128, D, F32, "bobc5")
    xp5, xp5_b = ar.alloc(128, D, F32, "xp5")
    hp5, hp5_b = ar.alloc(128, D, F32, "hp5")
    np5, np5_b = ar.alloc(128, D, F32, "np5")
    ld_row(b_o, bI["b_o"], D, "bo5")
    bcast_row(bobc5, bobc5_b, rowbuf[0:1, 0:D], rowbuf_b.sub(0, D * 4), D, "bo5")

    for c in range(NACH):
        r0 = c * ACH
        afx = f"_p5{c}"
        for f in range(16):
            P.dma("sp", lambda e, f=f, r0=r0: e.dma_start(
                out=qre[:, f * 512:(f + 1) * 512],
                in_=pr_quad[f * 128:(f + 1) * 128, r0:r0 + ACH]),
                reads=[b_quad], writes=[qre_b], tag=f"qr{f}" + afx)
            P.dma("sp", lambda e, f=f, r0=r0: e.dma_start(
                out=gre[:, f * 512:(f + 1) * 512],
                in_=pr_gate[f * 128:(f + 1) * 128, r0:r0 + ACH]),
                reads=[b_gate], writes=[gre_b], tag=f"gr{f}" + afx)
        for f in range(16):
            P.op("pe", lambda e, f=f, r0=r0: e.matmul(
                ps[0][:, 0:512], lkvb2[:, f * 128:(f + 1) * 128], lqAll[:, r0:r0 + ACH],
                start=True, stop=True),
                reads=[lkvb2_b, lqAll_b], writes=[psb[0]], tag=f"lo{f}" + afx)

            def mkG(e, f=f):
                ch = CH["dve"]
                ch(e, e.tensor_tensor(Gch[:, f * 512:(f + 1) * 512], ps[0][:, 0:512],
                                      qre[:, f * 512:(f + 1) * 512], ALU.add))
                return e.tensor_tensor(Gch[:, f * 512:(f + 1) * 512],
                                       Gch[:, f * 512:(f + 1) * 512],
                                       gre[:, f * 512:(f + 1) * 512], ALU.mult)
            P.op("dve", mkG, reads=[psb[0], qre_b, gre_b, Gch_b], writes=[Gch_b],
                 tag=f"G{f}" + afx)
        for q in range(NSUB):
            for half in range(2):
                for ct in range(16):
                    P.op("pe", lambda e, q=q, half=half, ct=ct: e.matmul(
                        pd[0][:, half * 512:(half + 1) * 512],
                        Gch[:, ct * 512 + q * 128:ct * 512 + q * 128 + 128],
                        wob[:, ct * D + half * 512:ct * D + (half + 1) * 512],
                        start=(ct == 0), stop=(ct == 15)),
                        reads=[Gch_b, wob_b], writes=[pdb[0]], tag=f"A{q}{half}{ct}" + afx)
            P.dma("sp", lambda e, q=q, r0=r0: e.dma_start(
                out=xp5[:, :], in_=pr_x[r0 + q * 128:r0 + (q + 1) * 128, :]),
                reads=[b_x], writes=[xp5_b], tag=f"xp{q}" + afx)
            P.dma("sp", lambda e, q=q, r0=r0: e.dma_start(
                out=hp5[:, :], in_=pr_haa[r0 + q * 128:r0 + (q + 1) * 128, :]),
                reads=[b_haa], writes=[hp5_b], tag=f"hp{q}" + afx)
            P.dma("sp", lambda e, q=q, r0=r0: e.dma_start(
                out=np5[:, :], in_=pr_xn[r0 + q * 128:r0 + (q + 1) * 128, :]),
                reads=[b_xn], writes=[np5_b], tag=f"np{q}" + afx)

            def aefold(e, q=q):
                ch = CH["dve"]
                ch(e, e.tensor_tensor(Aev[:, :], pd[0][:, 0:D], xp5[:, :], ALU.add))
                ch(e, e.tensor_tensor(Aev[:, :], Aev[:, :], hp5[:, :], ALU.add))
                ch(e, e.tensor_tensor(Aev[:, :], Aev[:, :], np5[:, :], ALU.add))
                return e.tensor_tensor(Aev[:, :], Aev[:, :], bobc5[:, :], ALU.add)
            P.op("dve", aefold, reads=[pdb[0], xp5_b, hp5_b, np5_b, bobc5_b, Aev_b],
                 writes=[Aev_b], tag=f"Ae{q}" + afx)
            P.dma("sp", lambda e, q=q, r0=r0: e.dma_start(
                out=pr_outp[r0 + q * 128:r0 + (q + 1) * 128, :], in_=Aev[:, :]),
                reads=[Aev_b], writes=[b_outp], tag=f"As{q}" + afx)

    if PH <= 4:
        return finish()
    ar.release()
    ar.release()
    ar.mark()
    w1sb, w1sb_b = ar.alloc(128, 8 * FFNH, BF16, "w1sb")
    w2sb2 = [ar.alloc(128, 32 * 512, BF16, f"w2sb{i}") for i in range(2)]
    actb, actb_b = ar.alloc(128, 32 * 512, BF16, "actb")
    x2g2 = [ar.alloc(128, D, BF16, f"x2g{i}") for i in range(2)]
    x2gT, x2gT_b = ar.alloc(128, 8 * 512, BF16, "x2gT")
    ysb2 = [ar.alloc(128, 512, F32, f"ysb{i}") for i in range(2)]
    Sg, Sg_b = ar.alloc(128, 16, F32, "Sg")
    h16c, h16c_b = ar.alloc(128, 1, F32, "h16c")
    eidx, eidx_b = ar.alloc(128, 8, F32, "eidx")
    eidxu, eidxu_b = ar.alloc(128, 8, U32, "eidxu")

    P.op("pe", lambda e: e.matmul(ps[0][:, 0:1], s_ones1f[0:1, 0:128], selr[0:1, 14:15],
                                  start=True, stop=True),
         reads=[selr_b, s_ones1f_b], writes=[psb[0]], tag="h16")
    P.op("dve", lambda e: e.tensor_tensor(h16c[:, 0:1], ps[0][:, 0:1], iotacol[:, 0:1],
                                          ALU.add),
         reads=[psb[0], iotacol_b], writes=[h16c_b], tag="h16e")

    for j in range(4):
        jfx = f"_e{j}"
        for ct in range(8):
            P.dma("pool", lambda e, j=j, ct=ct: e.dma_start(
                out=w1sb[:, ct * FFNH:(ct + 1) * FFNH],
                in_=moe_w1[j, ct * 128:(ct + 1) * 128, :]),
                reads=[bI["moe_w1"]], writes=[w1sb_b], tag=f"lw1{ct}" + jfx)
        for half in range(2):
            w2s, w2s_b = w2sb2[half]
            for ct in range(32):
                P.dma("pool", lambda e, j=j, ct=ct, half=half, w2s=w2s: e.dma_start(
                    out=w2s[:, ct * 512:(ct + 1) * 512],
                    in_=moe_w2[j, ct * 128:(ct + 1) * 128,
                               half * 512:(half + 1) * 512]),
                    reads=[bI["moe_w2"]], writes=[w2s_b], tag=f"lw2{ct}{half}" + jfx)
        for grp in range(NGROUP):
            gfx = jfx + f"g{grp}"
            _m = os.environ.get("KP4MODE", "real")
            if _m == "real":
                _gth = grp + 1
            elif _m == "mix2":
                _gth = 0 if grp == 0 else 9999
            elif _m == "mix1":
                _gth = 0 if j == 0 else 9999
            elif _m == "one_skip":
                _gth = 9999 if (j == 0 and grp == 1) else 0
            elif _m == "one_take":
                _gth = 0 if (j == 0 and grp == 0) else 9999
            elif _m == "jreal":
                _gth = (grp + 1) if j == 0 else 0
            elif _m == "realskip1":
                _gth = 9999 if (j == 0 and grp == 7) else 0
            else:
                _gth = {"open": 0, "else": 9999}.get(_m, 0)
            with P.gate(ngi[0:1, j:j + 1], ngi_b, _gth, key=("ng", j)):
                def mkeidx(e, j=j, grp=grp):
                    for tt in range(4):
                        inst = e.tensor_scalar(eidx[:, tt:tt + 1], h16c[:, 0:1],
                                               float(j * ISL + grp * EG + tt * 128),
                                               None, ALU.add)
                    return inst
                P.op("dve", mkeidx, reads=[h16c_b, eidx_b],
                     writes=[eidx_b], tag="eix" + gfx)
                P.op("dve", lambda e: e.tensor_copy(eidxu[:, 0:4], eidx[:, 0:4]),
                     reads=[eidx_b, eidxu_b], writes=[eidxu_b], tag="eixc" + gfx)
                for tt in range(4):
                    P.dma("pool", lambda e, tt=tt: e.indirect_dma_start(
                        Sg[:, tt:tt + 1], None, sh_S[:, :],
                        bass.IndirectOffsetOnAxis(eidxu[:, tt:tt + 1], 0),
                        bounds_check=rbc_S, oob_is_err=False),
                        reads=[b_shS, eidxu_b, btok[1]], writes=[Sg_b], tag=f"gS{tt}" + gfx)
                def mktid1(e):
                    for tt in range(4):
                        e.tensor_scalar(Sg[:, 4 + tt:5 + tt], Sg[:, tt:tt + 1], 0.0,
                                        None, ALU.is_equal)
                    for tt in range(4):
                        inst = e.tensor_scalar(Sg[:, 8 + tt:9 + tt], Sg[:, tt:tt + 1],
                                               -1.0, None, ALU.add)
                    return inst
                P.op("dve", mktid1, reads=[Sg_b], writes=[Sg_b], tag="tid1" + gfx)

                def mktid2(e):
                    for tt in range(4):
                        inst = e.scalar_tensor_tensor(
                            Sg[:, 12 + tt:13 + tt], Sg[:, 4 + tt:5 + tt], HUGE,
                            Sg[:, 8 + tt:9 + tt], ALU.mult, ALU.add)
                    return inst
                P.op("dve", mktid2, reads=[Sg_b], writes=[Sg_b], tag="tid2" + gfx)
                P.op("dve", lambda e: e.tensor_copy(eidxu[:, 4:8], Sg[:, 12:16]),
                     reads=[Sg_b, eidxu_b], writes=[eidxu_b], tag="tid3" + gfx)
                for tt in range(4):
                    x2g, x2g_b = x2g2[tt % 2]
                    P.op("dve", lambda e, x2g=x2g: e.memset(x2g[:, :], 0.0),
                         writes=[x2g_b], tag=f"z{tt}" + gfx)
                    P.dma("pool", lambda e, tt=tt, x2g=x2g: e.indirect_dma_start(
                        x2g[:, :], None, sh_x2[:, :],
                        bass.IndirectOffsetOnAxis(eidxu[:, 4 + tt:5 + tt], 0),
                        bounds_check=rbc_x2, oob_is_err=False),
                        reads=[b_shx2, eidxu_b, x2g_b, btok[1]], writes=[x2g_b],
                        tag=f"gx{tt}" + gfx)
                    for f in range(8):
                        tb = pst_b if f % 2 == 0 else pdt_b
                        tbb = psb[3] if f % 2 == 0 else pdb[1]
                        P.op("pe", lambda e, f=f, tb=tb, x2g=x2g: e.transpose(
                            tb[:, 0:128], x2g[:, f * 128:(f + 1) * 128], s_idb[:, :]),
                            reads=[x2g_b, s_idb_b], writes=[tbb], tag=f"tg{tt}{f}" + gfx)
                        P.op("act", lambda e, f=f, tt=tt, tb=tb: e.activation(
                            x2gT[:, f * 512 + tt * 128:f * 512 + tt * 128 + 128],
                            tb[:, 0:128], AF.Identity,
                            bias=cols[:, C_LN2B + f:C_LN2B + f + 1],
                            scale=cols[:, C_LN2W + f:C_LN2W + f + 1]),
                            reads=[tbb, cols_b], writes=[x2gT_b], tag=f"tge{tt}{f}" + gfx)
                for f4 in range(32):
                    apk = f4 % 2 * 2
                    for ct in range(8):
                        P.op("pe", lambda e, f4=f4, ct=ct, apk=apk: e.matmul(
                            ps[apk][:, 0:512],
                            w1sb[:, ct * FFNH + f4 * 128:ct * FFNH + (f4 + 1) * 128],
                            x2gT[:, ct * 512:(ct + 1) * 512],
                            start=(ct == 0), stop=(ct == 7)),
                            reads=[w1sb_b, x2gT_b], writes=[psb[apk]], tag=f"a{f4}{ct}" + gfx)
                    P.op("act", lambda e, f4=f4, apk=apk: e.activation(
                        actb[:, f4 * 512:(f4 + 1) * 512], ps[apk][:, 0:512], AF.Gelu),
                        reads=[psb[apk]], writes=[actb_b], tag=f"ag{f4}" + gfx)
                for half in range(2):
                    w2s, w2s_b = w2sb2[half]
                    for tt in range(4):
                        ypb = ps[1][:, 0:512] if tt % 2 == 0 else pd[0][:, 0:512]
                        ypbb = psb[1] if tt % 2 == 0 else pdb[0]
                        for ct in range(32):
                            P.op("pe", lambda e, tt=tt, ct=ct, ypb=ypb, w2s=w2s: e.matmul(
                                ypb,
                                actb[:, ct * 512 + tt * 128:ct * 512 + tt * 128 + 128],
                                w2s[:, ct * 512:(ct + 1) * 512],
                                start=(ct == 0), stop=(ct == 31)),
                                reads=[actb_b, w2s_b], writes=[ypbb], tag=f"y{tt}{ct}{half}" + gfx)
                        ysb, ysb_b = ysb2[tt % 2]
                        P.op("dve", lambda e, ypb=ypb, ysb=ysb: e.tensor_copy(
                            ysb[:, 0:512], ypb),
                            reads=[ypbb], writes=[ysb_b], tag=f"ye{tt}{half}" + gfx)

                        P.op("dve", lambda e, j=j, grp=grp, tt=tt: e.tensor_scalar(
                            eidx[:, 4:5], h16c[:, 0:1],
                            float(j * ISL + grp * EG + tt * 128), None, ALU.add),
                            reads=[h16c_b, eidx_b], writes=[eidx_b],
                            tag=f"ywi{tt}{half}" + gfx)
                        P.op("dve", lambda e: e.tensor_copy(eidxu[:, 4:5], eidx[:, 4:5]),
                             reads=[eidx_b, eidxu_b], writes=[eidxu_b],
                             tag=f"ywc{tt}{half}" + gfx)
                        P.dma("pool", lambda e, half=half, ysb=ysb: e.indirect_dma_start(
                            (sh_y1 if half == 0 else sh_y2)[:, :],
                            bass.IndirectOffsetOnAxis(eidxu[:, 4:5], 0),
                            ysb[:, 0:512], None, bounds_check=rbc_y, oob_is_err=False),
                            reads=[ysb_b, eidxu_b],
                            writes=[b_shy1 if half == 0 else b_shy2],
                            tag=f"yw{tt}{half}" + gfx)

    barrier(3, [b_shy1, b_shy2], tag="B3")

    if debug:
        ydmp, ydmp_b = ysb2[0]
        P.op("dve", lambda e: e.tensor_copy(ydmp[0:1, 0:8], ngi[0:1, 0:8]),
             reads=[ngi_b], writes=[ydmp_b], tag="dngi")
        P.dma("sp", lambda e: e.dma_start(out=dbg["ngi"][0:1, 0:8], in_=ydmp[0:1, 0:8]),
              reads=[ydmp_b], writes=[b_dbg["ngi"]], tag="dngis")
        for blk in range(4 * ISL // 128):
            for nm, sh in [("y1", sh_y1), ("y2", sh_y2)]:
                P.dma("sp", lambda e, blk=blk, sh=sh: e.dma_start(
                    out=ydmp[:, :], in_=sh[blk * 128:(blk + 1) * 128, :]),
                    reads=[b_shy1, b_shy2, b_dbg["ngi"]], writes=[ydmp_b],
                    tag=f"yd{nm}{blk}")
                P.dma("sp", lambda e, blk=blk, nm=nm: e.dma_start(
                    out=dbg[nm][blk * 128:(blk + 1) * 128, :], in_=ydmp[:, :]),
                    reads=[ydmp_b], writes=[b_dbg[nm]], tag=f"ye{nm}{blk}")

    if PH <= 5:
        return finish()

    ar.release()
    ar.mark()
    oach, oach_b = ar.alloc(128, D, F32, "oach")
    arel, arel_b = ar.alloc(128, D, F32, "arel")
    y1t, y1t_b = ar.alloc(128, D, F32, "y1t")
    y2t, y2t_b = ar.alloc(128, D, F32, "y2t")
    mte, mte_b = ar.alloc(128, D, F32, "mte")
    gix, gix_b = ar.alloc(128, 2, U32, "gix")

    for t in range(16):
        r0 = t * 128
        tfx = f"_o{t}"
        P.dma("sp", lambda e, r0=r0: e.dma_start(out=arel[:, :], in_=pr_outp[r0:r0 + 128, :]),
              reads=[b_outp], writes=[arel_b], tag="oa" + tfx)
        P.op("dve", lambda e, t=t: e.tensor_copy(gix[:, 0:2], gidx_all[:, 2 * t:2 * t + 2]),
             reads=[gidx_all_b, gix_b], writes=[gix_b], tag="og" + tfx)
        for kk, (yt, yt_b) in enumerate([(y1t, y1t_b), (y2t, y2t_b)]):
            P.op("dve", lambda e, yt=yt: e.memset(yt[:, :], 0.0), writes=[yt_b],
                 tag=f"yz{kk}" + tfx)
            for half in range(2):
                P.dma("pool", lambda e, kk=kk, yt=yt, half=half: e.indirect_dma_start(
                    yt[:, half * 512:(half + 1) * 512], None,
                    (sh_y1 if half == 0 else sh_y2)[:, :],
                    bass.IndirectOffsetOnAxis(gix[:, kk:kk + 1], 0),
                    bounds_check=rbc_y, oob_is_err=False),
                    reads=[b_shy1 if half == 0 else b_shy2, gix_b, yt_b, btok[2]],
                    writes=[yt_b], tag=f"yg{kk}{half}" + tfx)

        def comb(e, t=t):
            ch = CH["dve"]
            ch(e, e.tensor_scalar(mte[:, :], y1t[:, :], pk_all[:, 2 * t:2 * t + 1],
                                  None, ALU.mult))
            ch(e, e.scalar_tensor_tensor(mte[:, :], y2t[:, :],
                                         pk_all[:, 2 * t + 1:2 * t + 2], mte[:, :],
                                         ALU.mult, ALU.add))
            return e.tensor_tensor(oach[:, :], arel[:, :], mte[:, :], ALU.add)
        P.op("dve", comb,
             reads=[y1t_b, y2t_b, pk_all_b, arel_b, mte_b, oach_b],
             writes=[mte_b, oach_b], tag="oc" + tfx)
        P.dma("sp", lambda e, r0=r0: e.dma_start(out=out[r0:r0 + 128, :], in_=oach[:, :]),
              reads=[oach_b], writes=[b_out.sub(r0 * D * 4, (r0 + 128) * D * 4)],
              tag="os" + tfx)

    return finish(mini=False)


def _in_maps(inputs, rng_nonce):
    x = np.ascontiguousarray(np.asarray(inputs["L0_attn_x"], dtype=np.float32))
    qkv_wT = np.ascontiguousarray(np.asarray(inputs["L0_ffn_qkv_weight"], np.float32).T)
    proj_wT = np.ascontiguousarray(np.asarray(inputs["L0_ffn_proj_weight"], np.float32).T)
    w1 = np.ascontiguousarray(np.asarray(inputs["L0_ffn_moe_w1"], np.float32))
    w2 = np.ascontiguousarray(np.asarray(inputs["L0_ffn_moe_w2"], np.float32))

    def row(name, w):
        return np.asarray(inputs[name], np.float32).reshape(1, w)

    maps = []
    for core in range(8):
        p, h = core // 2, core % 2
        selv = np.zeros((1, 16), np.float32)
        selv[0, p] = 1.0
        selv[0, 4:4 + p] = 1.0
        selv[0, 8] = h
        selv[0, 9] = p
        selv[0, 10] = h * (B * NH) + p * NH
        selv[0, 11] = h * 4
        selv[0, 12] = p * HID
        selv[0, 14] = h * 4 * ISL
        m = {
            "xs": np.ascontiguousarray(x[:, h * NH:(h + 1) * NH, :]),
            "sel": selv,
            "nonce": rng_nonce,
            "norm_g": row("L0_attn_norm_g", D),
            "norm_b": row("L0_attn_norm_b", D),
            "w_h": np.ascontiguousarray(np.asarray(inputs["L0_attn_W_h"], np.float32)),
            "b_h": row("L0_attn_b_h", 2 * HID),
            "w_qk": np.ascontiguousarray(np.asarray(inputs["L0_attn_W_qk"], np.float32)),
            "b_qk": row("L0_attn_b_qk", QKD),
            "qk_gamma": np.ascontiguousarray(np.asarray(inputs["L0_attn_qk_gamma"], np.float32)),
            "qk_beta": np.ascontiguousarray(np.asarray(inputs["L0_attn_qk_beta"], np.float32)),
            "rel_emb": row("L0_attn_rel_emb", 32),
            "w_o": np.ascontiguousarray(np.asarray(inputs["L0_attn_W_o"], np.float32)),
            "b_o": row("L0_attn_b_o", D),
            "ln1_w": row("L0_ffn_ln1_weight", D),
            "ln1_b": row("L0_ffn_ln1_bias", D),
            "ln2_w": row("L0_ffn_ln2_weight", D),
            "ln2_b": row("L0_ffn_ln2_bias", D),
            "qkv_wT": qkv_wT,
            "proj_wT": proj_wT,
            "router_w": np.ascontiguousarray(np.asarray(inputs["L0_ffn_router_weight"], np.float32)),
            "moe_w1": np.ascontiguousarray(w1[4 * h:4 * h + 4]),
            "moe_w2": np.ascontiguousarray(w2[4 * h:4 * h + 4]),
        }
        maps.append(m)
    return maps


_CACHE = {}


def _build(debug=0):
    key = ("nc", debug)
    if key not in _CACHE:
        _CACHE[key] = build_program(debug=debug)
    return _CACHE[key]


def run_device(inputs, debug=0):
    nc, dbg = _build(debug)
    nonce_v = np.random.default_rng().standard_normal((1, 4)).astype(np.float32)
    maps = _in_maps(inputs, nonce_v)
    res = run_bass_kernel_spmd(nc, maps, core_ids=list(range(8)))
    return res


def kernel(**inputs):
    res = run_device(inputs, debug=0)
    outf = np.zeros((B, N, D), np.float32)
    for core in range(8):
        p, h = core // 2, core % 2
        outf[p, h * NH:(h + 1) * NH, :] = res.results[core]["out"]
    return outf
